# version 03
# baseline (speedup 1.0000x reference)
import numpy as np
from contextlib import ExitStack

import concourse.bass as bass
import concourse.mybir as mybir
from concourse.bass_utils import run_bass_kernel_spmd

F32 = mybir.dt.float32
BF16 = mybir.dt.bfloat16
I32 = mybir.dt.int32
AF = mybir.ActivationFunctionType
ALU = mybir.AluOpType
AX = mybir.AxisListType

T = 2048
D = 1024
DC = D // 128
NCORES = 8
NEG = -1.0e30



NP_DMA = 24


class Node:
    __slots__ = ("engine", "fn", "deps", "idx", "inc_no", "needs_inc", "tag",
                 "slot", "epoch")

    def __init__(self, engine, fn, idx):
        self.tag = ""
        self.slot = None
        self.epoch = None
        self.engine = engine
        self.fn = fn
        self.deps = []
        self.idx = idx
        self.inc_no = None
        self.needs_inc = False


class Builder:

    def __init__(self, nc):
        self.nc = nc
        self.nodes = []
        self.last_writer = {}
        self.readers = {}
        self.global_deps = []

    def emit(self, engine, fn, reads=(), writes=(), is_dma=False):
        key = f"dma:{engine}" if is_dma else engine
        n = Node(key, fn, len(self.nodes))
        for g in self.global_deps:
            if g.engine != key:
                n.deps.append(g)
        def needs_edge(a, b):
            if a.engine != b.engine:
                return True
            return a.engine.startswith("dma:")
        for r in reads:
            w = self.last_writer.get(r)
            if w is not None and needs_edge(w, n):
                n.deps.append(w)
            self.readers.setdefault(r, []).append(n)
        for w_ in writes:
            lw = self.last_writer.get(w_)
            if lw is not None and needs_edge(lw, n):
                n.deps.append(lw)
            for rd in self.readers.get(w_, ()):
                if needs_edge(rd, n) and rd is not n:
                    n.deps.append(rd)
            self.last_writer[w_] = n
            self.readers[w_] = [n] if w_ in [r for r in reads] else []
        self.nodes.append(n)
        return n

    def dma(self, out, in_, reads=(), writes=(), queue="sync"):
        def fn(eng):
            return eng.dma_start(out=out, in_=in_)
        n = self.emit(queue, fn, reads=reads, writes=writes, is_dma=True)
        n.tag = "dma r" + ",".join(reads) + " w" + ",".join(writes)
        return n

    def finalize(self, es, block, barrier_writes=()):
        nc = self.nc
        qcounters = {}
        for n in self.nodes:
            if n.engine.startswith("dma:"):
                n.needs_inc = True
                i = qcounters.get(n.engine, 0)
                qcounters[n.engine] = i + 1
                n.slot = i % NP_DMA
                n.epoch = i // NP_DMA
        for n in self.nodes:
            for d in n.deps:
                d.needs_inc = True
        finals = []
        for r in barrier_writes:
            w = self.last_writer.get(r)
            if w is not None:
                w.needs_inc = True
                finals.append(w)
        counters = {}
        for n in self.nodes:
            if n.needs_inc and not n.engine.startswith("dma:"):
                counters[n.engine] = counters.get(n.engine, 0) + 1
                n.inc_no = counters[n.engine]
        sems = {e: es.enter_context(nc.semaphore(f"sem_{e.replace(':', '_')}"))
                for e in counters}
        dsems = {q: [es.enter_context(nc.semaphore(f"dma_{q.split(':')[1]}_{s}"))
                     for s in range(min(NP_DMA, cnt))]
                 for q, cnt in qcounters.items()}

        def emit_waits(eng, deps, done):
            waits = {}
            for d in deps:
                if d.engine.startswith("dma:"):
                    key = (d.engine, d.slot)
                    waits[key] = max(waits.get(key, 0), 16 * (d.epoch + 1))
                else:
                    waits[d.engine] = max(waits.get(d.engine, 0), d.inc_no)
            for src, v in sorted(waits.items(), key=str):
                if done.get(src, 0) >= v:
                    continue
                done[src] = v
                if isinstance(src, tuple):
                    eng.wait_ge(dsems[src[0]][src[1]], v)
                else:
                    eng.wait_ge(sems[src], v)

        def make_body(hw):
            nodes = [n for n in self.nodes
                     if n.engine == hw or n.engine == f"dma:{hw}"]

            def body(eng):
                done = {}
                for n in nodes:
                    if n.engine.startswith("dma:"):
                        if n.epoch > 0:
                            key = (n.engine, n.slot)
                            v = 16 * n.epoch
                            if done.get(key, 0) < v:
                                done[key] = v
                                eng.wait_ge(dsems[n.engine][n.slot], v)
                    emit_waits(eng, n.deps, done)
                    inst = n.fn(eng)
                    if n.engine.startswith("dma:"):
                        inst.then_inc(dsems[n.engine][n.slot], 16)
                    elif n.needs_inc:
                        inst.then_inc(sems[n.engine], 1)
                if hw == "sync":
                    emit_waits(eng, finals, done)
            return body

        import os
        if os.environ.get("BASS_DBG_SCHED"):
            for n in self.nodes:
                deps = [(d.engine, d.inc_no) for d in n.deps if not d.engine.startswith("dma:gl")]
                print(f"[{n.idx:4d}] {n.engine:10s} inc={n.inc_no} {n.tag[:80]:80s} deps={deps}")
        block.sync(make_body("sync"))
        block.vector(make_body("vector"))
        block.scalar(make_body("scalar"))
        block.gpsimd(make_body("gpsimd"))
        block.tensor(make_body("tensor"))


    def v(self, method, *args, reads=(), writes=(), **kw):
        def fn(eng):
            return getattr(eng, method)(*args, **kw)
        n = self.emit("vector", fn, reads=reads, writes=writes)
        n.tag = method + " r" + ",".join(reads) + " w" + ",".join(writes)
        return n

    def s(self, method, *args, reads=(), writes=(), **kw):
        def fn(eng):
            return getattr(eng, method)(*args, **kw)
        n = self.emit("scalar", fn, reads=reads, writes=writes)
        n.tag = method + " r" + ",".join(reads) + " w" + ",".join(writes)
        return n

    def g(self, method, *args, reads=(), writes=(), **kw):
        def fn(eng):
            return getattr(eng, method)(*args, **kw)
        n = self.emit("gpsimd", fn, reads=reads, writes=writes)
        n.tag = method + " r" + ",".join(reads) + " w" + ",".join(writes)
        return n

    def t(self, method, *args, reads=(), writes=(), **kw):
        def fn(eng):
            return getattr(eng, method)(*args, **kw)
        n = self.emit("tensor", fn, reads=reads, writes=writes)
        n.tag = method + " r" + ",".join(reads) + " w" + ",".join(writes)
        return n

    def mm(self, out, lhsT, rhs, start, stop, reads=(), writes=()):
        def fn(eng):
            return eng.matmul(out, lhsT, rhs, start=start, stop=stop)
        n = self.emit("tensor", fn, reads=reads, writes=writes)
        n.tag = f"mm({int(start)}{int(stop)}) r" + ",".join(reads) + " w" + ",".join(writes)
        return n



class Prog:

    def __init__(self, nc, es):
        self.nc = nc
        self.es = es
        self.B = Builder(nc)
        S = self.sb = lambda name, shape, dt: es.enter_context(
            nc.sbuf_tensor(name, shape, dt))
        self.xs = [S(f"xs{c}", [128, T], F32) for c in range(DC)]
        self.xn = [S(f"xn{c}", [128, T], BF16) for c in range(DC)]
        self.sq = [S(f"sq{i}", [128, T], BF16) for i in range(2)]
        self.tmpf = [S("tmpf0", [128, T], F32)]
        self.rows = None
        self.bc32 = [S(f"bc32_{i}", [128, T], F32) for i in range(2)]
        self.rows = [self.bc32[1][0:1, 0:T], self.bc32[0][0:1, 0:T]]
        self._row2 = None
        self.ones_bf = S("ones_bf", [128, 130], BF16)
        self.ps = [es.enter_context(nc.psum_tensor(f"ps{i}", [128, 512], F32))
                   for i in range(8)]
        self.rows_dram = nc.dram_tensor("rows_dram", [8, T], F32, kind="Internal").ap()
        self.cat_dram = nc.dram_tensor("cat_dram", [D, T], F32, kind="Internal").ap()
        self.park_dram = nc.dram_tensor("park_dram", [D, T], F32, kind="Internal").ap()
        self.misc_dram = nc.dram_tensor("misc_dram", [64, T], F32, kind="Internal").ap()
        self.z_dram = nc.dram_tensor("z_dram", [2048, 256], BF16, kind="Internal").ap()
        self.vlin_dram = nc.dram_tensor("vlin_dram", [T, 1024], BF16, kind="Internal").ap()
        self.lin_dram = nc.dram_tensor("lin_dram", [D, T], BF16, kind="Internal").ap()
        self.win_dram = nc.dram_tensor("win_dram", [D, T], BF16, kind="Internal").ap()
        self._wt_cache = {}
        self.cpool = self.sb("cpool", [128, 16], F32)
        self._cvals = {}

    @property
    def row2(self):
        return self.tmpf[0][0:1, 0:T]

    def const_val(self, v):
        v = float(v)
        if v in self._cvals:
            return
        col = len(self._cvals)
        assert col < 16
        self._cvals[v] = col
        ap = self.cpool[:, col:col + 1]
        n = self.B.g("memset", ap, v, writes=("cpool",))
        self.B.global_deps.append(n)
        self.nc.const_aps.register(F32, v, ap)

    def stage_init(self):
        pass

    def load_bf16(self, dst_ap, src_ap, dst_key, cast_engine=None):
        self.B.dma(dst_ap, src_ap, writes=(dst_key,), queue="gpsimd")

    def bcast_row(self, dst_tile_ap, src_row_ap, slot, n, rows, src_key, dst_key):
        B = self.B
        dr = self.rows_dram[slot:slot + 1, 0:n]
        B.dma(dr, src_row_ap, reads=(src_key,), writes=(f"rows_dram{slot}",))
        brd = dr.partition_broadcast(rows).squeeze(1)
        B.dma(dst_tile_ap, brd, reads=(f"rows_dram{slot}",), writes=(dst_key,))



def emit_ln(P, gamma_beta=None, eps=1e-5, stats_only=False, park_u=False):
    B = P.B
    TT4 = T // 512
    for c in range(DC):
        B.v("tensor_copy", out=P.xn[c][:], in_=P.xs[c][:],
            reads=(f"xs{c}",), writes=(f"xn{c}",))
    for tt in range(TT4):
        for c in range(DC):
            B.mm(P.ps[4 + tt][0:1, :], P.ones_bf[:, 0:1],
                 P.xn[c][:, tt * 512:(tt + 1) * 512],
                 start=(c == 0), stop=(c == DC - 1),
                 reads=(f"xn{c}", "ones"), writes=(f"ps{4+tt}",))
        B.v("tensor_copy", out=P.rows[0][0:1, tt * 512:(tt + 1) * 512],
            in_=P.ps[4 + tt][0:1, :], reads=(f"ps{4+tt}",), writes=("bc1",))
    for c in range(DC):
        B.s("activation", out=P.sq[c % 2][:], in_=P.xn[c][:], func=AF.Square,
            reads=(f"xn{c}",), writes=(f"sq{c%2}",))
        for tt in range(TT4):
            B.mm(P.ps[4 + tt][0:1, :], P.ones_bf[:, 0:1],
                 P.sq[c % 2][:, tt * 512:(tt + 1) * 512],
                 start=(c == 0), stop=(c == DC - 1),
                 reads=(f"sq{c%2}", "ones"), writes=(f"ps{4+tt}",))
    for tt in range(TT4):
        B.v("tensor_copy", out=P.rows[1][0:1, tt * 512:(tt + 1) * 512],
            in_=P.ps[4 + tt][0:1, :], reads=(f"ps{4+tt}",), writes=("bc0",))
    if stats_only:
        return
    B.v("scalar_tensor_tensor", out=P.row2, in0=P.rows[0][:],
        scalar=1.0 / (D * D), in1=P.rows[0][:], op0=ALU.mult, op1=ALU.mult,
        reads=("bc1",), writes=("tmpf0",))
    B.v("scalar_tensor_tensor", out=P.rows[1][:], in0=P.rows[1][:],
        scalar=1.0 / D, in1=P.row2, op0=ALU.mult, op1=ALU.subtract,
        reads=("bc0", "tmpf0"), writes=("bc0",))
    P.const_val(eps)
    B.s("activation", out=P.rows[1][:], in_=P.rows[1][:], func=AF.Ln, bias=eps,
        reads=("bc0",), writes=("bc0",))
    B.s("activation", out=P.rows[1][:], in_=P.rows[1][:], func=AF.Exp, scale=-0.5,
        reads=("bc0",), writes=("bc0",))
    B.v("scalar_tensor_tensor", out=P.rows[0][:], in0=P.rows[0][:],
        scalar=1.0 / D, in1=P.rows[1][:], op0=ALU.mult, op1=ALU.mult,
        reads=("bc1", "bc0"), writes=("bc1",))
    P.bcast_row(P.bc32[0][:], P.rows[1][:], 0, T, 128, "bc0", "bc0")
    P.bcast_row(P.bc32[1][:], P.rows[0][:], 1, T, 128, "bc1", "bc1")
    for c in range(DC):
        B.v("tensor_mul", out=P.tmpf[0][:], in0=P.xs[c][:], in1=P.bc32[0][:],
            reads=(f"xs{c}", "bc0"), writes=("tmpf0",))
        if gamma_beta is None:
            if park_u:
                B.v("tensor_sub", out=P.tmpf[0][:], in0=P.tmpf[0][:],
                    in1=P.bc32[1][:], reads=("tmpf0", "bc1"), writes=("tmpf0",))
                B.dma(P.park_dram[c * 128:(c + 1) * 128, :], P.tmpf[0][:],
                      reads=("tmpf0",), writes=(f"park{c}",))
                B.v("tensor_copy", out=P.xn[c][:], in_=P.tmpf[0][:],
                    reads=("tmpf0",), writes=(f"xn{c}",))
            else:
                B.v("tensor_sub", out=P.xn[c][:], in0=P.tmpf[0][:], in1=P.bc32[1][:],
                    reads=("tmpf0", "bc1"), writes=(f"xn{c}",))
        else:
            (g_ap, g_key), (b_ap, b_key) = gamma_beta[0][c], gamma_beta[1][c]
            B.v("tensor_sub", out=P.tmpf[0][:], in0=P.tmpf[0][:],
                in1=P.bc32[1][:],
                reads=("tmpf0", "bc1"), writes=("tmpf0",))
            B.v("tensor_scalar", out=P.xn[c][:], in0=P.tmpf[0][:],
                scalar1=g_ap, scalar2=b_ap, op0=ALU.mult, op1=ALU.add,
                reads=("tmpf0", g_key, b_key), writes=(f"xn{c}",))


def emit_ln2(P, gamma_beta, eps=1e-5):
    B = P.B
    TT4 = T // 512
    for tt in range(TT4):
        for c in range(DC):
            B.mm(P.ps[4 + tt][0:1, :], P.ones_bf[:, 0:1],
                 P.xn[c][:, tt * 512:(tt + 1) * 512],
                 start=(c == 0), stop=(c == DC - 1),
                 reads=(f"xn{c}", "ones"), writes=(f"ps{4+tt}",))
        B.v("tensor_copy", out=P.rows[0][0:1, tt * 512:(tt + 1) * 512],
            in_=P.ps[4 + tt][0:1, :], reads=(f"ps{4+tt}",), writes=("bc1",))
    for c in range(DC):
        B.s("activation", out=P.sq[c % 2][:], in_=P.xn[c][:], func=AF.Square,
            reads=(f"xn{c}",), writes=(f"sq{c%2}",))
        for tt in range(TT4):
            B.mm(P.ps[4 + tt][0:1, :], P.ones_bf[:, 0:1],
                 P.sq[c % 2][:, tt * 512:(tt + 1) * 512],
                 start=(c == 0), stop=(c == DC - 1),
                 reads=(f"sq{c%2}", "ones"), writes=(f"ps{4+tt}",))
    for tt in range(TT4):
        B.v("tensor_copy", out=P.rows[1][0:1, tt * 512:(tt + 1) * 512],
            in_=P.ps[4 + tt][0:1, :], reads=(f"ps{4+tt}",), writes=("bc0",))
    B.v("scalar_tensor_tensor", out=P.row2, in0=P.rows[0][:],
        scalar=1.0 / (D * D), in1=P.rows[0][:], op0=ALU.mult, op1=ALU.mult,
        reads=("bc1",), writes=("tmpf0",))
    B.v("scalar_tensor_tensor", out=P.rows[1][:], in0=P.rows[1][:],
        scalar=1.0 / D, in1=P.row2, op0=ALU.mult, op1=ALU.subtract,
        reads=("bc0", "tmpf0"), writes=("bc0",))
    P.const_val(eps)
    B.s("activation", out=P.rows[1][:], in_=P.rows[1][:], func=AF.Ln, bias=eps,
        reads=("bc0",), writes=("bc0",))
    B.s("activation", out=P.rows[1][:], in_=P.rows[1][:], func=AF.Exp, scale=-0.5,
        reads=("bc0",), writes=("bc0",))
    B.v("scalar_tensor_tensor", out=P.rows[0][:], in0=P.rows[0][:],
        scalar=1.0 / D, in1=P.rows[1][:], op0=ALU.mult, op1=ALU.mult,
        reads=("bc1", "bc0"), writes=("bc1",))
    P.bcast_row(P.bc32[0][:], P.rows[1][:], 0, T, 128, "bc0", "bc0")
    P.bcast_row(P.bc32[1][:], P.rows[0][:], 1, T, 128, "bc1", "bc1")
    for c in range(DC):
        (g_ap, g_key), (b_ap, b_key) = gamma_beta[0][c], gamma_beta[1][c]
        B.v("tensor_mul", out=P.tmpf[0][:], in0=P.xn[c][:], in1=P.bc32[0][:],
            reads=(f"xn{c}", "bc0"), writes=("tmpf0",))
        B.v("tensor_sub", out=P.tmpf[0][:], in0=P.tmpf[0][:], in1=P.bc32[1][:],
            reads=("tmpf0", "bc1"), writes=("tmpf0",))
        B.v("tensor_scalar", out=P.xn[c][:], in0=P.tmpf[0][:],
            scalar1=g_ap, scalar2=b_ap, op0=ALU.mult, op1=ALU.add,
            reads=("tmpf0", g_key, b_key), writes=(f"xn{c}",))



def pool(P, name, shape, dt):
    if name == "onesr":
        return P.ones_bf
    k = (name, tuple(shape), dt)
    if k not in P._wt_cache:
        P._wt_cache[k] = P.sb(name, list(shape), dt)
    return P._wt_cache[k]


def load_wT_blocks(P, dram_T, col0, ncols, dst, key, nk=DC, cast_engine="vector"):
    for c in range(nk):
        P.load_bf16(dst[:, c * ncols:(c + 1) * ncols],
                    dram_T[c * 128:(c + 1) * 128, col0:col0 + ncols],
                    key, cast_engine=cast_engine)


def load_col(P, dst, dram_vec, off, n, key):
    P.B.dma(dst[0:n, 0:1], dram_vec[off:off + n].rearrange("(p o) -> p o", o=1),
            writes=(key,))



def emit_l0_ffn(P, W):
    B = P.B
    bfa = [pool(P, f"bfA{j}", [128, T], BF16) for j in range(4)]
    swT = [bfa[c // 2][:, (c % 2) * 1024:(c % 2) * 1024 + 1024] for c in range(DC)]
    swk = [f"bfA{c // 2}" for c in range(DC)]
    for c in range(DC):
        B.dma(P.tmpf[0][:, 0:D], W["l0_fwT"][c * 128:(c + 1) * 128, :],
              writes=("tmpf0",))
        B.s("sign", out=swT[c], in_=P.tmpf[0][:, 0:D],
            reads=("tmpf0",), writes=(swk[c],))
    for oc in range(DC):
        for c in range(DC):
            for tt in range(4):
                B.mm(P.ps[tt][:, :], swT[c][:, oc * 128:(oc + 1) * 128],
                     P.xn[c][:, tt * 512:(tt + 1) * 512],
                     start=(c == 0), stop=(c == DC - 1),
                     reads=(swk[c], f"xn{c}"), writes=(f"ps{tt}",))
        for tt in range(4):
            B.v("scalar_tensor_tensor",
                out=P.xs[oc][:, tt * 512:(tt + 1) * 512], in0=P.ps[tt][:, :],
                scalar=0.5, in1=P.xs[oc][:, tt * 512:(tt + 1) * 512],
                op0=ALU.mult, op1=ALU.add,
                reads=(f"ps{tt}", f"xs{oc}"), writes=(f"xs{oc}",))



def emit_l0_attn(P, W):
    B = P.B
    qp = pool(P, "bfC0", [128, T], BF16)
    kp = pool(P, "bfC1", [128, T], BF16)
    vtm = pool(P, "bfV", [128, 2210], BF16)
    pt = [pool(P, f"bfB{i}", [128, T], BF16) for i in range(2)]
    cat = [pool(P, f"bfA{h}", [128, T], BF16) for h in range(DC)]
    wq = pool(P, "wS0", [128, 1024], BF16)
    wk = pool(P, "wS1", [128, 1024], BF16)
    wv = wq
    bq = pool(P, "l0bq", [128, 1], F32)
    bk = pool(P, "l0bk", [128, 1], F32)
    bvr = pool(P, "l0bvr", [1, 128], BF16)
    ob = [pool(P, f"l0ob{d}", [128, 1], F32) for d in range(DC)]
    mh = [pool(P, f"l0mh{i}", [128, 1], F32) for i in range(2)]
    onesr = pool(P, "onesr", [128, 130], BF16)
    B.g("memset", onesr[:], 1.0, writes=("onesr",))
    v3 = vtm[:].rearrange("p (k c) -> p k c", c=65)
    B.g("memset", v3[:, :, 64:65], 1.0, writes=("vtm_ones",))

    WwT, Wb = W["l0_WwT"], W["l0_Wb"]
    for hp in range(8):
        load_wT_blocks(P, WwT, hp * 128, 128, wq, "wS0")
        load_wT_blocks(P, WwT, 1024 + hp * 128, 128, wk, "wS1")
        load_col(P, bq, Wb, hp * 128, 128, "l0bq")
        load_col(P, bk, Wb, 1024 + hp * 128, 128, "l0bk")
        B.dma(bvr[:],
              Wb[2048 + hp * 128:2048 + (hp + 1) * 128].rearrange("(o n) -> o n", o=1),
              writes=("l0bvr",), queue="gpsimd")
        for dst, wt, wkey, bias, bkey, pskey in ((qp, wq, "wS0", bq, "l0bq", 0),
                                                 (kp, wk, "wS1", bk, "l0bk", 1)):
            for tt in range(4):
                for c in range(DC):
                    B.mm(P.ps[pskey][:, :], wt[:, c * 128:(c + 1) * 128],
                         P.xn[c][:, tt * 512:(tt + 1) * 512],
                         start=(c == 0), stop=(c == DC - 1),
                         reads=(wkey, f"xn{c}"), writes=(f"ps{pskey}",))
                B.v("tensor_scalar", out=dst[:, tt * 512:(tt + 1) * 512],
                    in0=P.ps[pskey][:, :], scalar1=bias[:, 0:1], scalar2=None,
                    op0=ALU.add,
                    reads=(f"ps{pskey}", bkey), writes=("bfC0" if dst is qp else "bfC1",))
        load_wT_blocks(P, WwT, 2048 + hp * 128, 128, wv, "wS0")
        for tc in range(16):
            for c in range(DC):
                B.mm(P.ps[2][:, 0:128], P.xn[c][:, tc * 128:(tc + 1) * 128],
                     wv[:, c * 128:(c + 1) * 128],
                     start=(c == 0), stop=False,
                     reads=("wS0", f"xn{c}"), writes=("ps2",))
            B.mm(P.ps[2][:, 0:128], onesr[0:1, 0:128], bvr[0:1, :],
                 start=False, stop=True,
                 reads=("onesr", "l0bvr"), writes=("ps2",))
            B.v("tensor_copy", out=vtm[:, tc * 130:tc * 130 + 64],
                in_=P.ps[2][:, 0:64], reads=("ps2",), writes=("bfV",))
            B.v("tensor_copy", out=vtm[:, tc * 130 + 65:tc * 130 + 129],
                in_=P.ps[2][:, 64:128], reads=("ps2",), writes=("bfV",))
        B.s("activation", out=P.sq[0][:], in_=qp[:], func=AF.Square,
            reads=("bfC0",), writes=("sq0",))
        B.s("activation", out=P.sq[1][:], in_=kp[:], func=AF.Square,
            reads=("bfC1",), writes=("sq1",))
        for hh in range(2):
            base = 64 * hh
            for src_i, (sqt, sqk) in enumerate(((P.sq[0], "sq0"), (P.sq[1], "sq1"))):
                for tt in range(4):
                    B.mm(P.ps[3][0:1, :], P.ones_bf[base:base + 64, 0:1],
                         sqt[base:base + 64, tt * 512:(tt + 1) * 512],
                         start=True, stop=True,
                         reads=(sqk, "ones"), writes=("ps3",))
                    B.v("tensor_copy", out=P.row2[0:1, tt * 512:(tt + 1) * 512],
                        in_=P.ps[3][0:1, :], reads=("ps3",), writes=("tmpf0",))
                B.v("reduce_max", out=P.rows[src_i][0:1, src_i:src_i + 1],
                    in_=P.row2, axis=AX.X,
                    reads=("tmpf0",), writes=(('bc1', 'bc0')[src_i],))
            B.s("activation", out=P.rows[0][0:1, 0:1], in_=P.rows[0][0:1, 0:1],
                func=AF.Sqrt, reads=("bc1",), writes=("bc1",))
            B.s("activation", out=P.rows[1][0:1, 1:2], in_=P.rows[1][0:1, 1:2],
                func=AF.Sqrt, reads=("bc0",), writes=("bc0",))
            B.v("tensor_mul", out=P.rows[0][0:1, 2:3], in0=P.rows[0][0:1, 0:1],
                in1=P.rows[1][0:1, 1:2], reads=("bc1", "bc0"), writes=("bc1",))
            B.v("tensor_scalar", out=P.rows[0][0:1, 3:4], in0=P.rows[0][0:1, 2:3],
                scalar1=-0.125, scalar2=None, op0=ALU.mult,
                reads=("bc1",), writes=("bc1",))
            P.bcast_row(mh[hh][:, 0:1], P.rows[0][0:1, 3:4], 2 + hh, 1, 128,
                        "bc1", f"l0mh{hh}")
        for hh in range(2):
            base = 64 * hh
            for kc in range(16):
                for qt in range(4):
                    B.mm(P.ps[qt][:, :], kp[base:base + 64, kc * 128:(kc + 1) * 128],
                         qp[base:base + 64, qt * 512:(qt + 1) * 512],
                         start=True, stop=True,
                         reads=("bfC1", "bfC0"), writes=(f"ps{qt}",))
                for qt in range(4):
                    B.s("activation", out=pt[kc % 2][:, qt * 512:(qt + 1) * 512],
                        in_=P.ps[qt][:, :], func=AF.Exp, scale=0.125,
                        bias=mh[hh][:, 0:1],
                        reads=(f"ps{qt}", f"l0mh{hh}"), writes=(f"bfB{kc%2}",))
                for qt in range(4):
                    B.mm(P.ps[4 + qt][0:65, :],
                         vtm[:, kc * 130 + 65 * hh:kc * 130 + 65 * hh + 65],
                         pt[kc % 2][:, qt * 512:(qt + 1) * 512],
                         start=(kc == 0), stop=(kc == 15),
                         reads=("bfV", "vtm_ones", f"bfB{kc%2}"),
                         writes=(f"ps{4+qt}",))
            for qt in range(4):
                B.v("reciprocal", out=P.row2[0:1, qt * 512:(qt + 1) * 512],
                    in_=P.ps[4 + qt][64:65, :], reads=(f"ps{4+qt}",), writes=("tmpf0",))
            P.bcast_row(P.bc32[0][0:64, :], P.row2, 4, T, 64,
                        "tmpf0", "bc0")
            h = 2 * hp + hh
            for qt in range(4):
                B.v("tensor_mul", out=cat[h // 2][64 * (h % 2):64 * (h % 2) + 64,
                                                  qt * 512:(qt + 1) * 512],
                    in0=P.ps[4 + qt][0:64, :], in1=P.bc32[0][0:64, qt * 512:(qt + 1) * 512],
                    reads=(f"ps{4+qt}", "bc0"), writes=(f"bfA{h//2}",))
    wo_tiles = [qp, kp, pt[0], pt[1]]
    wo_keys = ["bfC0", "bfC1", "pt0", "pt1"]
    wo = [wo_tiles[hc // 2][:, (hc % 2) * 1024:(hc % 2) * 1024 + 1024]
          for hc in range(DC)]
    wokey = [wo_keys[hc // 2] for hc in range(DC)]
    for hc in range(DC):
        P.load_bf16(wo[hc], W["l0_owT"][hc * 128:(hc + 1) * 128, :], wokey[hc])
    for d in range(DC):
        load_col(P, ob[d], W["l0_ob"], d * 128, 128, f"l0ob{d}")
    for dc in range(DC):
        for tt in range(4):
            for hc in range(DC):
                B.mm(P.ps[tt % 4][:, :], wo[hc][:, dc * 128:(dc + 1) * 128],
                     cat[hc][:, tt * 512:(tt + 1) * 512],
                     start=(hc == 0), stop=(hc == DC - 1),
                     reads=(wokey[hc], f"bfA{hc}"), writes=(f"ps{tt%4}",))
            B.v("scalar_tensor_tensor",
                out=P.xs[dc][:, tt * 512:(tt + 1) * 512], in0=P.ps[tt % 4][:, :],
                scalar=ob[dc][:, 0:1], in1=P.xs[dc][:, tt * 512:(tt + 1) * 512],
                op0=ALU.add, op1=ALU.add,
                reads=(f"ps{tt%4}", f"l0ob{dc}", f"xs{dc}"), writes=(f"xs{dc}",))



def emit_l1_ffn(P, W):
    B = P.B
    wrw = pool(P, "wS0", [128, 1024], BF16)
    ident = pool(P, "identsb", [128, 128], F32)
    load_wT_blocks(P, W["l1_rwT"], 0, 16, wrw, "wS0")
    rb = pool(P, "l1rb", [16, 1], F32)
    load_col(P, rb, W["l1_rb"], 0, 16, "l1rb")
    B.dma(ident[:], W["ident"][:, :], writes=("ident",))
    s_fm = P.bc32[0][0:16, :]
    for tt in range(4):
        for c in range(DC):
            B.mm(P.ps[0][0:16, :], wrw[:, c * 16:(c + 1) * 16],
                 P.xn[c][:, tt * 512:(tt + 1) * 512],
                 start=(c == 0), stop=(c == DC - 1),
                 reads=("wS0", f"xn{c}"), writes=("ps0",))
        B.v("tensor_scalar", out=s_fm[:, tt * 512:(tt + 1) * 512],
            in0=P.ps[0][0:16, :], scalar1=rb[0:16, 0:1], scalar2=None, op0=ALU.add,
            reads=("ps0", "l1rb"), writes=("bc0",))
    s_tm = P.tmpf[0][:, 0:256]
    for tc in range(16):
        B.t("transpose", P.ps[1][:, tc % 2 * 16:tc % 2 * 16 + 16],
            s_fm[0:16, tc * 128:(tc + 1) * 128], ident[0:16, 0:16],
            reads=("bc0", "ident"), writes=("ps1",))
        B.v("tensor_copy", out=s_tm[:, tc * 16:(tc + 1) * 16],
            in_=P.ps[1][:, tc % 2 * 16:tc % 2 * 16 + 16],
            reads=("ps1",), writes=("tmpf0",))
    t0 = P.tmpf[0]
    s3 = s_tm.rearrange("p (c e) -> p c e", e=16)
    m1 = t0[:, 256:272]
    eq1 = t0[:, 288:544]
    msk = t0[:, 544:800]
    m2 = t0[:, 800:816]
    w1 = t0[:, 816:832]
    w2 = t0[:, 832:848]
    eq2 = t0[:, 848:1104]
    ctm = t0[:, 1104:1360]
    m1_3 = m1.rearrange("p (c e) -> p c e", e=1).broadcast_to((128, 16, 16))
    m2_3 = m2.rearrange("p (c e) -> p c e", e=1).broadcast_to((128, 16, 16))
    w1_3 = w1.rearrange("p (c e) -> p c e", e=1).broadcast_to((128, 16, 16))
    w2_3 = w2.rearrange("p (c e) -> p c e", e=1).broadcast_to((128, 16, 16))
    def T0(method, **kw):
        B.v(method, reads=("tmpf0",), writes=("tmpf0",), **kw)
    T0("tensor_reduce", out=m1, in_=s3, axis=AX.X, op=ALU.max)
    T0("tensor_tensor", out=eq1.rearrange("p (c e) -> p c e", e=16), in0=s3,
       in1=m1_3, op=ALU.is_equal)
    T0("scalar_tensor_tensor", out=msk, in0=eq1, scalar=NEG, in1=s_tm,
       op0=ALU.mult, op1=ALU.add)
    T0("tensor_reduce", out=m2, in_=msk.rearrange("p (c e) -> p c e", e=16),
       axis=AX.X, op=ALU.max)
    T0("tensor_sub", out=w1, in0=m1, in1=m2)
    B.s("activation", out=w1, in_=w1, func=AF.Sigmoid,
        reads=("tmpf0",), writes=("tmpf0",))
    T0("tensor_scalar", out=w2, in0=w1, scalar1=-1.0, scalar2=1.0,
       op0=ALU.mult, op1=ALU.add)
    T0("tensor_tensor", out=eq2.rearrange("p (c e) -> p c e", e=16), in0=s3,
       in1=m2_3, op=ALU.is_equal)
    T0("tensor_tensor", out=eq1.rearrange("p (c e) -> p c e", e=16),
       in0=eq1.rearrange("p (c e) -> p c e", e=16), in1=w1_3, op=ALU.mult)
    T0("tensor_tensor", out=eq2.rearrange("p (c e) -> p c e", e=16),
       in0=eq2.rearrange("p (c e) -> p c e", e=16), in1=w2_3, op=ALU.mult)
    T0("tensor_add", out=ctm, in0=eq1, in1=eq2)
    c_fm = P.bc32[1][0:16, :]
    for tc in range(16):
        B.t("transpose", P.ps[1][0:16, tc % 2 * 128:tc % 2 * 128 + 128],
            ctm[:, tc * 16:(tc + 1) * 16], ident[:],
            reads=("tmpf0", "ident"), writes=("ps1",))
        B.v("tensor_copy", out=c_fm[0:16, tc * 128:(tc + 1) * 128],
            in_=P.ps[1][0:16, tc % 2 * 128:tc % 2 * 128 + 128],
            reads=("ps1",), writes=("bc1",))
    B.dma(P.misc_dram[0:16, :], c_fm, reads=("bc1",), writes=("cdram",))
    wg = pool(P, "wS1", [128, 1024], BF16)
    bfb0 = pool(P, "bfB0", [128, T], BF16)
    wu = bfb0[:, 0:1024]
    hsh = [pool(P, f"bfA{j}", [128, T], BF16) for j in range(DC)]
    sgl = pool(P, "bfC0", [128, T], BF16)
    for j in range(DC):
        load_wT_blocks(P, W["l1_shgT"], j * 128, 128, wg, "wS1")
        load_wT_blocks(P, W["l1_shuT"], j * 128, 128, wu, "bfB0")
        for tt in range(4):
            for c in range(DC):
                B.mm(P.ps[0][:, :], wg[:, c * 128:(c + 1) * 128],
                     P.xn[c][:, tt * 512:(tt + 1) * 512],
                     start=(c == 0), stop=(c == DC - 1),
                     reads=("wS1", f"xn{c}"), writes=("ps0",))
                B.mm(P.ps[1][:, :], wu[:, c * 128:(c + 1) * 128],
                     P.xn[c][:, tt * 512:(tt + 1) * 512],
                     start=(c == 0), stop=(c == DC - 1),
                     reads=("bfB0", f"xn{c}"), writes=("ps1",))
            B.s("activation", out=sgl[:, tt * 512:(tt + 1) * 512], in_=P.ps[0][:, :],
                func=AF.Silu, reads=("ps0",), writes=("bfC0",))
            B.v("tensor_mul", out=hsh[j][:, tt * 512:(tt + 1) * 512],
                in0=sgl[:, tt * 512:(tt + 1) * 512], in1=P.ps[1][:, :],
                reads=("bfC0", "ps1"), writes=(f"bfA{j}",))
    wd = bfb0[:, 1024:2048]
    for dc in range(DC):
        load_wT_blocks(P, W["l1_shdT"], dc * 128, 128, wd, "bfB0")
        for tt in range(4):
            for j in range(DC):
                B.mm(P.ps[2 + tt % 2][:, :], wd[:, j * 128:(j + 1) * 128],
                     hsh[j][:, tt * 512:(tt + 1) * 512],
                     start=(j == 0), stop=(j == DC - 1),
                     reads=("bfB0", f"bfA{j}"), writes=(f"ps{2+tt%2}",))
            B.v("scalar_tensor_tensor",
                out=P.xs[dc][:, tt * 512:(tt + 1) * 512],
                in0=P.ps[2 + tt % 2][:, :], scalar=0.5,
                in1=P.xs[dc][:, tt * 512:(tt + 1) * 512],
                op0=ALU.mult, op1=ALU.add,
                reads=(f"ps{2+tt%2}", f"xs{dc}"), writes=(f"xs{dc}",))
    wup = [pool(P, "bfA0", [128, T], BF16), pool(P, "bfA1", [128, T], BF16)]
    wgt = [pool(P, "bfA2", [128, T], BF16), pool(P, "bfA3", [128, T], BF16)]
    wdn = [pool(P, "bfA4", [128, T], BF16), pool(P, "bfA5", [128, T], BF16)]
    he = [pool(P, "bfA6", [128, T], BF16), pool(P, "bfA7", [128, T], BF16)]
    ceb = [P.bc32[0], P.bc32[1]]
    cebk = ["bc0", "bc1"]
    for e in range(16):
        pe = e % 2
        B.dma(ceb[pe][:],
              P.misc_dram[e:e + 1, :].partition_broadcast(128).squeeze(1),
              reads=("cdram",), writes=(cebk[pe],))
        for half in range(2):
            for cc in range(4):
                c = half * 4 + cc
                P.load_bf16(wup[half][:, cc * 512:(cc + 1) * 512],
                            W["l1_up"][e, c * 128:(c + 1) * 128, :], f"bfA{half}")
                P.load_bf16(wgt[half][:, cc * 512:(cc + 1) * 512],
                            W["l1_gate"][e, c * 128:(c + 1) * 128, :], f"bfA{2+half}")
            for jc in range(2):
                P.load_bf16(wdn[half][:, jc * 1024:(jc + 1) * 1024],
                            W["l1_down"][e, (half * 2 + jc) * 128:(half * 2 + jc + 1) * 128, :],
                            f"bfA{4+half}")
        for jc in range(4):
            half, cc0 = jc // 2, 0
            for tt in range(4):
                for c in range(DC):
                    wt = wup[c // 4][:, (c % 4) * 512 + jc * 128:(c % 4) * 512 + jc * 128 + 128]
                    B.mm(P.ps[0][:, :], wt, P.xn[c][:, tt * 512:(tt + 1) * 512],
                         start=(c == 0), stop=(c == DC - 1),
                         reads=(f"bfA{c//4}", f"xn{c}"), writes=("ps0",))
                    wt2 = wgt[c // 4][:, (c % 4) * 512 + jc * 128:(c % 4) * 512 + jc * 128 + 128]
                    B.mm(P.ps[1][:, :], wt2, P.xn[c][:, tt * 512:(tt + 1) * 512],
                         start=(c == 0), stop=(c == DC - 1),
                         reads=(f"bfA{2+c//4}", f"xn{c}"), writes=("ps1",))
                B.s("activation", out=sgl[:, tt * 512:(tt + 1) * 512],
                    in_=P.ps[1][:, :], func=AF.Silu, reads=("ps1",), writes=("bfC0",))
                B.v("tensor_mul", out=sgl[:, tt * 512:(tt + 1) * 512],
                    in0=sgl[:, tt * 512:(tt + 1) * 512], in1=P.ps[0][:, :],
                    reads=("bfC0", "ps0"), writes=("bfC0",))
                B.v("tensor_mul", out=he[jc % 2][:, tt * 512:(tt + 1) * 512],
                    in0=sgl[:, tt * 512:(tt + 1) * 512],
                    in1=ceb[pe][:, tt * 512:(tt + 1) * 512],
                    reads=("bfC0", cebk[pe]), writes=(f"bfA{6+jc%2}",))
            if jc % 2 == 1:
                for dc in range(DC):
                    for tt in range(4):
                        for jj in range(2):
                            jcc = jc - 1 + jj
                            wt = wdn[jcc // 2][:, (jcc % 2) * 1024 + dc * 128:(jcc % 2) * 1024 + dc * 128 + 128]
                            B.mm(P.ps[2 + tt % 2][:, :], wt,
                                 he[jj][:, tt * 512:(tt + 1) * 512],
                                 start=(jcc == jc - 1), stop=(jcc == jc),
                                 reads=(f"bfA{4+jcc//2}", f"bfA{6+jj}"),
                                 writes=(f"ps{2+tt%2}",))
                        B.v("scalar_tensor_tensor",
                            out=P.xs[dc][:, tt * 512:(tt + 1) * 512],
                            in0=P.ps[2 + tt % 2][:, :], scalar=0.5,
                            in1=P.xs[dc][:, tt * 512:(tt + 1) * 512],
                            op0=ALU.mult, op1=ALU.add,
                            reads=(f"ps{2+tt%2}", f"xs{dc}"), writes=(f"xs{dc}",))



SC = 256
NSC = T // SC


def emit_l1_mamba(P, W):
    B = P.B
    emit_ln(P)
    for c in range(DC):
        B.dma(P.park_dram[c * 128:(c + 1) * 128, :], P.xs[c][:],
              reads=(f"xs{c}",), writes=(f"park{c}",))
        B.v("tensor_mul", out=P.tmpf[0][:], in0=P.xs[c][:], in1=P.bc32[0][:],
            reads=(f"xs{c}", "bc0"), writes=("tmpf0",))
        B.v("tensor_sub", out=P.xs[c][:], in0=P.tmpf[0][:], in1=P.bc32[1][:],
            reads=("tmpf0", "bc1"), writes=(f"xs{c}",))
    for i in range(4):
        emit_mamba_layer(P, W, i)
    for c in range(DC):
        B.dma(P.bc32[c % 2][:], P.park_dram[c * 128:(c + 1) * 128, :],
              reads=(f"park{c}",), writes=(f"bc{c%2}",))
        B.v("tensor_add", out=P.xs[c][:], in0=P.xs[c][:], in1=P.bc32[c % 2][:],
            reads=(f"xs{c}", f"bc{c%2}"), writes=(f"xs{c}",))


def emit_mamba_layer(P, W, i):
    B = P.B
    mxin = pool(P, "mxin", [128, 16 * 260], F32)
    xin = [mxin[:, m * 260:m * 260 + SC + 3] for m in range(16)]
    bfa = [pool(P, f"bfA{j}", [128, T], BF16) for j in range(DC)]
    xcb = [bfa[m // 8][:, (m % 8) * SC:(m % 8) * SC + SC] for m in range(16)]
    xck = [f"bfA{m // 8}" for m in range(16)]
    bfa6 = pool(P, "bfA6", [128, T], BF16)
    zld = [bfa6[:, j * SC:(j + 1) * SC] for j in range(2)]
    dtt = [xin[m][:, 0:SC] for m in range(16)]
    dtu = [bfa[2 + m // 8][:, (m % 8) * SC:(m % 8) * SC + SC] for m in range(16)]
    dtk = [f"bfA{2 + m // 8}" for m in range(16)]
    h_t = [bfa[4], bfa[5]]
    carry = pool(P, "mcarry", [128, 256], F32)
    def h_sl(n):
        return h_t[n // 8][:, (n % 8) * SC:(n % 8) * SC + SC]
    def h_key(n):
        return f"bfA{4 + n // 8}"
    tails = pool(P, "mtails", [128, 48], F32)
    nA = pool(P, "mnA", [128, 256], F32)
    dtb = pool(P, "mdtb", [128, 16], F32)
    cbt = pool(P, "mcb", [128, 16], F32)
    dsk = pool(P, "mD", [128, 16], F32)
    cwt = pool(P, "mcw", [128, 64], F32)
    bfb = [pool(P, "bfB0", [128, T], BF16), pool(P, "bfB1", [128, T], BF16)]
    xpw = bfb[1][:, 0:16 * 96]
    dtw = bfb[0][0:64, 0:T]
    dtrb = pool(P, "mdtrb", [64, SC], BF16)
    brow = pool(P, "mbrow", [16, SC], BF16)
    crow = pool(P, "mcrow", [16, SC], BF16)
    bfv = pool(P, "bfV", [128, 2210], BF16)
    seln = bfv[0:16, 0:2048]
    at = P.tmpf[0][:, 0:SC]
    bt = P.tmpf[0][:, SC:2 * SC]
    ca = at
    zsil = bt
    gtiles = [pool(P, f"mlg{c}", [128, 1], F32) for c in range(DC)]
    btiles = [pool(P, f"mlb{c}", [128, 1], F32) for c in range(DC)]
    onesr = pool(P, "onesr", [128, 130], BF16)
    wst = [pool(P, "wS0", [128, 1024], BF16), pool(P, "wS1", [128, 1024], BF16)]
    P.const_val(1.0)
    P.const_val(1e-5)

    for c in range(DC):
        load_col(P, gtiles[c], W[f"l1m_lnw{i}"], c * 128, 128, f"mlg{c}")
        load_col(P, btiles[c], W[f"l1m_lnb{i}"], c * 128, 128, f"mlb{c}")
    a3 = W[f"l1m_A{i}"].rearrange("(dc p) n -> p dc n", p=128)
    B.dma(P.tmpf[0][:, 0:256].rearrange("p (dc n) -> p dc n", n=16), a3,
          writes=("tmpf0",))
    B.s("activation", out=nA[:], in_=P.tmpf[0][:, 0:256], func=AF.Exp,
        reads=("tmpf0",), writes=("mnA",))
    B.v("tensor_scalar", out=nA[:], in0=nA[:], scalar1=-1.0, scalar2=None,
        op0=ALU.mult, reads=("mnA",), writes=("mnA",))
    cw3 = W[f"l1m_cw{i}"].rearrange("(dc p) w -> p dc w", p=128)
    B.dma(cwt[:].rearrange("p (dc w) -> p dc w", w=4), cw3, writes=("mcw",))
    for dc in range(16):
        load_col(P, dtb[:, dc:dc + 1], W[f"l1m_dtb{i}"], dc * 128, 128, "mdtb")
        load_col(P, cbt[:, dc:dc + 1], W[f"l1m_cb{i}"], dc * 128, 128, "mcb")
        load_col(P, dsk[:, dc:dc + 1], W[f"l1m_D{i}"], dc * 128, 128, "mD")
    load_wT_blocks(P, W[f"l1m_xpT{i}"], 0, 96, xpw, "bfB1", nk=16)
    P.load_bf16(dtw, W[f"l1m_dtT{i}"][0:64, :], "bfB0")
    B.dma(seln[0:16, :], W["seln"][0:16, :], writes=("bfV",), queue="gpsimd")
    B.g("memset", onesr[:], 1.0, writes=("onesr",))
    B.g("memset", carry[:], 0.0, writes=("mcarry",))
    B.g("memset", tails[:], 0.0, writes=("mtails",))

    emit_ln(P, gamma_beta=(
        [(gtiles[c][:, 0:1], f"mlg{c}") for c in range(DC)],
        [(btiles[c][:, 0:1], f"mlb{c}") for c in range(DC)]))

    inT = W[f"l1m_inT{i}"].rearrange("(c p) m -> p c m", p=128)
    outT = W[f"l1m_outT{i}"].rearrange("(m p) o -> p m o", p=128)
    for sc in range(NSC):
        s0 = sc * SC
        for mc in range(32):
            wt = wst[mc % 2]
            wkey = f"wS{mc % 2}"
            B.dma(wt[:].rearrange("p (c m) -> p c m", m=128),
                  inT[:, :, mc * 128:(mc + 1) * 128], writes=(wkey,),
                  queue="gpsimd")
            for c in range(DC):
                B.mm(P.ps[0][:, 0:SC], wt[:, c * 128:(c + 1) * 128],
                     P.xn[c][:, s0:s0 + SC],
                     start=(c == 0), stop=(c == DC - 1),
                     reads=(wkey, f"xn{c}"), writes=("ps0",))
            if mc < 16:
                B.v("tensor_copy", out=xin[mc][:, 3:3 + SC], in_=P.ps[0][:, 0:SC],
                    reads=("ps0",), writes=("mxin",))
            else:
                zc = mc - 16
                B.v("tensor_copy", out=zld[zc % 2], in_=P.ps[0][:, 0:SC],
                    reads=("ps0",), writes=("bfA6",))
                B.dma(P.z_dram[zc * 128:(zc + 1) * 128, :], zld[zc % 2],
                      reads=("bfA6",), writes=(f"zdram{zc}",))
        for mc in range(16):
            B.v("tensor_copy", out=xin[mc][:, 0:3], in_=tails[:, mc * 3:mc * 3 + 3],
                reads=("mtails",), writes=("mxin",))
            B.v("tensor_scalar", out=ca, in0=xin[mc][:, 0:SC],
                scalar1=cwt[:, mc * 4:mc * 4 + 1], scalar2=None, op0=ALU.mult,
                reads=("mxin", "mcw"), writes=("tmpf0",))
            for wv_ in range(1, 4):
                B.v("scalar_tensor_tensor", out=ca[:], in0=xin[mc][:, wv_:wv_ + SC],
                    scalar=cwt[:, mc * 4 + wv_:mc * 4 + wv_ + 1], in1=ca,
                    op0=ALU.mult, op1=ALU.add,
                    reads=("mxin", "mcw", "tmpf0"), writes=("tmpf0",))
            B.v("tensor_copy", out=tails[:, mc * 3:mc * 3 + 3],
                in_=xin[mc][:, SC:SC + 3],
                reads=("mxin",), writes=("mtails",))
            B.s("activation", out=xcb[mc][:], in_=ca[:], func=AF.Silu,
                bias=cbt[:, mc:mc + 1], reads=("tmpf0", "mcb"), writes=(xck[mc],))
        for mc in range(16):
            B.mm(P.ps[1][0:64, 0:SC], xpw[:, mc * 96:mc * 96 + 64], xcb[mc],
                 start=(mc == 0), stop=(mc == 15),
                 reads=("bfB1", xck[mc]), writes=("ps1",))
        for mc in range(16):
            B.mm(P.ps[6][0:16, 0:SC], xpw[:, mc * 96 + 64:mc * 96 + 80], xcb[mc],
                 start=(mc == 0), stop=(mc == 15),
                 reads=("bfB1", xck[mc]), writes=("ps6",))
        for mc in range(16):
            B.mm(P.ps[7][0:16, 0:SC], xpw[:, mc * 96 + 80:mc * 96 + 96], xcb[mc],
                 start=(mc == 0), stop=(mc == 15),
                 reads=("bfB1", xck[mc]), writes=("ps7",))
        B.v("tensor_copy", out=dtrb[0:64, :], in_=P.ps[1][0:64, 0:SC],
            reads=("ps1",), writes=("mdtrb",))
        B.v("tensor_copy", out=brow[0:16, :], in_=P.ps[6][0:16, 0:SC],
            reads=("ps6",), writes=("mbrow",))
        B.v("tensor_copy", out=crow[0:16, :], in_=P.ps[7][0:16, 0:SC],
            reads=("ps7",), writes=("mcrow",))
        for mc in range(16):
            B.mm(P.ps[2][:, 0:SC], dtw[0:64, mc * 128:(mc + 1) * 128], dtrb[0:64, :],
                 start=True, stop=True, reads=("bfB0", "mdtrb"), writes=("ps2",))
            B.v("tensor_scalar", out=dtt[mc], in0=P.ps[2][:, 0:SC],
                scalar1=dtb[:, mc:mc + 1], scalar2=None, op0=ALU.add,
                reads=("ps2", "mdtb"), writes=("mxin",))
            B.v("scalar_tensor_tensor", out=at, in0=dtt[mc], scalar=-1.0,
                in1=dtt[mc], op0=ALU.mult, op1=ALU.max,
                reads=("mxin",), writes=("tmpf0",))
            B.s("activation", out=at, in_=at, func=AF.Exp, scale=-1.0,
                reads=("tmpf0",), writes=("tmpf0",))
            B.s("activation", out=at, in_=at, func=AF.Ln, bias=1.0,
                reads=("tmpf0",), writes=("tmpf0",))
            B.v("scalar_tensor_tensor", out=dtt[mc], in0=dtt[mc], scalar=0.0,
                in1=at, op0=ALU.max, op1=ALU.add,
                reads=("mxin", "tmpf0"), writes=("mxin",))
            B.v("tensor_mul", out=dtu[mc], in0=dtt[mc], in1=xcb[mc],
                reads=("mxin", xck[mc]), writes=(dtk[mc],))
        for dc in range(16):
            for n in range(16):
                B.mm(P.ps[3][:, 0:SC], seln[0:16, n * 128:(n + 1) * 128],
                     brow[0:16, :],
                     start=True, stop=True, reads=("bfV", "mbrow"), writes=("ps3",))
                B.s("activation", out=at, in_=dtt[dc], func=AF.Exp,
                    scale=nA[:, dc * 16 + n:dc * 16 + n + 1],
                    reads=("mxin", "mnA"), writes=("tmpf0",))
                B.v("tensor_mul", out=bt, in0=dtu[dc], in1=P.ps[3][:, 0:SC],
                    reads=(dtk[dc], "ps3"), writes=("tmpf0",))
                B.v("tensor_tensor_scan", out=h_sl(n),
                    data0=at, data1=bt,
                    initial=carry[:, dc * 16 + n:dc * 16 + n + 1],
                    op0=ALU.mult, op1=ALU.add,
                    reads=("tmpf0", "tmpf0", "mcarry"), writes=(h_key(n),))
            for hb in range(2):
                B.v("tensor_copy", out=carry[:, dc * 16 + 8 * hb:dc * 16 + 8 * hb + 8],
                    in_=h_t[hb][:, 0:8 * SC].rearrange("p (n t) -> p n t", t=SC)[:, :, SC - 1:SC].squeeze(2),
                    reads=(f"bfA{4+hb}",), writes=("mcarry",))
            for n in range(16):
                B.mm(P.ps[4][:, 0:SC], seln[0:16, n * 128:(n + 1) * 128],
                     crow[0:16, :],
                     start=True, stop=True, reads=("bfV", "mcrow"), writes=("ps4",))
                B.v("tensor_mul", out=h_sl(n),
                    in0=h_sl(n), in1=P.ps[4][:, 0:SC],
                    reads=(h_key(n), "ps4"), writes=(h_key(n),))
            B.v("tensor_reduce", out=ca,
                in_=h_t[0][:, 0:8 * SC].rearrange("p (n t) -> p t n", t=SC),
                axis=AX.X, op=ALU.add, reads=("bfA4",), writes=("tmpf0",))
            B.v("tensor_reduce", out=bt,
                in_=h_t[1][:, 0:8 * SC].rearrange("p (n t) -> p t n", t=SC),
                axis=AX.X, op=ALU.add, reads=("bfA5",), writes=("tmpf0",))
            B.v("tensor_add", out=ca, in0=ca, in1=bt,
                reads=("tmpf0", "tmpf0"), writes=("tmpf0",))
            B.v("scalar_tensor_tensor", out=ca, in0=xcb[dc],
                scalar=dsk[:, dc:dc + 1], in1=ca, op0=ALU.mult, op1=ALU.add,
                reads=(xck[dc], "mD", "tmpf0"), writes=("tmpf0",))
            B.dma(zld[dc % 2], P.z_dram[dc * 128:(dc + 1) * 128, :],
                  reads=(f"zdram{dc}",), writes=("bfA6",))
            B.s("activation", out=zsil[:], in_=zld[dc % 2], func=AF.Silu,
                reads=("bfA6",), writes=("tmpf0",))
            B.v("tensor_mul", out=xcb[dc], in0=ca, in1=zsil,
                reads=("tmpf0", "tmpf0"), writes=(xck[dc],))
        for oc in range(DC):
            for hh in range(2):
                B.dma(wst[hh][:].rearrange("p (c m) -> p c m", m=128),
                      outT[:, hh * 8:(hh + 1) * 8, oc * 128:(oc + 1) * 128],
                      writes=(f"wS{hh}",), queue="gpsimd")
            for mc in range(16):
                B.mm(P.ps[5][:, 0:SC], wst[mc // 8][:, (mc % 8) * 128:(mc % 8) * 128 + 128],
                     xcb[mc],
                     start=(mc == 0), stop=(mc == 15),
                     reads=(f"wS{mc//8}", xck[mc]), writes=("ps5",))
            B.v("tensor_add", out=P.xs[oc][:, s0:s0 + SC],
                in0=P.xs[oc][:, s0:s0 + SC], in1=P.ps[5][:, 0:SC],
                reads=("ps5", f"xs{oc}"), writes=(f"xs{oc}",))



def emit_l2_ffn(P, W):
    B = P.B
    ident = pool(P, "identsb", [128, 128], F32)
    B.dma(ident[:], W["ident"][:, :], writes=("ident",))
    rw = pool(P, "l2rw", [128, 128], BF16)
    load_wT_blocks(P, W["l2_rw"], 0, 16, rw, "l2rw")
    rb = pool(P, "l1rb", [16, 1], F32)
    load_col(P, rb, W["l2_rb"], 0, 16, "l1rb")
    s_fm = P.bc32[0][0:16, :]
    for tt in range(4):
        for c in range(DC):
            B.mm(P.ps[0][0:16, :], rw[:, c * 16:(c + 1) * 16],
                 P.xn[c][:, tt * 512:(tt + 1) * 512],
                 start=(c == 0), stop=(c == DC - 1),
                 reads=("l2rw", f"xn{c}"), writes=("ps0",))
        B.v("tensor_scalar", out=s_fm[:, tt * 512:(tt + 1) * 512],
            in0=P.ps[0][0:16, :], scalar1=rb[0:16, 0:1], scalar2=None, op0=ALU.add,
            reads=("ps0", "l1rb"), writes=("bc0",))
    s_tm = P.tmpf[0][:, 0:256]
    for tc in range(16):
        B.t("transpose", P.ps[1][:, tc % 2 * 16:tc % 2 * 16 + 16],
            s_fm[0:16, tc * 128:(tc + 1) * 128], ident[0:16, 0:16],
            reads=("bc0", "ident"), writes=("ps1",))
        B.v("tensor_copy", out=s_tm[:, tc * 16:(tc + 1) * 16],
            in_=P.ps[1][:, tc % 2 * 16:tc % 2 * 16 + 16],
            reads=("ps1",), writes=("tmpf0",))
    t0 = P.tmpf[0]
    s3 = s_tm.rearrange("p (c e) -> p c e", e=16)
    ms = [t0[:, 256 + 16 * j:272 + 16 * j] for j in range(4)]
    msk = t0[:, 320:576]
    eqs = t0[:, 576:832]
    ctm = t0[:, 832:1088]
    esum = t0[:, 1088:1104]
    wj = t0[:, 1104:1120]
    ssrc = [s_tm, msk]

    def T0(method, **kw):
        B.v(method, reads=("tmpf0",), writes=("tmpf0",), **kw)
    T0("tensor_copy", out=msk, in_=s_tm)
    for j in range(4):
        T0("tensor_reduce", out=ms[j], in_=msk.rearrange("p (c e) -> p c e", e=16),
           axis=AX.X, op=ALU.max)
        if j < 3:
            mb = ms[j].rearrange("p (c e) -> p c e", e=1).broadcast_to((128, 16, 16))
            T0("tensor_tensor", out=eqs.rearrange("p (c e) -> p c e", e=16),
               in0=msk.rearrange("p (c e) -> p c e", e=16), in1=mb, op=ALU.is_equal)
            T0("scalar_tensor_tensor", out=msk, in0=eqs, scalar=NEG, in1=msk,
               op0=ALU.mult, op1=ALU.add)
    B.g("memset", esum, 0.0, writes=("tmpf0",))
    for j in (3, 2, 1, 0):
        T0("tensor_sub", out=wj, in0=ms[j], in1=ms[0])
        B.s("activation", out=ms[j], in_=wj, func=AF.Exp,
            reads=("tmpf0",), writes=("tmpf0",))
        T0("tensor_add", out=esum, in0=esum, in1=ms[j])
    B.v("reciprocal", out=esum, in_=esum, reads=("tmpf0",), writes=("tmpf0",))
    B.g("memset", ctm, 0.0, writes=("tmpf0",))
    T0("tensor_copy", out=msk, in_=s_tm)
    for j in range(4):
        T0("tensor_mul", out=ms[j], in0=ms[j], in1=esum)
        T0("tensor_reduce", out=wj, in_=msk.rearrange("p (c e) -> p c e", e=16),
           axis=AX.X, op=ALU.max)
        wb = wj.rearrange("p (c e) -> p c e", e=1).broadcast_to((128, 16, 16))
        T0("tensor_tensor", out=eqs.rearrange("p (c e) -> p c e", e=16),
           in0=msk.rearrange("p (c e) -> p c e", e=16), in1=wb, op=ALU.is_equal)
        T0("scalar_tensor_tensor", out=msk, in0=eqs, scalar=NEG, in1=msk,
           op0=ALU.mult, op1=ALU.add)
        wgt = ms[j].rearrange("p (c e) -> p c e", e=1).broadcast_to((128, 16, 16))
        T0("tensor_tensor", out=eqs.rearrange("p (c e) -> p c e", e=16),
           in0=eqs.rearrange("p (c e) -> p c e", e=16), in1=wgt, op=ALU.mult)
        T0("tensor_add", out=ctm, in0=ctm, in1=eqs)
    c_fm = P.bc32[1][0:16, :]
    for tc in range(16):
        B.t("transpose", P.ps[1][0:16, tc % 2 * 128:tc % 2 * 128 + 128],
            ctm[:, tc * 16:(tc + 1) * 16], ident[:],
            reads=("tmpf0", "ident"), writes=("ps1",))
        B.v("tensor_copy", out=c_fm[0:16, tc * 128:(tc + 1) * 128],
            in_=P.ps[1][0:16, tc % 2 * 128:tc % 2 * 128 + 128],
            reads=("ps1",), writes=("bc1",))
    B.dma(P.misc_dram[0:16, :], c_fm, reads=("bc1",), writes=("cdram",))
    w1t = [pool(P, f"bfA{j}", [128, T], BF16) for j in range(DC)]
    hts = [pool(P, "bfB0", [128, T], BF16), pool(P, "bfB1", [128, T], BF16),
           pool(P, "bfC0", [128, T], BF16), pool(P, "bfC1", [128, T], BF16)]
    w2t = pool(P, "bfV", [128, 2210], BF16)
    b1c = pool(P, "l2b1", [128, 16], F32)
    b2r = P.tmpf[0][0:1, 1024:1536].bitcast(BF16)
    cer = P.tmpf[0][0:1, 0:1024].bitcast(BF16)
    ceb = [P.bc32[0], P.bc32[1]]
    cebk = ["bc0", "bc1"]

    for e in range(16):
        pe = e % 2
        B.dma(ceb[pe][:],
              P.misc_dram[e:e + 1, :].partition_broadcast(128).squeeze(1),
              reads=("cdram",), writes=(cebk[pe],))
        B.dma(cer, P.misc_dram[e:e + 1, :], reads=("cdram",),
              writes=("tmpf0",), queue="gpsimd")
        B.dma(b2r, W["l2_b2"][e:e + 1, :], reads=(), writes=("tmpf0",),
              queue="gpsimd")
        for c in range(DC):
            B.dma(w1t[c][:], W["l2_w1"][e, c * 128:(c + 1) * 128, :],
                  writes=(f"bfA{c}",), queue="gpsimd")
        for ic in range(16):
            load_col(P, b1c[:, ic:ic + 1], W["l2_b1"][e], ic * 128, 128, "l2b1")
        for tt in range(4):
            for ic in range(16):
                for c in range(DC):
                    B.mm(P.ps[6][:, :], w1t[c][:, ic * 128:(ic + 1) * 128],
                         P.xn[c][:, tt * 512:(tt + 1) * 512],
                         start=(c == 0), stop=(c == DC - 1),
                         reads=(f"bfA{c}", f"xn{c}"), writes=("ps6",))
                B.s("activation", out=hts[ic // 4][:, (ic % 4) * 512:(ic % 4) * 512 + 512],
                    in_=P.ps[6][:, :], func=AF.Gelu, bias=b1c[:, ic:ic + 1],
                    reads=("ps6", "l2b1"), writes=(f"bf{'BC'[ic//8]}{(ic//4)%2}",))
                B.v("tensor_mul",
                    out=hts[ic // 4][:, (ic % 4) * 512:(ic % 4) * 512 + 512],
                    in0=hts[ic // 4][:, (ic % 4) * 512:(ic % 4) * 512 + 512],
                    in1=ceb[pe][:, tt * 512:(tt + 1) * 512],
                    reads=(f"bf{'BC'[ic//8]}{(ic//4)%2}", cebk[pe]),
                    writes=(f"bf{'BC'[ic//8]}{(ic//4)%2}",))
            for dc in range(DC):
                B.dma(w2t[:, 0:2048].rearrange("p (c o) -> p c o", o=128),
                      W["l2_w2"][e].rearrange("(c p) o -> p c o", p=128)[:, :, dc * 128:(dc + 1) * 128],
                      writes=("bfV",), queue="gpsimd")
                for ic in range(16):
                    B.mm(P.ps[2 + dc % 2][:, :],
                         w2t[:, ic * 128:(ic + 1) * 128],
                         hts[ic // 4][:, (ic % 4) * 512:(ic % 4) * 512 + 512],
                         start=(ic == 0), stop=False,
                         reads=("bfV", f"bf{'BC'[ic//8]}{(ic//4)%2}"),
                         writes=(f"ps{2+dc%2}",))
                B.mm(P.ps[2 + dc % 2][:, :], b2r[0:1, dc * 128:(dc + 1) * 128],
                     cer[0:1, tt * 512:(tt + 1) * 512],
                     start=False, stop=True,
                     reads=("tmpf0",), writes=(f"ps{2+dc%2}",))
                B.v("scalar_tensor_tensor",
                    out=P.xs[dc][:, tt * 512:(tt + 1) * 512],
                    in0=P.ps[2 + dc % 2][:, :], scalar=0.5,
                    in1=P.xs[dc][:, tt * 512:(tt + 1) * 512],
                    op0=ALU.mult, op1=ALU.add,
                    reads=(f"ps{2+dc%2}", f"xs{dc}"), writes=(f"xs{dc}",))



def emit_l2_attn(P, W):
    B = P.B
    gt = [pool(P, f"gbg{c}", [128, 1], F32) for c in range(DC)]
    bt_ = [pool(P, f"gbb{c}", [128, 1], F32) for c in range(DC)]
    for c in range(DC):
        load_col(P, gt[c], W["l2a_lng"], c * 128, 128, f"gbg{c}")
        load_col(P, bt_[c], W["l2a_lnb"], c * 128, 128, f"gbb{c}")
    emit_ln2(P, gamma_beta=(
        [(gt[c][:, 0:1], f"gbg{c}") for c in range(DC)],
        [(bt_[c][:, 0:1], f"gbb{c}") for c in range(DC)]))
    ktm = [pool(P, f"bfA{j}", [128, T], BF16) for j in range(DC)]
    wres = [pool(P, "bfB0", [128, T], BF16), pool(P, "bfB1", [128, T], BF16),
            pool(P, "bfC0", [128, T], BF16), pool(P, "bfC1", [128, T], BF16)]
    wkeys = ["bfB0", "bfB1", "bfC0", "bfC1"]
    bfv = pool(P, "bfV", [128, 2210], BF16)
    wst = [pool(P, "wS0", [128, 1024], BF16), pool(P, "wS1", [128, 1024], BF16)]
    brow = P.tmpf[0][0:1, 1536:1792].bitcast(BF16)
    bcol = pool(P, "l2bcol", [128, 8], F32)
    ksum = pool(P, "l2ksum", [128, 8], BF16)
    vl = None
    kp = pool(P, "mxin", [128, 16 * 260], F32).bitcast(BF16)[:, 0:2176]
    vl = [kp[:, 0:512], kp[:, 512:1024]]
    vlk = ["mxin", "mxin"]
    bias3 = ktm[1][:, 1024:1792]
    mb = pool(P, "l2mb", [16, 2], F32)
    mh = [pool(P, f"l0mh{i}", [128, 1], F32) for i in range(2)]
    wt = [ktm[1][0:64, 0:512], ktm[1][0:64, 512:1024]]
    wtk = ["bfA1", "bfA1"]
    onesr = pool(P, "onesr", [128, 130], BF16)
    bvr = pool(P, "l0bvr", [1, 128], BF16)
    B.g("memset", onesr[:], 1.0, writes=("onesr",))

    def proj_feature_major(wname, bname, dst_tiles, dstkeys, epilogue):
        for oc in range(DC):
            B.dma(wst[oc % 2][:].rearrange("p (c m) -> p c m", m=128),
                  W[wname].rearrange("(c p) m -> p c m", p=128)[:, :, oc * 128:(oc + 1) * 128],
                  writes=(f"wS{oc%2}",), queue="gpsimd")
            load_col(P, bcol[:, oc:oc + 1], W[bname], oc * 128, 128, "l2bcol")
            for tt in range(4):
                for c in range(DC):
                    B.mm(P.ps[tt % 2][:, :], wst[oc % 2][:, c * 128:(c + 1) * 128],
                         P.xn[c][:, tt * 512:(tt + 1) * 512],
                         start=(c == 0), stop=(c == DC - 1),
                         reads=(f"wS{oc%2}", f"xn{c}"), writes=(f"ps{tt%2}",))
                epilogue(oc, tt, P.ps[tt % 2], bcol[:, oc:oc + 1])

    def tm_proj(wname, bname, out_cb):
        for c in range(DC):
            B.dma(wres[c // 2][:, (c % 2) * 1024:(c % 2) * 1024 + 1024],
                  W[wname][c * 128:(c + 1) * 128, :],
                  writes=(wkeys[c // 2],), queue="gpsimd")
        for nv in range(2):
            B.dma(brow,
                  W[bname][nv * 512:(nv + 1) * 512].rearrange("(o n) -> o n", o=1),
                  writes=("tmpf0",), queue="gpsimd")
            for tc in range(16):
                for c in range(DC):
                    B.mm(P.ps[nv][:, :], P.xn[c][:, tc * 128:(tc + 1) * 128],
                         wres[c // 2][:, (c % 2) * 1024 + nv * 512:(c % 2) * 1024 + nv * 512 + 512],
                         start=(c == 0), stop=False,
                         reads=(wkeys[c // 2], f"xn{c}"), writes=(f"ps{nv}",))
                B.mm(P.ps[nv][:, :], onesr[0:1, 0:128], brow,
                     start=False, stop=True,
                     reads=("onesr", "tmpf0"), writes=(f"ps{nv}",))
                out_cb(tc, nv, P.ps[nv])

    t0 = P.tmpf[0]

    def elu1_from_psum(ps_ap, dst_ap, dkey, pskey, bias_col=None):
        if bias_col is None:
            B.v("tensor_scalar", out=t0[:, 0:512], in0=ps_ap, scalar1=0.0,
                scalar2=None, op0=ALU.min, reads=(pskey,), writes=("tmpf0",))
            B.v("tensor_scalar", out=t0[:, 512:1024], in0=ps_ap, scalar1=0.0,
                scalar2=None, op0=ALU.max, reads=(pskey,), writes=("tmpf0",))
        else:
            B.v("tensor_scalar", out=t0[:, 0:512], in0=ps_ap, scalar1=bias_col,
                scalar2=0.0, op0=ALU.add, op1=ALU.min,
                reads=(pskey, "l2bcol"), writes=("tmpf0",))
            B.v("tensor_scalar", out=t0[:, 512:1024], in0=ps_ap, scalar1=bias_col,
                scalar2=0.0, op0=ALU.add, op1=ALU.max,
                reads=(pskey, "l2bcol"), writes=("tmpf0",))
        B.s("activation", out=t0[:, 0:512], in_=t0[:, 0:512], func=AF.Exp,
            reads=("tmpf0",), writes=("tmpf0",))
        B.v("tensor_add", out=dst_ap, in0=t0[:, 0:512], in1=t0[:, 512:1024],
            reads=("tmpf0",), writes=(dkey,))

    def k_cb2(tc, nv, ps):
        dst = ktm[tc // 2][:, (tc % 2) * 1024 + nv * 512:(tc % 2) * 1024 + nv * 512 + 512]
        elu1_from_psum(ps[:, :], dst, f"bfA{tc//2}", f"ps{nv}")

    tm_proj("l2a_wkl", "l2a_bkl", k_cb2)
    def v_cb(tc, nv, ps):
        dst = bfv[:, nv * 512:nv * 512 + 512]
        B.v("tensor_copy", out=dst, in_=ps[:, :], reads=(f"ps{nv}",), writes=("bfV",))
        B.dma(P.vlin_dram[tc * 128:(tc + 1) * 128, nv * 512:(nv + 1) * 512], dst,
              reads=("bfV",), writes=("vlind",))
    tm_proj("l2a_wvl", "l2a_bvl", v_cb)
    kvsb = wres
    for ev in range(2):
        for tc in range(16):
            B.dma(vl[tc % 2], P.vlin_dram[tc * 128:(tc + 1) * 128,
                                          ev * 512:(ev + 1) * 512],
                  reads=("vlind",), writes=("mxin",))
            for dm in range(DC):
                B.mm(P.ps[dm][:, :],
                     ktm[tc // 2][:, (tc % 2) * 1024 + dm * 128:(tc % 2) * 1024 + dm * 128 + 128],
                     vl[tc % 2],
                     start=(tc == 0), stop=(tc == 15),
                     reads=(f"bfA{tc//2}", "mxin"), writes=(f"ps{dm}",))
        for dm in range(DC):
            B.v("tensor_copy",
                out=kvsb[dm // 2][:, (dm % 2) * 1024 + ev * 512:(dm % 2) * 1024 + ev * 512 + 512],
                in_=P.ps[dm][:, :], reads=(f"ps{dm}",), writes=(wkeys[dm // 2],))
    for dm in range(DC):
        for tc in range(16):
            B.mm(P.ps[0][:, 0:1],
                 ktm[tc // 2][:, (tc % 2) * 1024 + dm * 128:(tc % 2) * 1024 + dm * 128 + 128],
                 onesr[:, 128:129],
                 start=(tc == 0), stop=(tc == 15),
                 reads=(f"bfA{tc//2}", "onesr"), writes=("ps0",))
        B.v("tensor_copy", out=ksum[:, dm:dm + 1], in_=P.ps[0][:, 0:1],
            reads=("ps0",), writes=("l2ksum",))
    qfm = ktm

    def q_cb(oc, tt, ps, bias_col):
        dst = qfm[oc][:, tt * 512:(tt + 1) * 512]
        elu1_from_psum(ps[:, :], dst, f"bfA{oc}", f"ps{tt%2}", bias_col=bias_col)
    proj_feature_major("l2a_wql", "l2a_bql", None, None, q_cb)
    P.const_val(1e-6)
    for tt in range(4):
        for dm in range(DC):
            B.mm(P.ps[7][0:1, :], ksum[:, dm:dm + 1],
                 qfm[dm][:, tt * 512:(tt + 1) * 512],
                 start=(dm == 0), stop=(dm == DC - 1),
                 reads=("l2ksum", f"bfA{dm}"), writes=("ps7",))
        B.v("tensor_scalar", out=P.rows[0][0:1, 0:512], in0=P.ps[7][0:1, :],
            scalar1=1e-6, scalar2=None, op0=ALU.add,
            reads=("ps7",), writes=("bc1",))
        B.v("reciprocal", out=P.rows[0][0:1, 0:512], in_=P.rows[0][0:1, 0:512],
            reads=("bc1",), writes=("bc1",))
        P.bcast_row(P.bc32[0][:, 0:512], P.rows[0][0:1, 0:512], 0, 512, 128,
                    "bc1", "bc0")
        for em in range(DC):
            for dm in range(DC):
                B.mm(P.ps[2 + em % 2][:, :],
                     kvsb[dm // 2][:, (dm % 2) * 1024 + em * 128:(dm % 2) * 1024 + em * 128 + 128],
                     qfm[dm][:, tt * 512:(tt + 1) * 512],
                     start=(dm == 0), stop=(dm == DC - 1),
                     reads=(wkeys[dm // 2], f"bfA{dm}"), writes=(f"ps{2+em%2}",))
            B.v("tensor_mul", out=bfv[:, (em % 2) * 512:(em % 2) * 512 + 512],
                in0=P.ps[2 + em % 2][:, :], in1=P.bc32[0][:, 0:512],
                reads=(f"ps{2+em%2}", "bc0"), writes=("bfV",))
            B.dma(P.lin_dram[em * 128:(em + 1) * 128, tt * 512:(tt + 1) * 512],
                  bfv[:, (em % 2) * 512:(em % 2) * 512 + 512],
                  reads=("bfV",), writes=("lind",))
    id16 = pool(P, "id16", [16, 16], F32)
    B.dma(id16[0:16, 0:16], W["ident"][0:16, 0:16], writes=("ident",))
    rbt = ktm[2][0:16, 0:129]
    B.dma(rbt[0:16, :], W["l2a_rbT"][:, :], writes=("bfA2",), queue="gpsimd")
    B.v("reduce_max", out=mb[0:16, 0:1], in_=rbt[0:16, :], axis=AX.X,
        reads=("bfA2",), writes=("l2mb",))
    mbrow = pool(P, "l2mbrow", [1, 16], F32)
    B.mm(P.ps[0][0:1, 0:16], mb[0:16, 0:1], id16[0:16, 0:16],
         start=True, stop=True, reads=("l2mb", "ident"), writes=("ps0",))
    B.v("tensor_copy", out=mbrow[0:1, 0:16], in_=P.ps[0][0:1, 0:16],
        reads=("ps0",), writes=("l2mbrow",))
    for hp in range(8):
        B.dma(wst[0][:].rearrange("p (c m) -> p c m", m=128),
              W["l2a_wqw"].rearrange("(c p) m -> p c m", p=128)[:, :, hp * 128:(hp + 1) * 128],
              writes=("wS0",), queue="gpsimd")
        B.dma(wst[1][:].rearrange("p (c m) -> p c m", m=128),
              W["l2a_wkw"].rearrange("(c p) m -> p c m", p=128)[:, :, hp * 128:(hp + 1) * 128],
              writes=("wS1",), queue="gpsimd")
        load_col(P, bcol[:, 0:1], W["l2a_bqw"], hp * 128, 128, "l2bcol")
        load_col(P, bcol[:, 1:2], W["l2a_bkw"], hp * 128, 128, "l2bcol")
        B.dma(bvr[0:1, :],
              W["l2a_bvw"][hp * 128:(hp + 1) * 128].rearrange("(o n) -> o n", o=1),
              writes=("l0bvr",), queue="gpsimd")
        for tt in range(4):
            for c in range(DC):
                B.mm(P.ps[0][:, :], wst[0][:, c * 128:(c + 1) * 128],
                     P.xn[c][:, tt * 512:(tt + 1) * 512],
                     start=(c == 0), stop=(c == DC - 1),
                     reads=("wS0", f"xn{c}"), writes=("ps0",))
                B.mm(P.ps[1][:, :], wst[1][:, c * 128:(c + 1) * 128],
                     P.xn[c][:, tt * 512:(tt + 1) * 512],
                     start=(c == 0), stop=(c == DC - 1),
                     reads=("wS1", f"xn{c}"), writes=("ps1",))
            B.v("tensor_scalar", out=ktm[0][:, tt * 512:(tt + 1) * 512],
                in0=P.ps[0][:, :], scalar1=bcol[:, 0:1], scalar2=None, op0=ALU.add,
                reads=("ps0", "l2bcol"), writes=("bfA0",))
            B.v("tensor_scalar", out=kp[:, 64 + tt * 512:64 + (tt + 1) * 512],
                in0=P.ps[1][:, :], scalar1=bcol[:, 1:2], scalar2=None, op0=ALU.add,
                reads=("ps1", "l2bcol"), writes=("mxin",))
        B.g("memset", kp[:, 0:64], 0.0, writes=("mxin",))
        B.g("memset", kp[:, 2112:2176], 0.0, writes=("mxin",))
        v3 = bfv[:].rearrange("p (k c) -> p k c", c=130)
        B.g("memset", v3[:, :, 64:65], 1.0, writes=("vtm_ones",))
        B.g("memset", v3[:, :, 129:130], 1.0, writes=("vtm_ones",))
        B.g("memset", bfv[0:64, 0:130], 0.0, writes=("bfV",))
        B.g("memset", bfv[64:128, 16 * 130:17 * 130], 0.0, writes=("bfV",))
        B.dma(wst[0][:].rearrange("p (c m) -> p c m", m=128),
              W["l2a_wvw"].rearrange("(c p) m -> p c m", p=128)[:, :, hp * 128:(hp + 1) * 128],
              writes=("wS0",), queue="gpsimd")
        for tc in range(16):
            for c in range(DC):
                B.mm(P.ps[2][:, 0:128], P.xn[c][:, tc * 128:(tc + 1) * 128],
                     wst[0][:, c * 128:(c + 1) * 128],
                     start=(c == 0), stop=False,
                     reads=("wS0", f"xn{c}"), writes=("ps2",))
            B.mm(P.ps[2][:, 0:128], onesr[0:1, 0:128], bvr[0:1, :],
                 start=False, stop=True, reads=("onesr", "l0bvr"), writes=("ps2",))
            for hh in range(2):
                B.v("tensor_copy",
                    out=bfv[64:128, tc * 130 + 65 * hh:tc * 130 + 65 * hh + 64],
                    in_=P.ps[2][0:64, 64 * hh:64 * hh + 64],
                    reads=("ps2",), writes=("bfV",))
                B.v("tensor_copy",
                    out=bfv[0:64, (tc + 1) * 130 + 65 * hh:(tc + 1) * 130 + 65 * hh + 64],
                    in_=P.ps[2][64:128, 64 * hh:64 * hh + 64],
                    reads=("ps2",), writes=("bfV",))
        B.s("activation", out=P.sq[0][:], in_=ktm[0][:], func=AF.Square,
            reads=("bfA0",), writes=("sq0",))
        B.s("activation", out=P.sq[1][:, 0:2048], in_=kp[:, 64:2112], func=AF.Square,
            reads=("mxin",), writes=("sq1",))
        for hh in range(2):
            base = 64 * hh
            for src_i in range(2):
                for tt in range(4):
                    B.mm(P.ps[3][0:1, :], P.ones_bf[base:base + 64, 0:1],
                         P.sq[src_i][base:base + 64, tt * 512:(tt + 1) * 512],
                         start=True, stop=True,
                         reads=(f"sq{src_i}", "ones"), writes=("ps3",))
                    B.v("tensor_copy", out=P.row2[0:1, tt * 512:(tt + 1) * 512],
                        in_=P.ps[3][0:1, :], reads=("ps3",), writes=("tmpf0",))
                B.v("reduce_max", out=P.rows[src_i][0:1, src_i:src_i + 1],
                    in_=P.row2, axis=AX.X, reads=("tmpf0",), writes=(('bc1', 'bc0')[src_i],))
            B.s("activation", out=P.rows[0][0:1, 0:1], in_=P.rows[0][0:1, 0:1],
                func=AF.Sqrt, reads=("bc1",), writes=("bc1",))
            B.s("activation", out=P.rows[1][0:1, 1:2], in_=P.rows[1][0:1, 1:2],
                func=AF.Sqrt, reads=("bc0",), writes=("bc0",))
            B.v("tensor_mul", out=P.rows[0][0:1, 2:3], in0=P.rows[0][0:1, 0:1],
                in1=P.rows[1][0:1, 1:2], reads=("bc1", "bc0"), writes=("bc1",))
            B.v("scalar_tensor_tensor", out=P.rows[0][0:1, 3:4],
                in0=P.rows[0][0:1, 2:3], scalar=0.125,
                in1=mbrow[0:1, 2 * hp + hh:2 * hp + hh + 1],
                op0=ALU.mult, op1=ALU.add, reads=("bc1", "l2mbrow"), writes=("bc1",))
            B.v("tensor_scalar", out=P.rows[0][0:1, 4:5], in0=P.rows[0][0:1, 3:4],
                scalar1=-1.0, scalar2=None, op0=ALU.mult,
                reads=("bc1",), writes=("bc1",))
            P.bcast_row(mh[hh][:, 0:1], P.rows[0][0:1, 4:5], 2 + hh, 1, 128,
                        "bc1", f"l0mh{hh}")
        for hh in range(2):
            base = 64 * hh
            h = 2 * hp + hh
            for var_ in range(3):
                B.dma(bias3[:, var_ * 256:(var_ + 1) * 256].rearrange(
                          "p (jh i) -> p jh i", i=128),
                      W["l2a_biasT"][var_, h].rearrange("(jh p) i -> p jh i", p=128),
                      writes=("bfA1",), queue="gpsimd")
            for qc in range(16):
                var = 0 if qc == 0 else (2 if qc == 15 else 1)
                for jh in range(2):
                    B.mm(P.ps[jh][:, 0:128],
                         kp[base:base + 64, qc * 128 + jh * 128:qc * 128 + jh * 128 + 128],
                         ktm[0][base:base + 64, qc * 128:(qc + 1) * 128],
                         start=True, stop=True,
                         reads=("mxin", "bfA0"), writes=(f"ps{jh}",))
                    B.v("scalar_tensor_tensor", out=P.tmpf[0][:, jh * 128:jh * 128 + 128],
                        in0=P.ps[jh][:, 0:128], scalar=0.125,
                        in1=bias3[:, var * 256 + jh * 128:var * 256 + jh * 128 + 128],
                        op0=ALU.mult, op1=ALU.add,
                        reads=(f"ps{jh}", "bfA1"), writes=("tmpf0",))
                    B.s("activation", out=P.sq[0][:, jh * 128:jh * 128 + 128],
                        in_=P.tmpf[0][:, jh * 128:jh * 128 + 128], func=AF.Exp,
                        bias=mh[hh][:, 0:1],
                        reads=("tmpf0", f"l0mh{hh}"), writes=("sq0",))
                for jh in range(2):
                    B.mm(P.ps[4 + qc // 4][0:65, (qc % 4) * 128:(qc % 4) * 128 + 128],
                         bfv[:, (qc + jh) * 130 + 65 * hh:(qc + jh) * 130 + 65 * hh + 65],
                         P.sq[0][:, jh * 128:jh * 128 + 128],
                         start=(jh == 0), stop=(jh == 1),
                         reads=("bfV", "vtm_ones", "sq0"), writes=(f"ps{4+qc//4}",))
            for qt in range(4):
                B.v("reciprocal", out=P.row2[0:1, qt * 512:(qt + 1) * 512],
                    in_=P.ps[4 + qt][64:65, :], reads=(f"ps{4+qt}",), writes=("tmpf0",))
            P.bcast_row(P.bc32[0][0:64, :], P.row2, 4, T, 64, "tmpf0", "bc0")
            for qt in range(4):
                B.v("tensor_mul", out=wt[qt % 2],
                    in0=P.ps[4 + qt][0:64, :], in1=P.bc32[0][0:64, qt * 512:(qt + 1) * 512],
                    reads=(f"ps{4+qt}", "bc0"), writes=(wtk[qt % 2],))
                B.dma(P.win_dram[h * 64:(h + 1) * 64, qt * 512:(qt + 1) * 512],
                      wt[qt % 2], reads=(wtk[qt % 2],), writes=("wind",))
    lint = [pool(P, "bfB0", [128, T], BF16), pool(P, "bfB1", [128, T], BF16)]
    wint = [pool(P, "bfC0", [128, T], BF16), pool(P, "bfC1", [128, T], BF16)]
    g1r = [pool(P, f"bfA{j}", [128, T], BF16) for j in range(DC)]
    mixt = [bfv[:, 0:T], kp[:, 0:T]]
    mixk = ["bfV", "mxin"]
    for tt in range(4):
        for kc in range(DC):
            B.dma(lint[kc // 4][:, (kc % 4) * 512:(kc % 4) * 512 + 512],
                  P.lin_dram[kc * 128:(kc + 1) * 128, tt * 512:(tt + 1) * 512],
                  reads=("lind",), writes=(wkeys[kc // 4],))
            B.dma(wint[kc // 4][:, (kc % 4) * 512:(kc % 4) * 512 + 512],
                  P.win_dram[kc * 128:(kc + 1) * 128, tt * 512:(tt + 1) * 512],
                  reads=("wind",), writes=(wkeys[2 + kc // 4],))
        for oc in range(DC):
            B.dma(wst[oc % 2][:].rearrange("p (c m) -> p c m", m=128),
                  W["l2a_wg1"].rearrange("(c p) m -> p c m", p=128)[:, 0:8, oc * 128:(oc + 1) * 128],
                  writes=(f"wS{oc%2}",), queue="gpsimd")
            for c in range(DC):
                B.mm(P.ps[oc % 2][:, :], wst[oc % 2][:, c * 128:(c + 1) * 128],
                     lint[c // 4][:, (c % 4) * 512:(c % 4) * 512 + 512],
                     start=(c == 0), stop=False,
                     reads=(f"wS{oc%2}", wkeys[c // 4]), writes=(f"ps{oc%2}",))
            B.dma(wst[oc % 2][:].rearrange("p (c m) -> p c m", m=128),
                  W["l2a_wg1"].rearrange("(c p) m -> p c m", p=128)[:, 8:16, oc * 128:(oc + 1) * 128],
                  writes=(f"wS{oc%2}",), queue="gpsimd")
            for c in range(DC):
                B.mm(P.ps[oc % 2][:, :], wst[oc % 2][:, c * 128:(c + 1) * 128],
                     wint[c // 4][:, (c % 4) * 512:(c % 4) * 512 + 512],
                     start=False, stop=False,
                     reads=(f"wS{oc%2}", wkeys[2 + c // 4]), writes=(f"ps{oc%2}",))
            B.dma(wst[oc % 2][:].rearrange("p (c m) -> p c m", m=128),
                  W["l2a_wg1"].rearrange("(c p) m -> p c m", p=128)[:, 16:24, oc * 128:(oc + 1) * 128],
                  writes=(f"wS{oc%2}",), queue="gpsimd")
            for c in range(DC):
                B.mm(P.ps[oc % 2][:, :], wst[oc % 2][:, c * 128:(c + 1) * 128],
                     P.xn[c][:, tt * 512:(tt + 1) * 512],
                     start=False, stop=(c == DC - 1),
                     reads=(f"wS{oc%2}", f"xn{c}"), writes=(f"ps{oc%2}",))
            load_col(P, bcol[:, oc:oc + 1], W["l2a_bg1"], oc * 128, 128, "l2bcol")
            B.v("tensor_scalar", out=g1r[oc][:, tt * 512:(tt + 1) * 512],
                in0=P.ps[oc % 2][:, :], scalar1=bcol[:, oc:oc + 1], scalar2=0.0,
                op0=ALU.add, op1=ALU.max,
                reads=(f"ps{oc%2}", "l2bcol"), writes=(f"bfA{oc}",))
    for tt in range(4):
        for kc in range(DC):
            B.dma(lint[kc // 4][:, (kc % 4) * 512:(kc % 4) * 512 + 512],
                  P.lin_dram[kc * 128:(kc + 1) * 128, tt * 512:(tt + 1) * 512],
                  reads=("lind",), writes=(wkeys[kc // 4],))
            B.dma(wint[kc // 4][:, (kc % 4) * 512:(kc % 4) * 512 + 512],
                  P.win_dram[kc * 128:(kc + 1) * 128, tt * 512:(tt + 1) * 512],
                  reads=("wind",), writes=(wkeys[2 + kc // 4],))
        for oc in range(DC):
            B.dma(wst[oc % 2][:].rearrange("p (c m) -> p c m", m=128),
                  W["l2a_wg2"].rearrange("(c p) m -> p c m", p=128)[:, :, oc * 128:(oc + 1) * 128],
                  writes=(f"wS{oc%2}",), queue="gpsimd")
            for c in range(DC):
                B.mm(P.ps[oc % 2][:, :], wst[oc % 2][:, c * 128:(c + 1) * 128],
                     g1r[c][:, tt * 512:(tt + 1) * 512],
                     start=(c == 0), stop=(c == DC - 1),
                     reads=(f"wS{oc%2}", f"bfA{c}"), writes=(f"ps{oc%2}",))
            load_col(P, bcol[:, oc:oc + 1], W["l2a_bg2"], oc * 128, 128, "l2bcol")
            B.s("activation", out=P.tmpf[0][:, 0:512], in_=P.ps[oc % 2][:, :],
                func=AF.Sigmoid, bias=bcol[:, oc:oc + 1],
                reads=(f"ps{oc%2}", "l2bcol"), writes=("tmpf0",))
            B.v("tensor_sub", out=P.tmpf[0][:, 512:1024],
                in0=lint[oc // 4][:, (oc % 4) * 512:(oc % 4) * 512 + 512],
                in1=wint[oc // 4][:, (oc % 4) * 512:(oc % 4) * 512 + 512],
                reads=(wkeys[oc // 4], wkeys[2 + oc // 4]), writes=("tmpf0",))
            B.v("tensor_mul", out=P.tmpf[0][:, 512:1024],
                in0=P.tmpf[0][:, 512:1024], in1=P.tmpf[0][:, 0:512],
                reads=("tmpf0",), writes=("tmpf0",))
            B.v("tensor_add", out=mixt[oc % 2][:, (oc // 2) * 512:(oc // 2) * 512 + 512],
                in0=P.tmpf[0][:, 512:1024],
                in1=wint[oc // 4][:, (oc % 4) * 512:(oc % 4) * 512 + 512],
                reads=("tmpf0", wkeys[2 + oc // 4]), writes=(mixk[oc % 2],))
        for dc in range(DC):
            B.dma(wst[dc % 2][:].rearrange("p (c m) -> p c m", m=128),
                  W["l2a_wo"].rearrange("(c p) m -> p c m", p=128)[:, :, dc * 128:(dc + 1) * 128],
                  writes=(f"wS{dc%2}",), queue="gpsimd")
            for c in range(DC):
                B.mm(P.ps[2 + dc % 2][:, :], wst[dc % 2][:, c * 128:(c + 1) * 128],
                     mixt[c % 2][:, (c // 2) * 512:(c // 2) * 512 + 512],
                     start=(c == 0), stop=(c == DC - 1),
                     reads=(f"wS{dc%2}", mixk[c % 2]), writes=(f"ps{2+dc%2}",))
            load_col(P, bcol[:, dc:dc + 1], W["l2a_bo"], dc * 128, 128, "l2bcol")
            B.v("scalar_tensor_tensor",
                out=P.xs[dc][:, tt * 512:(tt + 1) * 512],
                in0=P.ps[2 + dc % 2][:, :], scalar=bcol[:, dc:dc + 1],
                in1=P.xs[dc][:, tt * 512:(tt + 1) * 512],
                op0=ALU.add, op1=ALU.add,
                reads=(f"ps{2+dc%2}", "l2bcol", f"xs{dc}"), writes=(f"xs{dc}",))
    for c in range(DC):
        B.dma(P.bc32[c % 2][:], P.park_dram[c * 128:(c + 1) * 128, :],
              reads=(f"park{c}",), writes=(f"bc{c%2}",))
        B.v("tensor_add", out=P.xs[c][:], in0=P.xs[c][:], in1=P.bc32[c % 2][:],
            reads=(f"xs{c}", f"bc{c%2}"), writes=(f"xs{c}",))



def emit_ln_apply_tiles(P, tiles, keys, gamma_beta=None, act=None, eps=1e-5):
    B = P.B
    TT4 = T // 512
    for tt in range(TT4):
        for c in range(DC):
            B.mm(P.ps[4 + tt][0:1, :], P.ones_bf[:, 0:1],
                 tiles[c][:, tt * 512:(tt + 1) * 512],
                 start=(c == 0), stop=(c == DC - 1),
                 reads=(keys[c], "ones"), writes=(f"ps{4+tt}",))
        B.v("tensor_copy", out=P.rows[0][0:1, tt * 512:(tt + 1) * 512],
            in_=P.ps[4 + tt][0:1, :], reads=(f"ps{4+tt}",), writes=("bc1",))
    for c in range(DC):
        B.s("activation", out=P.sq[c % 2][:], in_=tiles[c][:], func=AF.Square,
            reads=(keys[c],), writes=(f"sq{c%2}",))
        for tt in range(TT4):
            B.mm(P.ps[4 + tt][0:1, :], P.ones_bf[:, 0:1],
                 P.sq[c % 2][:, tt * 512:(tt + 1) * 512],
                 start=(c == 0), stop=(c == DC - 1),
                 reads=(f"sq{c%2}", "ones"), writes=(f"ps{4+tt}",))
    for tt in range(TT4):
        B.v("tensor_copy", out=P.rows[1][0:1, tt * 512:(tt + 1) * 512],
            in_=P.ps[4 + tt][0:1, :], reads=(f"ps{4+tt}",), writes=("bc0",))
    B.v("scalar_tensor_tensor", out=P.row2, in0=P.rows[0][:],
        scalar=1.0 / (D * D), in1=P.rows[0][:], op0=ALU.mult, op1=ALU.mult,
        reads=("bc1",), writes=("tmpf0",))
    B.v("scalar_tensor_tensor", out=P.rows[1][:], in0=P.rows[1][:],
        scalar=1.0 / D, in1=P.row2, op0=ALU.mult, op1=ALU.subtract,
        reads=("bc0", "tmpf0"), writes=("bc0",))
    P.const_val(eps)
    B.s("activation", out=P.rows[1][:], in_=P.rows[1][:], func=AF.Ln, bias=eps,
        reads=("bc0",), writes=("bc0",))
    B.s("activation", out=P.rows[1][:], in_=P.rows[1][:], func=AF.Exp, scale=-0.5,
        reads=("bc0",), writes=("bc0",))
    B.v("scalar_tensor_tensor", out=P.rows[0][:], in0=P.rows[0][:],
        scalar=1.0 / D, in1=P.rows[1][:], op0=ALU.mult, op1=ALU.mult,
        reads=("bc1", "bc0"), writes=("bc1",))
    P.bcast_row(P.bc32[0][:], P.rows[1][:], 0, T, 128, "bc0", "bc0")
    P.bcast_row(P.bc32[1][:], P.rows[0][:], 1, T, 128, "bc1", "bc1")
    for c in range(DC):
        B.v("tensor_mul", out=P.tmpf[0][:], in0=tiles[c][:], in1=P.bc32[0][:],
            reads=(keys[c], "bc0"), writes=("tmpf0",))
        B.v("tensor_sub", out=P.tmpf[0][:], in0=P.tmpf[0][:], in1=P.bc32[1][:],
            reads=("tmpf0", "bc1"), writes=("tmpf0",))
        g_ap, g_key = gamma_beta[0][c]
        b_ap, b_key = gamma_beta[1][c]
        if act is None:
            B.v("tensor_scalar", out=P.xn[c][:], in0=P.tmpf[0][:],
                scalar1=g_ap, scalar2=b_ap, op0=ALU.mult, op1=ALU.add,
                reads=("tmpf0", g_key, b_key), writes=(f"xn{c}",))
        else:
            B.s("activation", out=P.xn[c][:], in_=P.tmpf[0][:], func=act,
                scale=g_ap, bias=b_ap,
                reads=("tmpf0", g_key, b_key), writes=(f"xn{c}",))


def emit_l3_attn(P, W):
    B = P.B
    gat = [pool(P, f"bfA{j}", [128, T], BF16) for j in range(DC)]
    qch = [pool(P, "bfC0", [128, T], BF16), pool(P, "bfC1", [128, T], BF16)]
    qk_ = ["bfC0", "bfC1"]
    kch = [pool(P, "bfB0", [128, T], BF16), pool(P, "bfB1", [128, T], BF16)]
    kk_ = ["bfB0", "bfB1"]
    bfv3 = pool(P, "bfV", [128, 2210], BF16)
    pt = bfv3[:, 0:T]
    wst = [pool(P, "wS0", [128, 1024], BF16), pool(P, "wS1", [128, 1024], BF16)]
    blk16 = pool(P, "l3blk", [128, 8], BF16)
    exp16 = pool(P, "mnA", [128, 256], F32)[0:8, 0:128]
    gsc = pt[0:64, 0:T]
    mcar = pool(P, "mcarry", [128, 256], F32).bitcast(BF16)
    vld = [mcar[:, j * 64:(j + 1) * 64] for j in range(2)]
    ssc = pool(P, "l3ssc", [128, 8], F32)
    B.dma(blk16[:], W["blk16"][:, :], writes=("l3blk",), queue="gpsimd")
    B.dma(exp16[0:8, :], W["exp16"][:, :], writes=("mnA",))
    P.const_val(1e-12)
    rr = P.bc32[1][0:8, :]

    def qk_proj(wname, dst, dkeys):
        for oc in range(2):
            B.dma(wst[oc % 2][:].rearrange("p (c m) -> p c m", m=128),
                  W[wname].rearrange("(c p) m -> p c m", p=128)[:, :, oc * 128:(oc + 1) * 128],
                  writes=(f"wS{oc%2}",), queue="gpsimd")
            for tt in range(4):
                for c in range(DC):
                    B.mm(P.ps[tt % 2][:, :], wst[oc % 2][:, c * 128:(c + 1) * 128],
                         P.xn[c][:, tt * 512:(tt + 1) * 512],
                         start=(c == 0), stop=(c == DC - 1),
                         reads=(f"wS{oc%2}", f"xn{c}"), writes=(f"ps{tt%2}",))
                B.v("tensor_copy", out=dst[oc][:, tt * 512:(tt + 1) * 512],
                    in_=P.ps[tt % 2][:, :], reads=(f"ps{tt%2}",), writes=(dkeys[oc],))
            B.s("activation", out=P.sq[0][:], in_=dst[oc][:], func=AF.Square,
                reads=(dkeys[oc],), writes=("sq0",))
            for tt in range(4):
                B.mm(P.ps[6][0:8, :], blk16[:, 0:8],
                     P.sq[0][:, tt * 512:(tt + 1) * 512],
                     start=True, stop=True, reads=("sq0", "l3blk"), writes=("ps6",))
                B.v("tensor_copy", out=rr[0:8, tt * 512:(tt + 1) * 512],
                    in_=P.ps[6][0:8, :], reads=("ps6",), writes=("bc1",))
            B.s("activation", out=rr[0:8, :], in_=rr[0:8, :], func=AF.Sqrt,
                reads=("bc1",), writes=("bc1",))
            B.v("tensor_scalar", out=rr[0:8, :], in0=rr[0:8, :], scalar1=1e-12,
                scalar2=None, op0=ALU.max, reads=("bc1",), writes=("bc1",))
            B.v("reciprocal", out=rr[0:8, :], in_=rr[0:8, :],
                reads=("bc1",), writes=("bc1",))
            for tt in range(4):
                B.mm(P.ps[6][:, :], exp16[0:8, :], rr[0:8, tt * 512:(tt + 1) * 512],
                     start=True, stop=True, reads=("mnA", "bc1"), writes=("ps6",))
                B.v("tensor_mul", out=dst[oc][:, tt * 512:(tt + 1) * 512],
                    in0=dst[oc][:, tt * 512:(tt + 1) * 512], in1=P.ps[6][:, :],
                    reads=(dkeys[oc], "ps6"), writes=(dkeys[oc],))
    qk_proj("l3a_WqgP", qch, qk_)
    qk_proj("l3a_WkvP", kch, kk_)
    for oc in range(2, 10):
        B.dma(wst[oc % 2][:].rearrange("p (c m) -> p c m", m=128),
              W["l3a_WqgP"].rearrange("(c p) m -> p c m", p=128)[:, :, oc * 128:(oc + 1) * 128],
              writes=(f"wS{oc%2}",), queue="gpsimd")
        for tt in range(4):
            for c in range(DC):
                B.mm(P.ps[tt % 2][:, :], wst[oc % 2][:, c * 128:(c + 1) * 128],
                     P.xn[c][:, tt * 512:(tt + 1) * 512],
                     start=(c == 0), stop=(c == DC - 1),
                     reads=(f"wS{oc%2}", f"xn{c}"), writes=(f"ps{tt%2}",))
            B.v("tensor_copy", out=gat[oc - 2][:, tt * 512:(tt + 1) * 512],
                in_=P.ps[tt % 2][:, :], reads=(f"ps{tt%2}",), writes=(f"bfA{oc-2}",))
    for tc in range(16):
        for nv in range(2):
            for c in range(DC):
                B.dma(wst[c % 2][:, 0:512],
                      W["l3a_WkvP"][c * 128:(c + 1) * 128,
                                    256 + nv * 512:256 + (nv + 1) * 512],
                      writes=(f"wS{c%2}",), queue="gpsimd")
                B.mm(P.ps[2 + nv][:, :], P.xn[c][:, tc * 128:(tc + 1) * 128],
                     wst[c % 2][:, 0:512],
                     start=(c == 0), stop=(c == DC - 1),
                     reads=(f"wS{c%2}", f"xn{c}"), writes=(f"ps{2+nv}",))
            B.s("activation", out=P.sq[0][:, 0:512], in_=P.ps[2 + nv][:, :],
                func=AF.Square, reads=(f"ps{2+nv}",), writes=("sq0",))
            B.v("tensor_reduce", out=ssc[:, 0:8],
                in_=P.sq[0][:, 0:512].rearrange("p (g v) -> p g v", v=64),
                axis=AX.X, op=ALU.add, reads=("sq0",), writes=("l3ssc",))
            B.s("activation", out=ssc[:, 0:8], in_=ssc[:, 0:8], func=AF.Sqrt,
                reads=("l3ssc",), writes=("l3ssc",))
            B.v("tensor_scalar", out=ssc[:, 0:8], in0=ssc[:, 0:8], scalar1=1e-12,
                scalar2=None, op0=ALU.max, reads=("l3ssc",), writes=("l3ssc",))
            B.v("reciprocal", out=ssc[:, 0:8], in_=ssc[:, 0:8],
                reads=("l3ssc",), writes=("l3ssc",))
            for g in range(8):
                B.v("tensor_scalar", out=P.sq[1][:, g * 64:(g + 1) * 64],
                    in0=P.ps[2 + nv][:, g * 64:(g + 1) * 64],
                    scalar1=ssc[:, g:g + 1], scalar2=None, op0=ALU.mult,
                    reads=(f"ps{2+nv}", "l3ssc"), writes=("sq1",))
            B.dma(P.vlin_dram[tc * 128:(tc + 1) * 128, nv * 512:(nv + 1) * 512],
                  P.sq[1][:, 0:512], reads=("sq1",), writes=("vlind",))
    wsum = pool(P, "l3ws", [128, 1], F32)
    for h in range(16):
        if True:
            B.dma(P.xn[1][0:16, :], qch[h // 8][h % 8 * 16:h % 8 * 16 + 16, :],
                  reads=(qk_[h // 8],), writes=("xn1",))
            B.dma(P.xn[0][0:16, :], kch[h // 8][h % 8 * 16:h % 8 * 16 + 16, :],
                  reads=(kk_[h // 8],), writes=("xn0",), queue="gpsimd")
            for kc in range(16):
                B.dma(vld[kc % 2], P.vlin_dram[kc * 128:(kc + 1) * 128,
                                                  h * 64:(h + 1) * 64],
                      reads=("vlind",), writes=("mcarry",))
                for it in range(4):
                    B.mm(P.ps[it % 2][:, :], P.xn[0][0:16, kc * 128:(kc + 1) * 128],
                         P.xn[1][0:16, it * 512:(it + 1) * 512],
                         start=True, stop=True,
                         reads=("xn0", "xn1"), writes=(f"ps{it%2}",))
                    B.v("tensor_scalar", out=P.sq[0][:, it * 512:(it + 1) * 512],
                        in0=P.ps[it % 2][:, :], scalar1=0.0, scalar2=None,
                        op0=ALU.max, reads=(f"ps{it%2}",), writes=("sq0",))
                    B.v("tensor_mul", out=pt[:, it * 512:(it + 1) * 512],
                        in0=P.sq[0][:, it * 512:(it + 1) * 512],
                        in1=P.ps[it % 2][:, :],
                        reads=("sq0", f"ps{it%2}"), writes=("bfV",))
                for it in range(4):
                    B.mm(P.ps[4 + it][0:64, :], vld[kc % 2],
                         pt[:, it * 512:(it + 1) * 512],
                         start=(kc == 0), stop=(kc == 15),
                         reads=("mcarry", "bfV"), writes=(f"ps{4+it}",))
            for it in range(4):
                B.s("activation", out=P.sq[0][0:64, it * 512:(it + 1) * 512],
                    in_=P.ps[4 + it][0:64, :], func=AF.Square,
                    reads=(f"ps{4+it}",), writes=("sq0",))
                B.mm(P.ps[3][0:1, :], P.ones_bf[0:64, 0:1],
                     P.sq[0][0:64, it * 512:(it + 1) * 512],
                     start=True, stop=True, reads=("sq0", "ones"), writes=("ps3",))
                B.v("tensor_copy", out=P.row2[0:1, it * 512:(it + 1) * 512],
                    in_=P.ps[3][0:1, :], reads=("ps3",), writes=("tmpf0",))
            B.s("activation", out=P.row2, in_=P.row2, func=AF.Sqrt,
                reads=("tmpf0",), writes=("tmpf0",))
            B.s("activation", out=P.rows[0][:], in_=P.row2, func=AF.Tanh,
                reads=("tmpf0",), writes=("bc1",))
            B.v("tensor_scalar", out=P.row2, in0=P.row2, scalar1=1e-12,
                scalar2=None, op0=ALU.max, reads=("tmpf0",), writes=("tmpf0",))
            B.v("reciprocal", out=P.row2, in_=P.row2, reads=("tmpf0",), writes=("tmpf0",))
            B.v("tensor_mul", out=P.rows[0][:], in0=P.rows[0][:], in1=P.row2,
                reads=("bc1", "tmpf0"), writes=("bc1",))
            P.bcast_row(P.bc32[0][0:64, :], P.rows[0][:], 4, T, 64, "bc1", "bc0")
            B.s("activation", out=gsc,
                in_=gat[h // 2][(h % 2) * 64:(h % 2) * 64 + 64, :], func=AF.Silu,
                reads=(f"bfA{h//2}",), writes=("bfV",))
            B.s("activation", out=gsc, in_=gsc, func=AF.Tanh,
                reads=("bfV",), writes=("bfV",))
            for it in range(4):
                B.v("tensor_mul", out=P.tmpf[0][0:64, 0:512],
                    in0=P.ps[4 + it][0:64, :],
                    in1=P.bc32[0][0:64, it * 512:(it + 1) * 512],
                    reads=(f"ps{4+it}", "bc0"), writes=("tmpf0",))
                B.v("tensor_mul",
                    out=gat[h // 2][(h % 2) * 64:(h % 2) * 64 + 64, it * 512:(it + 1) * 512],
                    in0=P.tmpf[0][0:64, 0:512],
                    in1=gsc[:, it * 512:(it + 1) * 512],
                    reads=("tmpf0", "bfV"), writes=(f"bfA{h//2}",))
    for dc in range(DC):
        B.dma(wst[dc % 2][:].rearrange("p (c m) -> p c m", m=128),
              W["l3a_Wout"].rearrange("(c p) m -> p c m", p=128)[:, :, dc * 128:(dc + 1) * 128],
              writes=(f"wS{dc%2}",), queue="gpsimd")
        for tt in range(4):
            for hc in range(DC):
                B.mm(P.ps[2 + tt % 2][:, :], wst[dc % 2][:, hc * 128:(hc + 1) * 128],
                     gat[hc][:, tt * 512:(tt + 1) * 512],
                     start=(hc == 0), stop=(hc == DC - 1),
                     reads=(f"wS{dc%2}", f"bfA{hc}"), writes=(f"ps{2+tt%2}",))
            B.v("tensor_add", out=P.xs[dc][:, tt * 512:(tt + 1) * 512],
                in0=P.xs[dc][:, tt * 512:(tt + 1) * 512], in1=P.ps[2 + tt % 2][:, :],
                reads=(f"ps{2+tt%2}", f"xs{dc}"), writes=(f"xs{dc}",))



def emit_l3_ffn(P, W):
    B = P.B
    wst = [pool(P, "wS0", [128, 1024], BF16), pool(P, "wS1", [128, 1024], BF16)]
    yt = [pool(P, f"bfA{j}", [128, T], BF16) for j in range(DC)]
    kp2 = pool(P, "mxin", [128, 16 * 260], F32).bitcast(BF16)
    lr = kp2[0:32, 0:T]
    bfb0 = pool(P, "bfB0", [128, T], BF16)
    qi = bfb0[:].bitcast(I32)[:, 0:1024]
    bcolx = pool(P, "l2bcol", [128, 8], F32)
    wdq = P.nc.dram_tensor("wdq_dram", [6, 1024, 1024], BF16, kind="Internal").ap() \
        if not hasattr(P, "_wdq") else P._wdq
    P._wdq = wdq
    done_key = "l3_dequantized"
    if not getattr(P, "_l3_dq_done", False):
        P._l3_dq_done = True
        for mat in range(6):
            for c in range(DC):
                B.dma(qi, W["l3_qT"][mat, c * 128:(c + 1) * 128, :],
                      writes=("bfB0",))
                B.dma(P.bc32[0][:, 0:1024], W["l3_scT"][mat, c * 128:(c + 1) * 128, :],
                      writes=("bc0",))
                B.v("tensor_copy", out=P.tmpf[0][:, 0:1024], in_=qi,
                    reads=("bfB0",), writes=("tmpf0",))
                B.v("scalar_tensor_tensor", out=P.tmpf[0][:, 0:1024],
                    in0=P.tmpf[0][:, 0:1024], scalar=1.0 / 7.5,
                    in1=P.bc32[0][:, 0:1024], op0=ALU.mult, op1=ALU.mult,
                    reads=("tmpf0", "bc0"), writes=("tmpf0",))
                B.v("tensor_sub", out=P.tmpf[0][:, 0:1024],
                    in0=P.tmpf[0][:, 0:1024], in1=P.bc32[0][:, 0:1024],
                    reads=("tmpf0", "bc0"), writes=("tmpf0",))
                B.v("tensor_copy", out=P.sq[0][:, 0:1024], in_=P.tmpf[0][:, 0:1024],
                    reads=("tmpf0",), writes=("sq0",))
                B.dma(wdq[mat, c * 128:(c + 1) * 128, :], P.sq[0][:, 0:1024],
                      reads=("sq0",), writes=(done_key,))
    emit_ln(P)
    for c in range(DC):
        B.dma(P.park_dram[c * 128:(c + 1) * 128, :], P.xs[c][:],
              reads=(f"xs{c}",), writes=(f"park{c}",))
        B.v("tensor_mul", out=P.tmpf[0][:], in0=P.xs[c][:], in1=P.bc32[0][:],
            reads=(f"xs{c}", "bc0"), writes=("tmpf0",))
        B.v("tensor_sub", out=P.xs[c][:], in0=P.tmpf[0][:], in1=P.bc32[1][:],
            reads=("tmpf0", "bc1"), writes=(f"xs{c}",))

    def qlora(mat, src_tiles, src_keys, epilogue):
        B.dma(wst[0][:, 0:256].rearrange("p (c m) -> p c m", m=32),
              W["l3_laT"].rearrange("e (c p) r -> e p c r", p=128)[mat],
              writes=("wS0",), queue="gpsimd")
        for tt in range(4):
            for c in range(DC):
                B.mm(P.ps[6][0:32, :], wst[0][:, c * 32:(c + 1) * 32],
                     src_tiles[c][:, tt * 512:(tt + 1) * 512],
                     start=(c == 0), stop=(c == DC - 1),
                     reads=("wS0", src_keys[c]), writes=("ps6",))
            B.v("tensor_copy", out=lr[0:32, tt * 512:(tt + 1) * 512],
                in_=P.ps[6][0:32, :], reads=("ps6",), writes=("mxin",))
        B.dma(wst[1][0:32, :], W["l3_lbT"][mat, 0:32, :],
              writes=("wS1",), queue="gpsimd")
        for oc in range(DC):
            B.dma(wst[0][:].rearrange("p (c m) -> p c m", m=128),
                  wdq[mat].rearrange("(c p) m -> p c m", p=128)[:, :, oc * 128:(oc + 1) * 128],
                  writes=("wS0",), queue="gpsimd")
            for tt in range(4):
                for c in range(DC):
                    B.mm(P.ps[2 + tt % 2][:, :], wst[0][:, c * 128:(c + 1) * 128],
                         src_tiles[c][:, tt * 512:(tt + 1) * 512],
                         start=(c == 0), stop=False,
                         reads=("wS0", src_keys[c]), writes=(f"ps{2+tt%2}",))
                B.mm(P.ps[2 + tt % 2][:, :], wst[1][0:32, oc * 128:(oc + 1) * 128],
                     lr[0:32, tt * 512:(tt + 1) * 512],
                     start=False, stop=True,
                     reads=("wS1", "mxin"), writes=(f"ps{2+tt%2}",))
                epilogue(oc, tt, P.ps[2 + tt % 2])

    for blk in range(3):
        i0, i1 = 2 * blk, 2 * blk + 1
        for c in range(DC):
            B.v("tensor_copy", out=P.xn[c][:], in_=P.xs[c][:],
                reads=(f"xs{c}",), writes=(f"xn{c}",))
        def ep1(oc, tt, ps, _b=[None]):
            if tt == 0:
                load_col(P, bcolx[:, oc:oc + 1], W["l3_bias"][i0], oc * 128, 128,
                         "l2bcol")
            B.v("tensor_scalar", out=yt[oc][:, tt * 512:(tt + 1) * 512],
                in0=ps[:, :], scalar1=bcolx[:, oc:oc + 1], scalar2=None,
                op0=ALU.add, reads=(f"ps{2+tt%2}", "l2bcol"), writes=(f"bfA{oc}",))
        qlora(i0, P.xn, [f"xn{c}" for c in range(DC)], ep1)
        gb_g = []
        gb_b = []
        gts = [pool(P, f"gbg{c}", [128, 1], F32) for c in range(DC)]
        bts = [pool(P, f"gbb{c}", [128, 1], F32) for c in range(DC)]
        for c in range(DC):
            load_col(P, gts[c], W["l3_lng"][blk], c * 128, 128, f"gbg{c}")
            load_col(P, bts[c], W["l3_lnb"][blk], c * 128, 128, f"gbb{c}")
            gb_g.append((gts[c][:, 0:1], f"gbg{c}"))
            gb_b.append((bts[c][:, 0:1], f"gbb{c}"))
        emit_ln_apply_tiles(P, yt, [f"bfA{j}" for j in range(DC)],
                            gamma_beta=(gb_g, gb_b), act=AF.Gelu)
        def ep2(oc, tt, ps):
            if tt == 0:
                load_col(P, bcolx[:, oc:oc + 1], W["l3_bias"][i1], oc * 128, 128,
                         "l2bcol")
            B.v("scalar_tensor_tensor",
                out=P.xs[oc][:, tt * 512:(tt + 1) * 512], in0=ps[:, :],
                scalar=bcolx[:, oc:oc + 1],
                in1=P.xs[oc][:, tt * 512:(tt + 1) * 512],
                op0=ALU.add, op1=ALU.add,
                reads=(f"ps{2+tt%2}", "l2bcol", f"xs{oc}"), writes=(f"xs{oc}",))
        qlora(i1, P.xn, [f"xn{c}" for c in range(DC)], ep2)
    for c in range(DC):
        B.dma(P.bc32[c % 2][:], P.park_dram[c * 128:(c + 1) * 128, :],
              reads=(f"park{c}",), writes=(f"bc{c%2}",))
        B.v("scalar_tensor_tensor", out=P.xs[c][:], in0=P.xs[c][:], scalar=0.5,
            in1=P.bc32[c % 2][:], op0=ALU.mult, op1=ALU.add,
            reads=(f"xs{c}", f"bc{c%2}"), writes=(f"xs{c}",))



def host_prep(inp):
    w = {}
    tr = lambda a: np.ascontiguousarray(np.asarray(a).T)
    w["l0_fwT"] = tr(inp["L0_ffn_weight"])
    w["l0_WwT"] = tr(inp["L0_attn_W_w"])
    w["l0_Wb"] = np.asarray(inp["L0_attn_W_b"])
    w["l0_owT"] = tr(inp["L0_attn_out_w"])
    w["l0_ob"] = np.asarray(inp["L0_attn_out_b"])
    w["ident"] = np.eye(128, dtype=np.float32)
    w["l1_rwT"] = tr(inp["L1_ffn_router_w"])
    w["l1_rb"] = np.asarray(inp["L1_ffn_router_bias"])
    w["l1_up"] = np.asarray(inp["L1_ffn_up_proj"])
    w["l1_gate"] = np.asarray(inp["L1_ffn_gate_proj"])
    w["l1_down"] = np.asarray(inp["L1_ffn_down_proj"])
    w["l1_shgT"] = tr(inp["L1_ffn_sh_gate"])
    w["l1_shuT"] = tr(inp["L1_ffn_sh_up"])
    w["l1_shdT"] = tr(inp["L1_ffn_sh_down"])
    sel = np.zeros((16, 16 * 128), dtype=np.float32)
    for n in range(16):
        sel[n, n * 128:(n + 1) * 128] = 1.0
    w["seln"] = sel
    for i in range(4):
        w[f"l1m_inT{i}"] = tr(inp["L1_attn_in_proj_w"][i])
        w[f"l1m_cw{i}"] = np.ascontiguousarray(
            np.asarray(inp["L1_attn_conv_w"][i]).reshape(2048, 4))
        w[f"l1m_cb{i}"] = np.asarray(inp["L1_attn_conv_b"][i])
        w[f"l1m_xpT{i}"] = tr(inp["L1_attn_x_proj_w"][i])
        w[f"l1m_dtT{i}"] = tr(inp["L1_attn_dt_proj_w"][i])
        w[f"l1m_dtb{i}"] = np.asarray(inp["L1_attn_dt_proj_b"][i])
        w[f"l1m_A{i}"] = np.asarray(inp["L1_attn_A_log"][i])
        w[f"l1m_D{i}"] = np.asarray(inp["L1_attn_D_skip"][i])
        w[f"l1m_outT{i}"] = tr(inp["L1_attn_out_proj_w"][i])
        w[f"l1m_lnw{i}"] = np.asarray(inp["L1_attn_ln_w"][i])
        w[f"l1m_lnb{i}"] = np.asarray(inp["L1_attn_ln_b"][i])
    w["l2_rw"] = np.asarray(inp["L2_ffn_router_w"])
    w["l2_rb"] = np.asarray(inp["L2_ffn_router_b"])
    w["l2_w1"] = np.asarray(inp["L2_ffn_w1"])
    w["l2_b1"] = np.asarray(inp["L2_ffn_b1"])
    w["l2_w2"] = np.asarray(inp["L2_ffn_w2"])
    w["l2_b2"] = np.asarray(inp["L2_ffn_b2"])
    for nm, key in (("wql", "L2_attn_wq_lin"), ("wkl", "L2_attn_wk_lin"),
                    ("wvl", "L2_attn_wv_lin"), ("wqw", "L2_attn_wq_loc"),
                    ("wkw", "L2_attn_wk_loc"), ("wvw", "L2_attn_wv_loc"),
                    ("wo", "L2_attn_wo"), ("wg1", "L2_attn_wg1"),
                    ("wg2", "L2_attn_wg2")):
        w[f"l2a_{nm}"] = np.asarray(inp[key])
    for nm, key in (("bql", "L2_attn_bq_lin"), ("bkl", "L2_attn_bk_lin"),
                    ("bvl", "L2_attn_bv_lin"), ("bqw", "L2_attn_bq_loc"),
                    ("bkw", "L2_attn_bk_loc"), ("bvw", "L2_attn_bv_loc"),
                    ("bo", "L2_attn_bo"), ("bg1", "L2_attn_bg1"),
                    ("bg2", "L2_attn_bg2"), ("lng", "L2_attn_ln_g"),
                    ("lnb", "L2_attn_ln_b")):
        w[f"l2a_{nm}"] = np.asarray(inp[key])
    rb_ = np.asarray(inp["L2_attn_rel_bias"])
    w["l2a_rbT"] = np.ascontiguousarray(rb_.T)
    bias = np.full((3, 16, 256, 128), NEG, dtype=np.float32)
    jj = np.arange(256)[:, None]
    ii = np.arange(128)[None, :]
    off = jj - 64 - ii
    band = np.abs(off) <= 32
    for v, qc in ((0, 0), (1, 1), (2, 15)):
        kglob = qc * 128 - 64 + jj + 0 * ii
        valid = band & (kglob >= 0) & (kglob < T)
        for h in range(16):
            vals = np.where(valid, rb_[np.clip(off + 64, 0, 128), h], NEG)
            bias[v, h] = vals.astype(np.float32)
    w["l2a_biasT"] = bias
    perm_q = [80 * h + i for h in range(16) for i in range(16)]
    perm_g = [80 * h + 16 + j for h in range(16) for j in range(64)]
    perm = perm_q + perm_g
    w["l3a_WqgP"] = np.ascontiguousarray(np.asarray(inp["L3_attn_Wqg"])[:, perm])
    w["l3a_WkvP"] = np.ascontiguousarray(np.asarray(inp["L3_attn_Wkv"])[:, perm])
    w["l3a_Wout"] = np.asarray(inp["L3_attn_Wout"])
    b16 = np.zeros((128, 8), dtype=np.float32)
    for k_ in range(128):
        b16[k_, k_ // 16] = 1.0
    w["blk16"] = b16
    e16 = np.zeros((8, 128), dtype=np.float32)
    for m_ in range(128):
        e16[m_ // 16, m_] = 1.0
    w["exp16"] = e16
    w["l3_qT"] = np.ascontiguousarray(np.asarray(inp["L3_ffn_qweights"]).transpose(0, 2, 1))
    sc_ = np.asarray(inp["L3_ffn_scales"])
    w["l3_scT"] = np.ascontiguousarray(np.repeat(sc_, 16, axis=2).transpose(0, 2, 1))
    w["l3_bias"] = np.asarray(inp["L3_ffn_biases"])
    w["l3_laT"] = np.ascontiguousarray(np.asarray(inp["L3_ffn_lora_a"]).transpose(0, 2, 1))
    w["l3_lbT"] = np.asarray(inp["L3_ffn_lora_b"]).transpose(0, 2, 1)
    w["l3_lbT"] = np.ascontiguousarray(np.asarray(inp["L3_ffn_lora_b"]).transpose(0, 2, 1))
    w["l3_lng"] = np.asarray(inp["L3_ffn_ln_gamma"])
    w["l3_lnb"] = np.asarray(inp["L3_ffn_ln_beta"])
    return w


L1F_W = ["l1_rwT", "l1_rb", "l1_up", "l1_gate", "l1_down", "l1_shgT",
         "l1_shuT", "l1_shdT", "ident"]
SUBLAYERS = {
    "l0f_a": (emit_l0_ffn, ["l0_fwT"]),
    "l0_attn": (emit_l0_attn, ["l0_WwT", "l0_Wb", "l0_owT", "l0_ob"]),
    "l0f_b": (emit_l0_ffn, ["l0_fwT"]),
    "l1f_a": (emit_l1_ffn, L1F_W),
    "l1f_b": (emit_l1_ffn, L1F_W),
    "l1_mamba": (emit_l1_mamba, ["seln"] + [f"l1m_{k}{i}" for i in range(4)
                 for k in ("inT", "cw", "cb", "xpT", "dtT", "dtb", "A", "D",
                           "outT", "lnw", "lnb")]),
    "l2f_a": (emit_l2_ffn, ["l2_rw", "l2_rb", "l2_w1", "l2_b1", "l2_w2",
                            "l2_b2", "ident"]),
    "l2f_b": (emit_l2_ffn, ["l2_rw", "l2_rb", "l2_w1", "l2_b1", "l2_w2",
                            "l2_b2", "ident"]),
    "l2_attn": (emit_l2_attn, ["l2a_wql", "l2a_wkl", "l2a_wvl", "l2a_wqw",
                "l2a_wkw", "l2a_wvw", "l2a_wo", "l2a_wg1", "l2a_wg2",
                "l2a_bql", "l2a_bkl", "l2a_bvl", "l2a_bqw", "l2a_bkw",
                "l2a_bvw", "l2a_bo", "l2a_bg1", "l2a_bg2", "l2a_rbT",
                "l2a_biasT", "l2a_lng", "l2a_lnb", "ident"]),
    "l3_attn": (emit_l3_attn, ["l3a_WqgP", "l3a_WkvP", "l3a_Wout", "blk16",
                               "exp16"]),
    "l3f_a": (emit_l3_ffn, ["l3_qT", "l3_scT", "l3_bias", "l3_laT", "l3_lbT",
                            "l3_lng", "l3_lnb"]),
    "l3f_b": (emit_l3_ffn, ["l3_qT", "l3_scT", "l3_bias", "l3_laT", "l3_lbT",
                            "l3_lng", "l3_lnb"]),
}

LN_KINDS = {
    "l0f_a": "plain", "l0_attn": "plain", "l0f_b": "plain",
    "l1f_a": "plain", "l1_mamba": "plain", "l1f_b": "plain",
    "l2f_a": "plain", "l2_attn": "plain_post", "l2f_b": "plain",
    "l3f_a": "plain", "l3_attn": "plain", "l3f_b": "plain",
}


def build_group(names, prep):
    nc = bass.Bass("TRN2", target_bir_lowering=False)
    es = ExitStack()
    x_in = nc.dram_tensor("x_in", [D, T], F32, kind="ExternalInput").ap()
    x_out = nc.dram_tensor("x_out", [D, T], F32, kind="ExternalOutput").ap()
    wnames = []
    for nm in names:
        wnames += SUBLAYERS[nm][1]
    wnames = sorted(set(wnames))
    W = {k: nc.dram_tensor(k, list(prep[k].shape),
                           I32 if prep[k].dtype == np.int32 else F32,
                           kind="ExternalInput").ap()
         for k in wnames}
    P = Prog(nc, es)
    P.stage_init()
    B = P.B
    block = es.enter_context(nc.Block())
    B.g("memset", P.ones_bf[:], 1.0, writes=("ones",))
    for c in range(DC):
        B.dma(P.xs[c][:], x_in[c * 128:(c + 1) * 128, :], writes=(f"xs{c}",))
    for nm in names:
        emit_ln(P, park_u=(nm == "l2_attn"))
        SUBLAYERS[nm][0](P, W)
    for c in range(DC):
        B.dma(x_out[c * 128:(c + 1) * 128, :], P.xs[c][:],
              reads=(f"xs{c}",), writes=("x_out",), queue="sync")
    B.finalize(es, block, barrier_writes=("x_out",))
    es.close()
    return nc, wnames


ALL_SUBLAYERS = ["l0f_a", "l0_attn", "l0f_b",
                 "l1f_a", "l1_mamba", "l1f_b",
                 "l2f_a", "l2_attn", "l2f_b",
                 "l3f_a", "l3_attn", "l3f_b"]
GROUPS = [ALL_SUBLAYERS]


def kernel(**inputs):
    prep = host_prep(inputs)
    x = np.asarray(inputs["L0_attn_x"])
    streams = [np.ascontiguousarray(x[b].T) for b in range(NCORES)]
    for names in GROUPS:
        nc, wnames = build_group(names, prep)
        in_maps = []
        for b in range(NCORES):
            m = {"x_in": streams[b]}
            for k in wnames:
                m[k] = prep[k]
            in_maps.append(m)
        res = run_bass_kernel_spmd(nc, in_maps, core_ids=list(range(NCORES)))
        streams = [res.results[b]["x_out"] for b in range(NCORES)]
    out = np.stack([s.T for s in streams]).astype(np.float32)
    return out












# version 04
# speedup vs baseline: 1.0115x; 1.0115x over previous
import numpy as np
from contextlib import ExitStack

import concourse.bass as bass
import concourse.mybir as mybir
from concourse.bass_utils import run_bass_kernel_spmd

F32 = mybir.dt.float32
BF16 = mybir.dt.bfloat16
I32 = mybir.dt.int32
AF = mybir.ActivationFunctionType
ALU = mybir.AluOpType
AX = mybir.AxisListType

T = 2048
D = 1024
DC = D // 128
NCORES = 8
NEG = -1.0e30



NP_DMA = 24


class Node:
    __slots__ = ("engine", "fn", "deps", "idx", "inc_no", "needs_inc", "tag",
                 "slot", "epoch")

    def __init__(self, engine, fn, idx):
        self.tag = ""
        self.slot = None
        self.epoch = None
        self.engine = engine
        self.fn = fn
        self.deps = []
        self.idx = idx
        self.inc_no = None
        self.needs_inc = False


class Builder:

    def __init__(self, nc):
        self.nc = nc
        self.nodes = []
        self.last_writer = {}
        self.readers = {}
        self.global_deps = []

    def emit(self, engine, fn, reads=(), writes=(), is_dma=False):
        key = f"dma:{engine}" if is_dma else engine
        n = Node(key, fn, len(self.nodes))
        for g in self.global_deps:
            if g.engine != key:
                n.deps.append(g)
        def needs_edge(a, b):
            if a.engine != b.engine:
                return True
            return a.engine.startswith("dma:")
        for r in reads:
            w = self.last_writer.get(r)
            if w is not None and needs_edge(w, n):
                n.deps.append(w)
            self.readers.setdefault(r, []).append(n)
        for w_ in writes:
            lw = self.last_writer.get(w_)
            if lw is not None and needs_edge(lw, n):
                n.deps.append(lw)
            for rd in self.readers.get(w_, ()):
                if needs_edge(rd, n) and rd is not n:
                    n.deps.append(rd)
            self.last_writer[w_] = n
            self.readers[w_] = [n] if w_ in [r for r in reads] else []
        self.nodes.append(n)
        return n

    def dma(self, out, in_, reads=(), writes=(), queue="sync"):
        def fn(eng):
            return eng.dma_start(out=out, in_=in_)
        n = self.emit(queue, fn, reads=reads, writes=writes, is_dma=True)
        n.tag = "dma r" + ",".join(reads) + " w" + ",".join(writes)
        return n

    def finalize(self, es, block, barrier_writes=()):
        nc = self.nc
        qcounters = {}
        for n in self.nodes:
            if n.engine.startswith("dma:"):
                n.needs_inc = True
                i = qcounters.get(n.engine, 0)
                qcounters[n.engine] = i + 1
                n.slot = i % NP_DMA
                n.epoch = i // NP_DMA
        for n in self.nodes:
            for d in n.deps:
                d.needs_inc = True
        finals = []
        for r in barrier_writes:
            w = self.last_writer.get(r)
            if w is not None:
                w.needs_inc = True
                finals.append(w)
        counters = {}
        for n in self.nodes:
            if n.needs_inc and not n.engine.startswith("dma:"):
                counters[n.engine] = counters.get(n.engine, 0) + 1
                n.inc_no = counters[n.engine]
        sems = {e: es.enter_context(nc.semaphore(f"sem_{e.replace(':', '_')}"))
                for e in counters}
        dsems = {q: [es.enter_context(nc.semaphore(f"dma_{q.split(':')[1]}_{s}"))
                     for s in range(min(NP_DMA, cnt))]
                 for q, cnt in qcounters.items()}

        def emit_waits(eng, deps, done):
            waits = {}
            for d in deps:
                if d.engine.startswith("dma:"):
                    key = (d.engine, d.slot)
                    waits[key] = max(waits.get(key, 0), 16 * (d.epoch + 1))
                else:
                    waits[d.engine] = max(waits.get(d.engine, 0), d.inc_no)
            for src, v in sorted(waits.items(), key=str):
                if done.get(src, 0) >= v:
                    continue
                done[src] = v
                if isinstance(src, tuple):
                    eng.wait_ge(dsems[src[0]][src[1]], v)
                else:
                    eng.wait_ge(sems[src], v)

        def make_body(hw):
            nodes = [n for n in self.nodes
                     if n.engine == hw or n.engine == f"dma:{hw}"]

            def body(eng):
                done = {}
                for n in nodes:
                    if n.engine.startswith("dma:"):
                        if n.epoch > 0:
                            key = (n.engine, n.slot)
                            v = 16 * n.epoch
                            if done.get(key, 0) < v:
                                done[key] = v
                                eng.wait_ge(dsems[n.engine][n.slot], v)
                    emit_waits(eng, n.deps, done)
                    inst = n.fn(eng)
                    if n.engine.startswith("dma:"):
                        inst.then_inc(dsems[n.engine][n.slot], 16)
                    elif n.needs_inc:
                        inst.then_inc(sems[n.engine], 1)
                if hw == "sync":
                    emit_waits(eng, finals, done)
            return body

        import os
        if os.environ.get("BASS_DBG_SCHED"):
            for n in self.nodes:
                deps = [(d.engine, d.inc_no) for d in n.deps if not d.engine.startswith("dma:gl")]
                print(f"[{n.idx:4d}] {n.engine:10s} inc={n.inc_no} {n.tag[:80]:80s} deps={deps}")
        block.sync(make_body("sync"))
        block.vector(make_body("vector"))
        block.scalar(make_body("scalar"))
        block.gpsimd(make_body("gpsimd"))
        block.tensor(make_body("tensor"))


    def v(self, method, *args, reads=(), writes=(), **kw):
        def fn(eng):
            return getattr(eng, method)(*args, **kw)
        n = self.emit("vector", fn, reads=reads, writes=writes)
        n.tag = method + " r" + ",".join(reads) + " w" + ",".join(writes)
        return n

    def s(self, method, *args, reads=(), writes=(), **kw):
        def fn(eng):
            return getattr(eng, method)(*args, **kw)
        n = self.emit("scalar", fn, reads=reads, writes=writes)
        n.tag = method + " r" + ",".join(reads) + " w" + ",".join(writes)
        return n

    def g(self, method, *args, reads=(), writes=(), **kw):
        def fn(eng):
            return getattr(eng, method)(*args, **kw)
        n = self.emit("gpsimd", fn, reads=reads, writes=writes)
        n.tag = method + " r" + ",".join(reads) + " w" + ",".join(writes)
        return n

    def t(self, method, *args, reads=(), writes=(), **kw):
        def fn(eng):
            return getattr(eng, method)(*args, **kw)
        n = self.emit("tensor", fn, reads=reads, writes=writes)
        n.tag = method + " r" + ",".join(reads) + " w" + ",".join(writes)
        return n

    def mm(self, out, lhsT, rhs, start, stop, reads=(), writes=()):
        def fn(eng):
            return eng.matmul(out, lhsT, rhs, start=start, stop=stop)
        n = self.emit("tensor", fn, reads=reads, writes=writes)
        n.tag = f"mm({int(start)}{int(stop)}) r" + ",".join(reads) + " w" + ",".join(writes)
        return n



class Prog:

    def __init__(self, nc, es):
        self.nc = nc
        self.es = es
        self.B = Builder(nc)
        S = self.sb = lambda name, shape, dt: es.enter_context(
            nc.sbuf_tensor(name, shape, dt))
        self.xs = [S(f"xs{c}", [128, T], F32) for c in range(DC)]
        self.xn = [S(f"xn{c}", [128, T], BF16) for c in range(DC)]
        self.sq = [S(f"sq{i}", [128, T], BF16) for i in range(2)]
        self.tmpf = [S("tmpf0", [128, T], F32)]
        self.rows = None
        self.bc32 = [S(f"bc32_{i}", [128, T], F32) for i in range(2)]
        self.rows = [self.bc32[1][0:1, 0:T], self.bc32[0][0:1, 0:T]]
        self._row2 = None
        self.ones_bf = S("ones_bf", [128, 130], BF16)
        self.ps = [es.enter_context(nc.psum_tensor(f"ps{i}", [128, 512], F32))
                   for i in range(8)]
        self.rows_dram = nc.dram_tensor("rows_dram", [8, T], F32, kind="Internal").ap()
        self.cat_dram = nc.dram_tensor("cat_dram", [D, T], F32, kind="Internal").ap()
        self.park_dram = nc.dram_tensor("park_dram", [D, T], F32, kind="Internal").ap()
        self.misc_dram = nc.dram_tensor("misc_dram", [64, T], F32, kind="Internal").ap()
        self.z_dram = nc.dram_tensor("z_dram", [2048, 256], BF16, kind="Internal").ap()
        self.vlin_dram = nc.dram_tensor("vlin_dram", [T, 1024], BF16, kind="Internal").ap()
        self.lin_dram = nc.dram_tensor("lin_dram", [D, T], BF16, kind="Internal").ap()
        self.win_dram = nc.dram_tensor("win_dram", [D, T], BF16, kind="Internal").ap()
        self._wt_cache = {}
        self.cpool = self.sb("cpool", [128, 16], F32)
        self._cvals = {}

    @property
    def row2(self):
        return self.tmpf[0][0:1, 0:T]

    def const_val(self, v):
        v = float(v)
        if v in self._cvals:
            return
        col = len(self._cvals)
        assert col < 16
        self._cvals[v] = col
        ap = self.cpool[:, col:col + 1]
        n = self.B.g("memset", ap, v, writes=("cpool",))
        self.B.global_deps.append(n)
        self.nc.const_aps.register(F32, v, ap)

    def stage_init(self):
        pass

    def load_bf16(self, dst_ap, src_ap, dst_key, cast_engine=None):
        self.B.dma(dst_ap, src_ap, writes=(dst_key,), queue="gpsimd")

    def bcast_row(self, dst_tile_ap, src_row_ap, slot, n, rows, src_key, dst_key):
        B = self.B
        dr = self.rows_dram[slot:slot + 1, 0:n]
        B.dma(dr, src_row_ap, reads=(src_key,), writes=(f"rows_dram{slot}",))
        brd = dr.partition_broadcast(rows).squeeze(1)
        B.dma(dst_tile_ap, brd, reads=(f"rows_dram{slot}",), writes=(dst_key,))



def emit_ln(P, gamma_beta=None, eps=1e-5, stats_only=False, park_u=False):
    B = P.B
    TT4 = T // 512
    for c in range(DC):
        B.v("tensor_copy", out=P.xn[c][:], in_=P.xs[c][:],
            reads=(f"xs{c}",), writes=(f"xn{c}",))
    for tt in range(TT4):
        for c in range(DC):
            B.mm(P.ps[4 + tt][0:1, :], P.ones_bf[:, 0:1],
                 P.xn[c][:, tt * 512:(tt + 1) * 512],
                 start=(c == 0), stop=(c == DC - 1),
                 reads=(f"xn{c}", "ones"), writes=(f"ps{4+tt}",))
        B.v("tensor_copy", out=P.rows[0][0:1, tt * 512:(tt + 1) * 512],
            in_=P.ps[4 + tt][0:1, :], reads=(f"ps{4+tt}",), writes=("bc1",))
    for c in range(DC):
        B.s("activation", out=P.sq[c % 2][:], in_=P.xn[c][:], func=AF.Square,
            reads=(f"xn{c}",), writes=(f"sq{c%2}",))
        for tt in range(TT4):
            B.mm(P.ps[4 + tt][0:1, :], P.ones_bf[:, 0:1],
                 P.sq[c % 2][:, tt * 512:(tt + 1) * 512],
                 start=(c == 0), stop=(c == DC - 1),
                 reads=(f"sq{c%2}", "ones"), writes=(f"ps{4+tt}",))
    for tt in range(TT4):
        B.v("tensor_copy", out=P.rows[1][0:1, tt * 512:(tt + 1) * 512],
            in_=P.ps[4 + tt][0:1, :], reads=(f"ps{4+tt}",), writes=("bc0",))
    if stats_only:
        return
    B.v("scalar_tensor_tensor", out=P.row2, in0=P.rows[0][:],
        scalar=1.0 / (D * D), in1=P.rows[0][:], op0=ALU.mult, op1=ALU.mult,
        reads=("bc1",), writes=("tmpf0",))
    B.v("scalar_tensor_tensor", out=P.rows[1][:], in0=P.rows[1][:],
        scalar=1.0 / D, in1=P.row2, op0=ALU.mult, op1=ALU.subtract,
        reads=("bc0", "tmpf0"), writes=("bc0",))
    P.const_val(eps)
    B.s("activation", out=P.rows[1][:], in_=P.rows[1][:], func=AF.Ln, bias=eps,
        reads=("bc0",), writes=("bc0",))
    B.s("activation", out=P.rows[1][:], in_=P.rows[1][:], func=AF.Exp, scale=-0.5,
        reads=("bc0",), writes=("bc0",))
    B.v("scalar_tensor_tensor", out=P.rows[0][:], in0=P.rows[0][:],
        scalar=1.0 / D, in1=P.rows[1][:], op0=ALU.mult, op1=ALU.mult,
        reads=("bc1", "bc0"), writes=("bc1",))
    P.bcast_row(P.bc32[0][:], P.rows[1][:], 0, T, 128, "bc0", "bc0")
    P.bcast_row(P.bc32[1][:], P.rows[0][:], 1, T, 128, "bc1", "bc1")
    for c in range(DC):
        B.v("tensor_mul", out=P.tmpf[0][:], in0=P.xs[c][:], in1=P.bc32[0][:],
            reads=(f"xs{c}", "bc0"), writes=("tmpf0",))
        if gamma_beta is None:
            if park_u:
                B.v("tensor_sub", out=P.tmpf[0][:], in0=P.tmpf[0][:],
                    in1=P.bc32[1][:], reads=("tmpf0", "bc1"), writes=("tmpf0",))
                B.dma(P.park_dram[c * 128:(c + 1) * 128, :], P.tmpf[0][:],
                      reads=("tmpf0",), writes=(f"park{c}",))
                B.v("tensor_copy", out=P.xn[c][:], in_=P.tmpf[0][:],
                    reads=("tmpf0",), writes=(f"xn{c}",))
            else:
                B.v("tensor_sub", out=P.xn[c][:], in0=P.tmpf[0][:], in1=P.bc32[1][:],
                    reads=("tmpf0", "bc1"), writes=(f"xn{c}",))
        else:
            (g_ap, g_key), (b_ap, b_key) = gamma_beta[0][c], gamma_beta[1][c]
            B.v("tensor_sub", out=P.tmpf[0][:], in0=P.tmpf[0][:],
                in1=P.bc32[1][:],
                reads=("tmpf0", "bc1"), writes=("tmpf0",))
            B.v("tensor_scalar", out=P.xn[c][:], in0=P.tmpf[0][:],
                scalar1=g_ap, scalar2=b_ap, op0=ALU.mult, op1=ALU.add,
                reads=("tmpf0", g_key, b_key), writes=(f"xn{c}",))


def emit_ln2(P, gamma_beta, eps=1e-5):
    B = P.B
    TT4 = T // 512
    for tt in range(TT4):
        for c in range(DC):
            B.mm(P.ps[4 + tt][0:1, :], P.ones_bf[:, 0:1],
                 P.xn[c][:, tt * 512:(tt + 1) * 512],
                 start=(c == 0), stop=(c == DC - 1),
                 reads=(f"xn{c}", "ones"), writes=(f"ps{4+tt}",))
        B.v("tensor_copy", out=P.rows[0][0:1, tt * 512:(tt + 1) * 512],
            in_=P.ps[4 + tt][0:1, :], reads=(f"ps{4+tt}",), writes=("bc1",))
    for c in range(DC):
        B.s("activation", out=P.sq[c % 2][:], in_=P.xn[c][:], func=AF.Square,
            reads=(f"xn{c}",), writes=(f"sq{c%2}",))
        for tt in range(TT4):
            B.mm(P.ps[4 + tt][0:1, :], P.ones_bf[:, 0:1],
                 P.sq[c % 2][:, tt * 512:(tt + 1) * 512],
                 start=(c == 0), stop=(c == DC - 1),
                 reads=(f"sq{c%2}", "ones"), writes=(f"ps{4+tt}",))
    for tt in range(TT4):
        B.v("tensor_copy", out=P.rows[1][0:1, tt * 512:(tt + 1) * 512],
            in_=P.ps[4 + tt][0:1, :], reads=(f"ps{4+tt}",), writes=("bc0",))
    B.v("scalar_tensor_tensor", out=P.row2, in0=P.rows[0][:],
        scalar=1.0 / (D * D), in1=P.rows[0][:], op0=ALU.mult, op1=ALU.mult,
        reads=("bc1",), writes=("tmpf0",))
    B.v("scalar_tensor_tensor", out=P.rows[1][:], in0=P.rows[1][:],
        scalar=1.0 / D, in1=P.row2, op0=ALU.mult, op1=ALU.subtract,
        reads=("bc0", "tmpf0"), writes=("bc0",))
    P.const_val(eps)
    B.s("activation", out=P.rows[1][:], in_=P.rows[1][:], func=AF.Ln, bias=eps,
        reads=("bc0",), writes=("bc0",))
    B.s("activation", out=P.rows[1][:], in_=P.rows[1][:], func=AF.Exp, scale=-0.5,
        reads=("bc0",), writes=("bc0",))
    B.v("scalar_tensor_tensor", out=P.rows[0][:], in0=P.rows[0][:],
        scalar=1.0 / D, in1=P.rows[1][:], op0=ALU.mult, op1=ALU.mult,
        reads=("bc1", "bc0"), writes=("bc1",))
    P.bcast_row(P.bc32[0][:], P.rows[1][:], 0, T, 128, "bc0", "bc0")
    P.bcast_row(P.bc32[1][:], P.rows[0][:], 1, T, 128, "bc1", "bc1")
    for c in range(DC):
        (g_ap, g_key), (b_ap, b_key) = gamma_beta[0][c], gamma_beta[1][c]
        B.v("tensor_mul", out=P.tmpf[0][:], in0=P.xn[c][:], in1=P.bc32[0][:],
            reads=(f"xn{c}", "bc0"), writes=("tmpf0",))
        B.v("tensor_sub", out=P.tmpf[0][:], in0=P.tmpf[0][:], in1=P.bc32[1][:],
            reads=("tmpf0", "bc1"), writes=("tmpf0",))
        B.v("tensor_scalar", out=P.xn[c][:], in0=P.tmpf[0][:],
            scalar1=g_ap, scalar2=b_ap, op0=ALU.mult, op1=ALU.add,
            reads=("tmpf0", g_key, b_key), writes=(f"xn{c}",))



def pool(P, name, shape, dt):
    if name == "ones":
        return P.ones_bf
    k = (name, tuple(shape), dt)
    if k not in P._wt_cache:
        P._wt_cache[k] = P.sb(name, list(shape), dt)
    return P._wt_cache[k]


def load_wT_blocks(P, dram_T, col0, ncols, dst, key, nk=DC, cast_engine="vector"):
    for c in range(nk):
        P.load_bf16(dst[:, c * ncols:(c + 1) * ncols],
                    dram_T[c * 128:(c + 1) * 128, col0:col0 + ncols],
                    key, cast_engine=cast_engine)


def load_col(P, dst, dram_vec, off, n, key):
    P.B.dma(dst[0:n, 0:1], dram_vec[off:off + n].rearrange("(p o) -> p o", o=1),
            writes=(key,))



def emit_l0_ffn(P, W):
    B = P.B
    bfa = [pool(P, f"bfA{j}", [128, T], BF16) for j in range(4)]
    swT = [bfa[c // 2][:, (c % 2) * 1024:(c % 2) * 1024 + 1024] for c in range(DC)]
    swk = [f"bfA{c // 2}" for c in range(DC)]
    for c in range(DC):
        B.dma(P.tmpf[0][:, 0:D], W["l0_fwT"][c * 128:(c + 1) * 128, :],
              writes=("tmpf0",))
        B.s("sign", out=swT[c], in_=P.tmpf[0][:, 0:D],
            reads=("tmpf0",), writes=(swk[c],))
    for oc in range(DC):
        for c in range(DC):
            for tt in range(4):
                B.mm(P.ps[tt][:, :], swT[c][:, oc * 128:(oc + 1) * 128],
                     P.xn[c][:, tt * 512:(tt + 1) * 512],
                     start=(c == 0), stop=(c == DC - 1),
                     reads=(swk[c], f"xn{c}"), writes=(f"ps{tt}",))
        for tt in range(4):
            B.v("scalar_tensor_tensor",
                out=P.xs[oc][:, tt * 512:(tt + 1) * 512], in0=P.ps[tt][:, :],
                scalar=0.5, in1=P.xs[oc][:, tt * 512:(tt + 1) * 512],
                op0=ALU.mult, op1=ALU.add,
                reads=(f"ps{tt}", f"xs{oc}"), writes=(f"xs{oc}",))



def emit_l0_attn(P, W):
    B = P.B
    qp = pool(P, "bfC0", [128, T], BF16)
    kp = pool(P, "bfC1", [128, T], BF16)
    vtm = pool(P, "bfV", [128, 2210], BF16)
    pt = [pool(P, f"bfB{i}", [128, T], BF16) for i in range(2)]
    cat = [pool(P, f"bfA{h}", [128, T], BF16) for h in range(DC)]
    wq = pool(P, "wS0", [128, 1024], BF16)
    wk = pool(P, "wS1", [128, 1024], BF16)
    wv = wq
    bq = pool(P, "l0bq", [128, 1], F32)
    bk = pool(P, "l0bk", [128, 1], F32)
    bvr = pool(P, "l0bvr", [1, 128], BF16)
    ob = [pool(P, f"l0ob{d}", [128, 1], F32) for d in range(DC)]
    mh = [pool(P, f"l0mh{i}", [128, 1], F32) for i in range(2)]
    onesr = pool(P, "ones", [128, 130], BF16)
    B.g("memset", onesr[:], 1.0, writes=("ones",))
    v3 = vtm[:].rearrange("p (k c) -> p k c", c=65)
    B.g("memset", v3[:, :, 64:65], 1.0, writes=("vtm_ones",))

    WwT, Wb = W["l0_WwT"], W["l0_Wb"]
    for hp in range(8):
        load_wT_blocks(P, WwT, hp * 128, 128, wq, "wS0")
        load_wT_blocks(P, WwT, 1024 + hp * 128, 128, wk, "wS1")
        load_col(P, bq, Wb, hp * 128, 128, "l0bq")
        load_col(P, bk, Wb, 1024 + hp * 128, 128, "l0bk")
        B.dma(bvr[:],
              Wb[2048 + hp * 128:2048 + (hp + 1) * 128].rearrange("(o n) -> o n", o=1),
              writes=("l0bvr",), queue="gpsimd")
        for dst, wt, wkey, bias, bkey, pskey in ((qp, wq, "wS0", bq, "l0bq", 0),
                                                 (kp, wk, "wS1", bk, "l0bk", 1)):
            for tt in range(4):
                for c in range(DC):
                    B.mm(P.ps[pskey][:, :], wt[:, c * 128:(c + 1) * 128],
                         P.xn[c][:, tt * 512:(tt + 1) * 512],
                         start=(c == 0), stop=(c == DC - 1),
                         reads=(wkey, f"xn{c}"), writes=(f"ps{pskey}",))
                B.v("tensor_scalar", out=dst[:, tt * 512:(tt + 1) * 512],
                    in0=P.ps[pskey][:, :], scalar1=bias[:, 0:1], scalar2=None,
                    op0=ALU.add,
                    reads=(f"ps{pskey}", bkey), writes=("bfC0" if dst is qp else "bfC1",))
        load_wT_blocks(P, WwT, 2048 + hp * 128, 128, wv, "wS0")
        for tc in range(16):
            for c in range(DC):
                B.mm(P.ps[2][:, 0:128], P.xn[c][:, tc * 128:(tc + 1) * 128],
                     wv[:, c * 128:(c + 1) * 128],
                     start=(c == 0), stop=False,
                     reads=("wS0", f"xn{c}"), writes=("ps2",))
            B.mm(P.ps[2][:, 0:128], onesr[0:1, 0:128], bvr[0:1, :],
                 start=False, stop=True,
                 reads=("ones", "l0bvr"), writes=("ps2",))
            B.v("tensor_copy", out=vtm[:, tc * 130:tc * 130 + 64],
                in_=P.ps[2][:, 0:64], reads=("ps2",), writes=("bfV",))
            B.v("tensor_copy", out=vtm[:, tc * 130 + 65:tc * 130 + 129],
                in_=P.ps[2][:, 64:128], reads=("ps2",), writes=("bfV",))
        B.s("activation", out=P.sq[0][:], in_=qp[:], func=AF.Square,
            reads=("bfC0",), writes=("sq0",))
        B.s("activation", out=P.sq[1][:], in_=kp[:], func=AF.Square,
            reads=("bfC1",), writes=("sq1",))
        for hh in range(2):
            base = 64 * hh
            for src_i, (sqt, sqk) in enumerate(((P.sq[0], "sq0"), (P.sq[1], "sq1"))):
                for tt in range(4):
                    B.mm(P.ps[3][0:1, :], P.ones_bf[base:base + 64, 0:1],
                         sqt[base:base + 64, tt * 512:(tt + 1) * 512],
                         start=True, stop=True,
                         reads=(sqk, "ones"), writes=("ps3",))
                    B.v("tensor_copy", out=P.row2[0:1, tt * 512:(tt + 1) * 512],
                        in_=P.ps[3][0:1, :], reads=("ps3",), writes=("tmpf0",))
                B.v("reduce_max", out=P.rows[src_i][0:1, src_i:src_i + 1],
                    in_=P.row2, axis=AX.X,
                    reads=("tmpf0",), writes=(('bc1', 'bc0')[src_i],))
            B.s("activation", out=P.rows[0][0:1, 0:1], in_=P.rows[0][0:1, 0:1],
                func=AF.Sqrt, reads=("bc1",), writes=("bc1",))
            B.s("activation", out=P.rows[1][0:1, 1:2], in_=P.rows[1][0:1, 1:2],
                func=AF.Sqrt, reads=("bc0",), writes=("bc0",))
            B.v("tensor_mul", out=P.rows[0][0:1, 2:3], in0=P.rows[0][0:1, 0:1],
                in1=P.rows[1][0:1, 1:2], reads=("bc1", "bc0"), writes=("bc1",))
            B.v("tensor_scalar", out=P.rows[0][0:1, 3:4], in0=P.rows[0][0:1, 2:3],
                scalar1=-0.125, scalar2=None, op0=ALU.mult,
                reads=("bc1",), writes=("bc1",))
            P.bcast_row(mh[hh][:, 0:1], P.rows[0][0:1, 3:4], 2 + hh, 1, 128,
                        "bc1", f"l0mh{hh}")
        for hh in range(2):
            base = 64 * hh
            for kc in range(16):
                for qt in range(4):
                    B.mm(P.ps[qt][:, :], kp[base:base + 64, kc * 128:(kc + 1) * 128],
                         qp[base:base + 64, qt * 512:(qt + 1) * 512],
                         start=True, stop=True,
                         reads=("bfC1", "bfC0"), writes=(f"ps{qt}",))
                for qt in range(4):
                    B.s("activation", out=pt[kc % 2][:, qt * 512:(qt + 1) * 512],
                        in_=P.ps[qt][:, :], func=AF.Exp, scale=0.125,
                        bias=mh[hh][:, 0:1],
                        reads=(f"ps{qt}", f"l0mh{hh}"), writes=(f"bfB{kc%2}",))
                for qt in range(4):
                    B.mm(P.ps[4 + qt][0:65, :],
                         vtm[:, kc * 130 + 65 * hh:kc * 130 + 65 * hh + 65],
                         pt[kc % 2][:, qt * 512:(qt + 1) * 512],
                         start=(kc == 0), stop=(kc == 15),
                         reads=("bfV", "vtm_ones", f"bfB{kc%2}"),
                         writes=(f"ps{4+qt}",))
            for qt in range(4):
                B.v("reciprocal", out=P.row2[0:1, qt * 512:(qt + 1) * 512],
                    in_=P.ps[4 + qt][64:65, :], reads=(f"ps{4+qt}",), writes=("tmpf0",))
            P.bcast_row(P.bc32[0][0:64, :], P.row2, 4, T, 64,
                        "tmpf0", "bc0")
            h = 2 * hp + hh
            for qt in range(4):
                B.v("tensor_mul", out=cat[h // 2][64 * (h % 2):64 * (h % 2) + 64,
                                                  qt * 512:(qt + 1) * 512],
                    in0=P.ps[4 + qt][0:64, :], in1=P.bc32[0][0:64, qt * 512:(qt + 1) * 512],
                    reads=(f"ps{4+qt}", "bc0"), writes=(f"bfA{h//2}",))
    wo_tiles = [qp, kp, pt[0], pt[1]]
    wo_keys = ["bfC0", "bfC1", "pt0", "pt1"]
    wo = [wo_tiles[hc // 2][:, (hc % 2) * 1024:(hc % 2) * 1024 + 1024]
          for hc in range(DC)]
    wokey = [wo_keys[hc // 2] for hc in range(DC)]
    for hc in range(DC):
        P.load_bf16(wo[hc], W["l0_owT"][hc * 128:(hc + 1) * 128, :], wokey[hc])
    for d in range(DC):
        load_col(P, ob[d], W["l0_ob"], d * 128, 128, f"l0ob{d}")
    for dc in range(DC):
        for tt in range(4):
            for hc in range(DC):
                B.mm(P.ps[tt % 4][:, :], wo[hc][:, dc * 128:(dc + 1) * 128],
                     cat[hc][:, tt * 512:(tt + 1) * 512],
                     start=(hc == 0), stop=(hc == DC - 1),
                     reads=(wokey[hc], f"bfA{hc}"), writes=(f"ps{tt%4}",))
            B.v("scalar_tensor_tensor",
                out=P.xs[dc][:, tt * 512:(tt + 1) * 512], in0=P.ps[tt % 4][:, :],
                scalar=ob[dc][:, 0:1], in1=P.xs[dc][:, tt * 512:(tt + 1) * 512],
                op0=ALU.add, op1=ALU.add,
                reads=(f"ps{tt%4}", f"l0ob{dc}", f"xs{dc}"), writes=(f"xs{dc}",))



def emit_l1_ffn(P, W):
    B = P.B
    wrw = pool(P, "wS0", [128, 1024], BF16)
    ident = pool(P, "identsb", [128, 128], F32)
    load_wT_blocks(P, W["l1_rwT"], 0, 16, wrw, "wS0")
    rb = pool(P, "l1rb", [16, 1], F32)
    load_col(P, rb, W["l1_rb"], 0, 16, "l1rb")
    B.dma(ident[:], W["ident"][:, :], writes=("ident",))
    s_fm = P.bc32[0][0:16, :]
    for tt in range(4):
        for c in range(DC):
            B.mm(P.ps[0][0:16, :], wrw[:, c * 16:(c + 1) * 16],
                 P.xn[c][:, tt * 512:(tt + 1) * 512],
                 start=(c == 0), stop=(c == DC - 1),
                 reads=("wS0", f"xn{c}"), writes=("ps0",))
        B.v("tensor_scalar", out=s_fm[:, tt * 512:(tt + 1) * 512],
            in0=P.ps[0][0:16, :], scalar1=rb[0:16, 0:1], scalar2=None, op0=ALU.add,
            reads=("ps0", "l1rb"), writes=("bc0",))
    s_tm = P.tmpf[0][:, 0:256]
    for tc in range(16):
        B.t("transpose", P.ps[1][:, tc % 2 * 16:tc % 2 * 16 + 16],
            s_fm[0:16, tc * 128:(tc + 1) * 128], ident[0:16, 0:16],
            reads=("bc0", "ident"), writes=("ps1",))
        B.v("tensor_copy", out=s_tm[:, tc * 16:(tc + 1) * 16],
            in_=P.ps[1][:, tc % 2 * 16:tc % 2 * 16 + 16],
            reads=("ps1",), writes=("tmpf0",))
    t0 = P.tmpf[0]
    s3 = s_tm.rearrange("p (c e) -> p c e", e=16)
    m1 = t0[:, 256:272]
    eq1 = t0[:, 288:544]
    msk = t0[:, 544:800]
    m2 = t0[:, 800:816]
    w1 = t0[:, 816:832]
    w2 = t0[:, 832:848]
    eq2 = t0[:, 848:1104]
    ctm = t0[:, 1104:1360]
    m1_3 = m1.rearrange("p (c e) -> p c e", e=1).broadcast_to((128, 16, 16))
    m2_3 = m2.rearrange("p (c e) -> p c e", e=1).broadcast_to((128, 16, 16))
    w1_3 = w1.rearrange("p (c e) -> p c e", e=1).broadcast_to((128, 16, 16))
    w2_3 = w2.rearrange("p (c e) -> p c e", e=1).broadcast_to((128, 16, 16))
    def T0(method, **kw):
        B.v(method, reads=("tmpf0",), writes=("tmpf0",), **kw)
    T0("tensor_reduce", out=m1, in_=s3, axis=AX.X, op=ALU.max)
    T0("tensor_tensor", out=eq1.rearrange("p (c e) -> p c e", e=16), in0=s3,
       in1=m1_3, op=ALU.is_equal)
    T0("scalar_tensor_tensor", out=msk, in0=eq1, scalar=NEG, in1=s_tm,
       op0=ALU.mult, op1=ALU.add)
    T0("tensor_reduce", out=m2, in_=msk.rearrange("p (c e) -> p c e", e=16),
       axis=AX.X, op=ALU.max)
    T0("tensor_sub", out=w1, in0=m1, in1=m2)
    B.s("activation", out=w1, in_=w1, func=AF.Sigmoid,
        reads=("tmpf0",), writes=("tmpf0",))
    T0("tensor_scalar", out=w2, in0=w1, scalar1=-1.0, scalar2=1.0,
       op0=ALU.mult, op1=ALU.add)
    T0("tensor_tensor", out=eq2.rearrange("p (c e) -> p c e", e=16), in0=s3,
       in1=m2_3, op=ALU.is_equal)
    T0("tensor_tensor", out=eq1.rearrange("p (c e) -> p c e", e=16),
       in0=eq1.rearrange("p (c e) -> p c e", e=16), in1=w1_3, op=ALU.mult)
    T0("tensor_tensor", out=eq2.rearrange("p (c e) -> p c e", e=16),
       in0=eq2.rearrange("p (c e) -> p c e", e=16), in1=w2_3, op=ALU.mult)
    T0("tensor_add", out=ctm, in0=eq1, in1=eq2)
    c_fm = P.bc32[1][0:16, :]
    for tc in range(16):
        B.t("transpose", P.ps[1][0:16, tc % 2 * 128:tc % 2 * 128 + 128],
            ctm[:, tc * 16:(tc + 1) * 16], ident[:],
            reads=("tmpf0", "ident"), writes=("ps1",))
        B.v("tensor_copy", out=c_fm[0:16, tc * 128:(tc + 1) * 128],
            in_=P.ps[1][0:16, tc % 2 * 128:tc % 2 * 128 + 128],
            reads=("ps1",), writes=("bc1",))
    B.dma(P.misc_dram[0:16, :], c_fm, reads=("bc1",), writes=("cdram",))
    wg = pool(P, "wS1", [128, 1024], BF16)
    bfb0 = pool(P, "bfB0", [128, T], BF16)
    wu = bfb0[:, 0:1024]
    hsh = [pool(P, f"bfA{j}", [128, T], BF16) for j in range(DC)]
    sgl = pool(P, "bfC0", [128, T], BF16)
    for j in range(DC):
        load_wT_blocks(P, W["l1_shgT"], j * 128, 128, wg, "wS1")
        load_wT_blocks(P, W["l1_shuT"], j * 128, 128, wu, "bfB0")
        for tt in range(4):
            for c in range(DC):
                B.mm(P.ps[0][:, :], wg[:, c * 128:(c + 1) * 128],
                     P.xn[c][:, tt * 512:(tt + 1) * 512],
                     start=(c == 0), stop=(c == DC - 1),
                     reads=("wS1", f"xn{c}"), writes=("ps0",))
                B.mm(P.ps[1][:, :], wu[:, c * 128:(c + 1) * 128],
                     P.xn[c][:, tt * 512:(tt + 1) * 512],
                     start=(c == 0), stop=(c == DC - 1),
                     reads=("bfB0", f"xn{c}"), writes=("ps1",))
            B.s("activation", out=sgl[:, tt * 512:(tt + 1) * 512], in_=P.ps[0][:, :],
                func=AF.Silu, reads=("ps0",), writes=("bfC0",))
            B.v("tensor_mul", out=hsh[j][:, tt * 512:(tt + 1) * 512],
                in0=sgl[:, tt * 512:(tt + 1) * 512], in1=P.ps[1][:, :],
                reads=("bfC0", "ps1"), writes=(f"bfA{j}",))
    wd = bfb0[:, 1024:2048]
    for dc in range(DC):
        load_wT_blocks(P, W["l1_shdT"], dc * 128, 128, wd, "bfB0")
        for tt in range(4):
            for j in range(DC):
                B.mm(P.ps[2 + tt % 2][:, :], wd[:, j * 128:(j + 1) * 128],
                     hsh[j][:, tt * 512:(tt + 1) * 512],
                     start=(j == 0), stop=(j == DC - 1),
                     reads=("bfB0", f"bfA{j}"), writes=(f"ps{2+tt%2}",))
            B.v("scalar_tensor_tensor",
                out=P.xs[dc][:, tt * 512:(tt + 1) * 512],
                in0=P.ps[2 + tt % 2][:, :], scalar=0.5,
                in1=P.xs[dc][:, tt * 512:(tt + 1) * 512],
                op0=ALU.mult, op1=ALU.add,
                reads=(f"ps{2+tt%2}", f"xs{dc}"), writes=(f"xs{dc}",))
    wup = [pool(P, "bfA0", [128, T], BF16), pool(P, "bfA1", [128, T], BF16)]
    wgt = [pool(P, "bfA2", [128, T], BF16), pool(P, "bfA3", [128, T], BF16)]
    wdn = [pool(P, "bfA4", [128, T], BF16), pool(P, "bfA5", [128, T], BF16)]
    he = [pool(P, "bfA6", [128, T], BF16), pool(P, "bfA7", [128, T], BF16)]
    ceb = [P.bc32[0], P.bc32[1]]
    cebk = ["bc0", "bc1"]
    for e in range(16):
        pe = e % 2
        B.dma(ceb[pe][:],
              P.misc_dram[e:e + 1, :].partition_broadcast(128).squeeze(1),
              reads=("cdram",), writes=(cebk[pe],))
        for half in range(2):
            for cc in range(4):
                c = half * 4 + cc
                P.load_bf16(wup[half][:, cc * 512:(cc + 1) * 512],
                            W["l1_up"][e, c * 128:(c + 1) * 128, :], f"bfA{half}")
                P.load_bf16(wgt[half][:, cc * 512:(cc + 1) * 512],
                            W["l1_gate"][e, c * 128:(c + 1) * 128, :], f"bfA{2+half}")
            for jc in range(2):
                P.load_bf16(wdn[half][:, jc * 1024:(jc + 1) * 1024],
                            W["l1_down"][e, (half * 2 + jc) * 128:(half * 2 + jc + 1) * 128, :],
                            f"bfA{4+half}")
        for jc in range(4):
            half, cc0 = jc // 2, 0
            for tt in range(4):
                for c in range(DC):
                    wt = wup[c // 4][:, (c % 4) * 512 + jc * 128:(c % 4) * 512 + jc * 128 + 128]
                    B.mm(P.ps[0][:, :], wt, P.xn[c][:, tt * 512:(tt + 1) * 512],
                         start=(c == 0), stop=(c == DC - 1),
                         reads=(f"bfA{c//4}", f"xn{c}"), writes=("ps0",))
                    wt2 = wgt[c // 4][:, (c % 4) * 512 + jc * 128:(c % 4) * 512 + jc * 128 + 128]
                    B.mm(P.ps[1][:, :], wt2, P.xn[c][:, tt * 512:(tt + 1) * 512],
                         start=(c == 0), stop=(c == DC - 1),
                         reads=(f"bfA{2+c//4}", f"xn{c}"), writes=("ps1",))
                B.s("activation", out=sgl[:, tt * 512:(tt + 1) * 512],
                    in_=P.ps[1][:, :], func=AF.Silu, reads=("ps1",), writes=("bfC0",))
                B.v("tensor_mul", out=sgl[:, tt * 512:(tt + 1) * 512],
                    in0=sgl[:, tt * 512:(tt + 1) * 512], in1=P.ps[0][:, :],
                    reads=("bfC0", "ps0"), writes=("bfC0",))
                B.v("tensor_mul", out=he[jc % 2][:, tt * 512:(tt + 1) * 512],
                    in0=sgl[:, tt * 512:(tt + 1) * 512],
                    in1=ceb[pe][:, tt * 512:(tt + 1) * 512],
                    reads=("bfC0", cebk[pe]), writes=(f"bfA{6+jc%2}",))
            if jc % 2 == 1:
                for dc in range(DC):
                    for tt in range(4):
                        for jj in range(2):
                            jcc = jc - 1 + jj
                            wt = wdn[jcc // 2][:, (jcc % 2) * 1024 + dc * 128:(jcc % 2) * 1024 + dc * 128 + 128]
                            B.mm(P.ps[2 + tt % 2][:, :], wt,
                                 he[jj][:, tt * 512:(tt + 1) * 512],
                                 start=(jcc == jc - 1), stop=(jcc == jc),
                                 reads=(f"bfA{4+jcc//2}", f"bfA{6+jj}"),
                                 writes=(f"ps{2+tt%2}",))
                        B.v("scalar_tensor_tensor",
                            out=P.xs[dc][:, tt * 512:(tt + 1) * 512],
                            in0=P.ps[2 + tt % 2][:, :], scalar=0.5,
                            in1=P.xs[dc][:, tt * 512:(tt + 1) * 512],
                            op0=ALU.mult, op1=ALU.add,
                            reads=(f"ps{2+tt%2}", f"xs{dc}"), writes=(f"xs{dc}",))



SC = 256
NSC = T // SC


def emit_l1_mamba(P, W):
    B = P.B
    emit_ln(P)
    for c in range(DC):
        B.dma(P.park_dram[c * 128:(c + 1) * 128, :], P.xs[c][:],
              reads=(f"xs{c}",), writes=(f"park{c}",))
        B.v("tensor_mul", out=P.tmpf[0][:], in0=P.xs[c][:], in1=P.bc32[0][:],
            reads=(f"xs{c}", "bc0"), writes=("tmpf0",))
        B.v("tensor_sub", out=P.xs[c][:], in0=P.tmpf[0][:], in1=P.bc32[1][:],
            reads=("tmpf0", "bc1"), writes=(f"xs{c}",))
    for i in range(4):
        emit_mamba_layer(P, W, i)
    for c in range(DC):
        B.dma(P.bc32[c % 2][:], P.park_dram[c * 128:(c + 1) * 128, :],
              reads=(f"park{c}",), writes=(f"bc{c%2}",))
        B.v("tensor_add", out=P.xs[c][:], in0=P.xs[c][:], in1=P.bc32[c % 2][:],
            reads=(f"xs{c}", f"bc{c%2}"), writes=(f"xs{c}",))


def emit_mamba_layer(P, W, i):
    B = P.B
    mxin = pool(P, "mxin", [128, 16 * 260], F32)
    xin = [mxin[:, m * 260:m * 260 + SC + 3] for m in range(16)]
    bfa = [pool(P, f"bfA{j}", [128, T], BF16) for j in range(DC)]
    xcb = [bfa[m // 8][:, (m % 8) * SC:(m % 8) * SC + SC] for m in range(16)]
    xck = [f"bfA{m // 8}" for m in range(16)]
    bfa6 = pool(P, "bfA6", [128, T], BF16)
    zld = [bfa6[:, j * SC:(j + 1) * SC] for j in range(2)]
    dtt = [xin[m][:, 0:SC] for m in range(16)]
    dtu = [bfa[2 + m // 8][:, (m % 8) * SC:(m % 8) * SC + SC] for m in range(16)]
    dtk = [f"bfA{2 + m // 8}" for m in range(16)]
    h_t = [bfa[4], bfa[5]]
    carry = pool(P, "mcarry", [128, 256], F32)
    def h_sl(n):
        return h_t[n // 8][:, (n % 8) * SC:(n % 8) * SC + SC]
    def h_key(n):
        return f"bfA{4 + n // 8}"
    tails = pool(P, "mtails", [128, 48], F32)
    nA = pool(P, "mnA", [128, 256], F32)
    dtb = pool(P, "mdtb", [128, 16], F32)
    cbt = pool(P, "mcb", [128, 16], F32)
    dsk = pool(P, "mD", [128, 16], F32)
    cwt = pool(P, "mcw", [128, 64], F32)
    bfb = [pool(P, "bfB0", [128, T], BF16), pool(P, "bfB1", [128, T], BF16)]
    xpw = bfb[1][:, 0:16 * 96]
    dtw = bfb[0][0:64, 0:T]
    dtrb = pool(P, "mdtrb", [64, SC], BF16)
    brow = pool(P, "mbrow", [16, SC], BF16)
    crow = pool(P, "mcrow", [16, SC], BF16)
    bfv = pool(P, "bfV", [128, 2210], BF16)
    seln = bfv[0:16, 0:2048]
    at = P.tmpf[0][:, 0:SC]
    bt = P.tmpf[0][:, SC:2 * SC]
    ca = at
    zsil = bt
    gtiles = [pool(P, f"mlg{c}", [128, 1], F32) for c in range(DC)]
    btiles = [pool(P, f"mlb{c}", [128, 1], F32) for c in range(DC)]
    onesr = pool(P, "ones", [128, 130], BF16)
    wst = [pool(P, "wS0", [128, 1024], BF16), pool(P, "wS1", [128, 1024], BF16)]
    P.const_val(1.0)
    P.const_val(1e-5)

    for c in range(DC):
        load_col(P, gtiles[c], W[f"l1m_lnw{i}"], c * 128, 128, f"mlg{c}")
        load_col(P, btiles[c], W[f"l1m_lnb{i}"], c * 128, 128, f"mlb{c}")
    a3 = W[f"l1m_A{i}"].rearrange("(dc p) n -> p dc n", p=128)
    B.dma(P.tmpf[0][:, 0:256].rearrange("p (dc n) -> p dc n", n=16), a3,
          writes=("tmpf0",))
    B.s("activation", out=nA[:], in_=P.tmpf[0][:, 0:256], func=AF.Exp,
        reads=("tmpf0",), writes=("mnA",))
    B.v("tensor_scalar", out=nA[:], in0=nA[:], scalar1=-1.0, scalar2=None,
        op0=ALU.mult, reads=("mnA",), writes=("mnA",))
    cw3 = W[f"l1m_cw{i}"].rearrange("(dc p) w -> p dc w", p=128)
    B.dma(cwt[:].rearrange("p (dc w) -> p dc w", w=4), cw3, writes=("mcw",))
    for dc in range(16):
        load_col(P, dtb[:, dc:dc + 1], W[f"l1m_dtb{i}"], dc * 128, 128, "mdtb")
        load_col(P, cbt[:, dc:dc + 1], W[f"l1m_cb{i}"], dc * 128, 128, "mcb")
        load_col(P, dsk[:, dc:dc + 1], W[f"l1m_D{i}"], dc * 128, 128, "mD")
    load_wT_blocks(P, W[f"l1m_xpT{i}"], 0, 96, xpw, "bfB1", nk=16)
    P.load_bf16(dtw, W[f"l1m_dtT{i}"][0:64, :], "bfB0")
    B.dma(seln[0:16, :], W["seln"][0:16, :], writes=("bfV",), queue="gpsimd")
    B.g("memset", onesr[:], 1.0, writes=("ones",))
    B.g("memset", carry[:], 0.0, writes=("mcarry",))
    B.g("memset", tails[:], 0.0, writes=("mtails",))

    emit_ln(P, gamma_beta=(
        [(gtiles[c][:, 0:1], f"mlg{c}") for c in range(DC)],
        [(btiles[c][:, 0:1], f"mlb{c}") for c in range(DC)]))

    inT = W[f"l1m_inT{i}"].rearrange("(c p) m -> p c m", p=128)
    outT = W[f"l1m_outT{i}"].rearrange("(m p) o -> p m o", p=128)
    for sc in range(NSC):
        s0 = sc * SC
        for mc in range(32):
            wt = wst[mc % 2]
            wkey = f"wS{mc % 2}"
            B.dma(wt[:].rearrange("p (c m) -> p c m", m=128),
                  inT[:, :, mc * 128:(mc + 1) * 128], writes=(wkey,),
                  queue="gpsimd")
            for c in range(DC):
                B.mm(P.ps[0][:, 0:SC], wt[:, c * 128:(c + 1) * 128],
                     P.xn[c][:, s0:s0 + SC],
                     start=(c == 0), stop=(c == DC - 1),
                     reads=(wkey, f"xn{c}"), writes=("ps0",))
            if mc < 16:
                B.v("tensor_copy", out=xin[mc][:, 3:3 + SC], in_=P.ps[0][:, 0:SC],
                    reads=("ps0",), writes=("mxin",))
            else:
                zc = mc - 16
                B.v("tensor_copy", out=zld[zc % 2], in_=P.ps[0][:, 0:SC],
                    reads=("ps0",), writes=("bfA6",))
                B.dma(P.z_dram[zc * 128:(zc + 1) * 128, :], zld[zc % 2],
                      reads=("bfA6",), writes=(f"zdram{zc}",))
        for mc in range(16):
            B.v("tensor_copy", out=xin[mc][:, 0:3], in_=tails[:, mc * 3:mc * 3 + 3],
                reads=("mtails",), writes=("mxin",))
            B.v("tensor_scalar", out=ca, in0=xin[mc][:, 0:SC],
                scalar1=cwt[:, mc * 4:mc * 4 + 1], scalar2=None, op0=ALU.mult,
                reads=("mxin", "mcw"), writes=("tmpf0",))
            for wv_ in range(1, 4):
                B.v("scalar_tensor_tensor", out=ca[:], in0=xin[mc][:, wv_:wv_ + SC],
                    scalar=cwt[:, mc * 4 + wv_:mc * 4 + wv_ + 1], in1=ca,
                    op0=ALU.mult, op1=ALU.add,
                    reads=("mxin", "mcw", "tmpf0"), writes=("tmpf0",))
            B.v("tensor_copy", out=tails[:, mc * 3:mc * 3 + 3],
                in_=xin[mc][:, SC:SC + 3],
                reads=("mxin",), writes=("mtails",))
            B.s("activation", out=xcb[mc][:], in_=ca[:], func=AF.Silu,
                bias=cbt[:, mc:mc + 1], reads=("tmpf0", "mcb"), writes=(xck[mc],))
        for mc in range(16):
            B.mm(P.ps[1][0:64, 0:SC], xpw[:, mc * 96:mc * 96 + 64], xcb[mc],
                 start=(mc == 0), stop=(mc == 15),
                 reads=("bfB1", xck[mc]), writes=("ps1",))
        for mc in range(16):
            B.mm(P.ps[6][0:16, 0:SC], xpw[:, mc * 96 + 64:mc * 96 + 80], xcb[mc],
                 start=(mc == 0), stop=(mc == 15),
                 reads=("bfB1", xck[mc]), writes=("ps6",))
        for mc in range(16):
            B.mm(P.ps[7][0:16, 0:SC], xpw[:, mc * 96 + 80:mc * 96 + 96], xcb[mc],
                 start=(mc == 0), stop=(mc == 15),
                 reads=("bfB1", xck[mc]), writes=("ps7",))
        B.v("tensor_copy", out=dtrb[0:64, :], in_=P.ps[1][0:64, 0:SC],
            reads=("ps1",), writes=("mdtrb",))
        B.v("tensor_copy", out=brow[0:16, :], in_=P.ps[6][0:16, 0:SC],
            reads=("ps6",), writes=("mbrow",))
        B.v("tensor_copy", out=crow[0:16, :], in_=P.ps[7][0:16, 0:SC],
            reads=("ps7",), writes=("mcrow",))
        for mc in range(16):
            B.mm(P.ps[2][:, 0:SC], dtw[0:64, mc * 128:(mc + 1) * 128], dtrb[0:64, :],
                 start=True, stop=True, reads=("bfB0", "mdtrb"), writes=("ps2",))
            B.v("tensor_scalar", out=dtt[mc], in0=P.ps[2][:, 0:SC],
                scalar1=dtb[:, mc:mc + 1], scalar2=None, op0=ALU.add,
                reads=("ps2", "mdtb"), writes=("mxin",))
            B.v("scalar_tensor_tensor", out=at, in0=dtt[mc], scalar=-1.0,
                in1=dtt[mc], op0=ALU.mult, op1=ALU.max,
                reads=("mxin",), writes=("tmpf0",))
            B.s("activation", out=at, in_=at, func=AF.Exp, scale=-1.0,
                reads=("tmpf0",), writes=("tmpf0",))
            B.s("activation", out=at, in_=at, func=AF.Ln, bias=1.0,
                reads=("tmpf0",), writes=("tmpf0",))
            B.v("scalar_tensor_tensor", out=dtt[mc], in0=dtt[mc], scalar=0.0,
                in1=at, op0=ALU.max, op1=ALU.add,
                reads=("mxin", "tmpf0"), writes=("mxin",))
            B.v("tensor_mul", out=dtu[mc], in0=dtt[mc], in1=xcb[mc],
                reads=("mxin", xck[mc]), writes=(dtk[mc],))
        for dc in range(16):
            for n in range(16):
                B.mm(P.ps[3][:, 0:SC], seln[0:16, n * 128:(n + 1) * 128],
                     brow[0:16, :],
                     start=True, stop=True, reads=("bfV", "mbrow"), writes=("ps3",))
                B.s("activation", out=at, in_=dtt[dc], func=AF.Exp,
                    scale=nA[:, dc * 16 + n:dc * 16 + n + 1],
                    reads=("mxin", "mnA"), writes=("tmpf0",))
                B.v("tensor_mul", out=bt, in0=dtu[dc], in1=P.ps[3][:, 0:SC],
                    reads=(dtk[dc], "ps3"), writes=("tmpf0",))
                B.v("tensor_tensor_scan", out=h_sl(n),
                    data0=at, data1=bt,
                    initial=carry[:, dc * 16 + n:dc * 16 + n + 1],
                    op0=ALU.mult, op1=ALU.add,
                    reads=("tmpf0", "tmpf0", "mcarry"), writes=(h_key(n),))
            for hb in range(2):
                B.v("tensor_copy", out=carry[:, dc * 16 + 8 * hb:dc * 16 + 8 * hb + 8],
                    in_=h_t[hb][:, 0:8 * SC].rearrange("p (n t) -> p n t", t=SC)[:, :, SC - 1:SC].squeeze(2),
                    reads=(f"bfA{4+hb}",), writes=("mcarry",))
            for n in range(16):
                B.mm(P.ps[4][:, 0:SC], seln[0:16, n * 128:(n + 1) * 128],
                     crow[0:16, :],
                     start=True, stop=True, reads=("bfV", "mcrow"), writes=("ps4",))
                B.v("tensor_mul", out=h_sl(n),
                    in0=h_sl(n), in1=P.ps[4][:, 0:SC],
                    reads=(h_key(n), "ps4"), writes=(h_key(n),))
            B.v("tensor_reduce", out=ca,
                in_=h_t[0][:, 0:8 * SC].rearrange("p (n t) -> p t n", t=SC),
                axis=AX.X, op=ALU.add, reads=("bfA4",), writes=("tmpf0",))
            B.v("tensor_reduce", out=bt,
                in_=h_t[1][:, 0:8 * SC].rearrange("p (n t) -> p t n", t=SC),
                axis=AX.X, op=ALU.add, reads=("bfA5",), writes=("tmpf0",))
            B.v("tensor_add", out=ca, in0=ca, in1=bt,
                reads=("tmpf0", "tmpf0"), writes=("tmpf0",))
            B.v("scalar_tensor_tensor", out=ca, in0=xcb[dc],
                scalar=dsk[:, dc:dc + 1], in1=ca, op0=ALU.mult, op1=ALU.add,
                reads=(xck[dc], "mD", "tmpf0"), writes=("tmpf0",))
            B.dma(zld[dc % 2], P.z_dram[dc * 128:(dc + 1) * 128, :],
                  reads=(f"zdram{dc}",), writes=("bfA6",))
            B.s("activation", out=zsil[:], in_=zld[dc % 2], func=AF.Silu,
                reads=("bfA6",), writes=("tmpf0",))
            B.v("tensor_mul", out=xcb[dc], in0=ca, in1=zsil,
                reads=("tmpf0", "tmpf0"), writes=(xck[dc],))
        for oc in range(DC):
            for hh in range(2):
                B.dma(wst[hh][:].rearrange("p (c m) -> p c m", m=128),
                      outT[:, hh * 8:(hh + 1) * 8, oc * 128:(oc + 1) * 128],
                      writes=(f"wS{hh}",), queue="gpsimd")
            for mc in range(16):
                B.mm(P.ps[5][:, 0:SC], wst[mc // 8][:, (mc % 8) * 128:(mc % 8) * 128 + 128],
                     xcb[mc],
                     start=(mc == 0), stop=(mc == 15),
                     reads=(f"wS{mc//8}", xck[mc]), writes=("ps5",))
            B.v("tensor_add", out=P.xs[oc][:, s0:s0 + SC],
                in0=P.xs[oc][:, s0:s0 + SC], in1=P.ps[5][:, 0:SC],
                reads=("ps5", f"xs{oc}"), writes=(f"xs{oc}",))



def emit_l2_ffn(P, W):
    B = P.B
    ident = pool(P, "identsb", [128, 128], F32)
    B.dma(ident[:], W["ident"][:, :], writes=("ident",))
    rw = pool(P, "l2rw", [128, 128], BF16)
    load_wT_blocks(P, W["l2_rw"], 0, 16, rw, "l2rw")
    rb = pool(P, "l1rb", [16, 1], F32)
    load_col(P, rb, W["l2_rb"], 0, 16, "l1rb")
    s_fm = P.bc32[0][0:16, :]
    for tt in range(4):
        for c in range(DC):
            B.mm(P.ps[0][0:16, :], rw[:, c * 16:(c + 1) * 16],
                 P.xn[c][:, tt * 512:(tt + 1) * 512],
                 start=(c == 0), stop=(c == DC - 1),
                 reads=("l2rw", f"xn{c}"), writes=("ps0",))
        B.v("tensor_scalar", out=s_fm[:, tt * 512:(tt + 1) * 512],
            in0=P.ps[0][0:16, :], scalar1=rb[0:16, 0:1], scalar2=None, op0=ALU.add,
            reads=("ps0", "l1rb"), writes=("bc0",))
    s_tm = P.tmpf[0][:, 0:256]
    for tc in range(16):
        B.t("transpose", P.ps[1][:, tc % 2 * 16:tc % 2 * 16 + 16],
            s_fm[0:16, tc * 128:(tc + 1) * 128], ident[0:16, 0:16],
            reads=("bc0", "ident"), writes=("ps1",))
        B.v("tensor_copy", out=s_tm[:, tc * 16:(tc + 1) * 16],
            in_=P.ps[1][:, tc % 2 * 16:tc % 2 * 16 + 16],
            reads=("ps1",), writes=("tmpf0",))
    t0 = P.tmpf[0]
    s3 = s_tm.rearrange("p (c e) -> p c e", e=16)
    ms = [t0[:, 256 + 16 * j:272 + 16 * j] for j in range(4)]
    msk = t0[:, 320:576]
    eqs = t0[:, 576:832]
    ctm = t0[:, 832:1088]
    esum = t0[:, 1088:1104]
    wj = t0[:, 1104:1120]
    ssrc = [s_tm, msk]

    def T0(method, **kw):
        B.v(method, reads=("tmpf0",), writes=("tmpf0",), **kw)
    T0("tensor_copy", out=msk, in_=s_tm)
    for j in range(4):
        T0("tensor_reduce", out=ms[j], in_=msk.rearrange("p (c e) -> p c e", e=16),
           axis=AX.X, op=ALU.max)
        if j < 3:
            mb = ms[j].rearrange("p (c e) -> p c e", e=1).broadcast_to((128, 16, 16))
            T0("tensor_tensor", out=eqs.rearrange("p (c e) -> p c e", e=16),
               in0=msk.rearrange("p (c e) -> p c e", e=16), in1=mb, op=ALU.is_equal)
            T0("scalar_tensor_tensor", out=msk, in0=eqs, scalar=NEG, in1=msk,
               op0=ALU.mult, op1=ALU.add)
    B.g("memset", esum, 0.0, writes=("tmpf0",))
    for j in (3, 2, 1, 0):
        T0("tensor_sub", out=wj, in0=ms[j], in1=ms[0])
        B.s("activation", out=ms[j], in_=wj, func=AF.Exp,
            reads=("tmpf0",), writes=("tmpf0",))
        T0("tensor_add", out=esum, in0=esum, in1=ms[j])
    B.v("reciprocal", out=esum, in_=esum, reads=("tmpf0",), writes=("tmpf0",))
    B.g("memset", ctm, 0.0, writes=("tmpf0",))
    T0("tensor_copy", out=msk, in_=s_tm)
    for j in range(4):
        T0("tensor_mul", out=ms[j], in0=ms[j], in1=esum)
        T0("tensor_reduce", out=wj, in_=msk.rearrange("p (c e) -> p c e", e=16),
           axis=AX.X, op=ALU.max)
        wb = wj.rearrange("p (c e) -> p c e", e=1).broadcast_to((128, 16, 16))
        T0("tensor_tensor", out=eqs.rearrange("p (c e) -> p c e", e=16),
           in0=msk.rearrange("p (c e) -> p c e", e=16), in1=wb, op=ALU.is_equal)
        T0("scalar_tensor_tensor", out=msk, in0=eqs, scalar=NEG, in1=msk,
           op0=ALU.mult, op1=ALU.add)
        wgt = ms[j].rearrange("p (c e) -> p c e", e=1).broadcast_to((128, 16, 16))
        T0("tensor_tensor", out=eqs.rearrange("p (c e) -> p c e", e=16),
           in0=eqs.rearrange("p (c e) -> p c e", e=16), in1=wgt, op=ALU.mult)
        T0("tensor_add", out=ctm, in0=ctm, in1=eqs)
    c_fm = P.bc32[1][0:16, :]
    for tc in range(16):
        B.t("transpose", P.ps[1][0:16, tc % 2 * 128:tc % 2 * 128 + 128],
            ctm[:, tc * 16:(tc + 1) * 16], ident[:],
            reads=("tmpf0", "ident"), writes=("ps1",))
        B.v("tensor_copy", out=c_fm[0:16, tc * 128:(tc + 1) * 128],
            in_=P.ps[1][0:16, tc % 2 * 128:tc % 2 * 128 + 128],
            reads=("ps1",), writes=("bc1",))
    B.dma(P.misc_dram[0:16, :], c_fm, reads=("bc1",), writes=("cdram",))
    w1t = [pool(P, f"bfA{j}", [128, T], BF16) for j in range(DC)]
    hts = [pool(P, "bfB0", [128, T], BF16), pool(P, "bfB1", [128, T], BF16),
           pool(P, "bfC0", [128, T], BF16), pool(P, "bfC1", [128, T], BF16)]
    w2t = pool(P, "bfV", [128, 2210], BF16)
    b1c = pool(P, "l2b1", [128, 16], F32)
    b2r = P.tmpf[0][0:1, 1024:1536].bitcast(BF16)
    cer = P.tmpf[0][0:1, 0:1024].bitcast(BF16)
    ceb = [P.bc32[0], P.bc32[1]]
    cebk = ["bc0", "bc1"]

    for e in range(16):
        pe = e % 2
        B.dma(ceb[pe][:],
              P.misc_dram[e:e + 1, :].partition_broadcast(128).squeeze(1),
              reads=("cdram",), writes=(cebk[pe],))
        B.dma(cer, P.misc_dram[e:e + 1, :], reads=("cdram",),
              writes=("tmpf0",), queue="gpsimd")
        B.dma(b2r, W["l2_b2"][e:e + 1, :], reads=(), writes=("tmpf0",),
              queue="gpsimd")
        for c in range(DC):
            B.dma(w1t[c][:], W["l2_w1"][e, c * 128:(c + 1) * 128, :],
                  writes=(f"bfA{c}",), queue="gpsimd")
        for ic in range(16):
            load_col(P, b1c[:, ic:ic + 1], W["l2_b1"][e], ic * 128, 128, "l2b1")
        for tt in range(4):
            for ic in range(16):
                for c in range(DC):
                    B.mm(P.ps[6][:, :], w1t[c][:, ic * 128:(ic + 1) * 128],
                         P.xn[c][:, tt * 512:(tt + 1) * 512],
                         start=(c == 0), stop=(c == DC - 1),
                         reads=(f"bfA{c}", f"xn{c}"), writes=("ps6",))
                B.s("activation", out=hts[ic // 4][:, (ic % 4) * 512:(ic % 4) * 512 + 512],
                    in_=P.ps[6][:, :], func=AF.Gelu, bias=b1c[:, ic:ic + 1],
                    reads=("ps6", "l2b1"), writes=(f"bf{'BC'[ic//8]}{(ic//4)%2}",))
                B.v("tensor_mul",
                    out=hts[ic // 4][:, (ic % 4) * 512:(ic % 4) * 512 + 512],
                    in0=hts[ic // 4][:, (ic % 4) * 512:(ic % 4) * 512 + 512],
                    in1=ceb[pe][:, tt * 512:(tt + 1) * 512],
                    reads=(f"bf{'BC'[ic//8]}{(ic//4)%2}", cebk[pe]),
                    writes=(f"bf{'BC'[ic//8]}{(ic//4)%2}",))
            for dc in range(DC):
                B.dma(w2t[:, 0:2048].rearrange("p (c o) -> p c o", o=128),
                      W["l2_w2"][e].rearrange("(c p) o -> p c o", p=128)[:, :, dc * 128:(dc + 1) * 128],
                      writes=("bfV",), queue="gpsimd")
                for ic in range(16):
                    B.mm(P.ps[2 + dc % 2][:, :],
                         w2t[:, ic * 128:(ic + 1) * 128],
                         hts[ic // 4][:, (ic % 4) * 512:(ic % 4) * 512 + 512],
                         start=(ic == 0), stop=False,
                         reads=("bfV", f"bf{'BC'[ic//8]}{(ic//4)%2}"),
                         writes=(f"ps{2+dc%2}",))
                B.mm(P.ps[2 + dc % 2][:, :], b2r[0:1, dc * 128:(dc + 1) * 128],
                     cer[0:1, tt * 512:(tt + 1) * 512],
                     start=False, stop=True,
                     reads=("tmpf0",), writes=(f"ps{2+dc%2}",))
                B.v("scalar_tensor_tensor",
                    out=P.xs[dc][:, tt * 512:(tt + 1) * 512],
                    in0=P.ps[2 + dc % 2][:, :], scalar=0.5,
                    in1=P.xs[dc][:, tt * 512:(tt + 1) * 512],
                    op0=ALU.mult, op1=ALU.add,
                    reads=(f"ps{2+dc%2}", f"xs{dc}"), writes=(f"xs{dc}",))



def emit_l2_attn(P, W):
    B = P.B
    gt = [pool(P, f"gbg{c}", [128, 1], F32) for c in range(DC)]
    bt_ = [pool(P, f"gbb{c}", [128, 1], F32) for c in range(DC)]
    for c in range(DC):
        load_col(P, gt[c], W["l2a_lng"], c * 128, 128, f"gbg{c}")
        load_col(P, bt_[c], W["l2a_lnb"], c * 128, 128, f"gbb{c}")
    emit_ln2(P, gamma_beta=(
        [(gt[c][:, 0:1], f"gbg{c}") for c in range(DC)],
        [(bt_[c][:, 0:1], f"gbb{c}") for c in range(DC)]))
    ktm = [pool(P, f"bfA{j}", [128, T], BF16) for j in range(DC)]
    wres = [pool(P, "bfB0", [128, T], BF16), pool(P, "bfB1", [128, T], BF16),
            pool(P, "bfC0", [128, T], BF16), pool(P, "bfC1", [128, T], BF16)]
    wkeys = ["bfB0", "bfB1", "bfC0", "bfC1"]
    bfv = pool(P, "bfV", [128, 2210], BF16)
    wst = [pool(P, "wS0", [128, 1024], BF16), pool(P, "wS1", [128, 1024], BF16)]
    brow = P.tmpf[0][0:1, 1536:1792].bitcast(BF16)
    bcol = pool(P, "l2bcol", [128, 8], F32)
    ksum = pool(P, "l2ksum", [128, 8], BF16)
    vl = None
    kp = pool(P, "mxin", [128, 16 * 260], F32).bitcast(BF16)[:, 0:2176]
    vl = [kp[:, 0:512], kp[:, 512:1024]]
    vlk = ["mxin", "mxin"]
    bias3 = ktm[1][:, 1024:1792]
    mb = pool(P, "l2mb", [16, 2], F32)
    mh = [pool(P, f"l0mh{i}", [128, 1], F32) for i in range(2)]
    wt = [ktm[1][0:64, 0:512], ktm[1][0:64, 512:1024]]
    wtk = ["bfA1", "bfA1"]
    onesr = pool(P, "ones", [128, 130], BF16)
    bvr = pool(P, "l0bvr", [1, 128], BF16)
    B.g("memset", onesr[:], 1.0, writes=("ones",))

    def proj_feature_major(wname, bname, dst_tiles, dstkeys, epilogue):
        for oc in range(DC):
            B.dma(wst[oc % 2][:].rearrange("p (c m) -> p c m", m=128),
                  W[wname].rearrange("(c p) m -> p c m", p=128)[:, :, oc * 128:(oc + 1) * 128],
                  writes=(f"wS{oc%2}",), queue="gpsimd")
            load_col(P, bcol[:, oc:oc + 1], W[bname], oc * 128, 128, "l2bcol")
            for tt in range(4):
                for c in range(DC):
                    B.mm(P.ps[tt % 2][:, :], wst[oc % 2][:, c * 128:(c + 1) * 128],
                         P.xn[c][:, tt * 512:(tt + 1) * 512],
                         start=(c == 0), stop=(c == DC - 1),
                         reads=(f"wS{oc%2}", f"xn{c}"), writes=(f"ps{tt%2}",))
                epilogue(oc, tt, P.ps[tt % 2], bcol[:, oc:oc + 1])

    def tm_proj(wname, bname, out_cb):
        for c in range(DC):
            B.dma(wres[c // 2][:, (c % 2) * 1024:(c % 2) * 1024 + 1024],
                  W[wname][c * 128:(c + 1) * 128, :],
                  writes=(wkeys[c // 2],), queue="gpsimd")
        for nv in range(2):
            B.dma(brow,
                  W[bname][nv * 512:(nv + 1) * 512].rearrange("(o n) -> o n", o=1),
                  writes=("tmpf0",), queue="gpsimd")
            for tc in range(16):
                for c in range(DC):
                    B.mm(P.ps[nv][:, :], P.xn[c][:, tc * 128:(tc + 1) * 128],
                         wres[c // 2][:, (c % 2) * 1024 + nv * 512:(c % 2) * 1024 + nv * 512 + 512],
                         start=(c == 0), stop=False,
                         reads=(wkeys[c // 2], f"xn{c}"), writes=(f"ps{nv}",))
                B.mm(P.ps[nv][:, :], onesr[0:1, 0:128], brow,
                     start=False, stop=True,
                     reads=("ones", "tmpf0"), writes=(f"ps{nv}",))
                out_cb(tc, nv, P.ps[nv])

    t0 = P.tmpf[0]

    def elu1_from_psum(ps_ap, dst_ap, dkey, pskey, bias_col=None):
        if bias_col is None:
            B.v("tensor_scalar", out=t0[:, 0:512], in0=ps_ap, scalar1=0.0,
                scalar2=None, op0=ALU.min, reads=(pskey,), writes=("tmpf0",))
            B.v("tensor_scalar", out=t0[:, 512:1024], in0=ps_ap, scalar1=0.0,
                scalar2=None, op0=ALU.max, reads=(pskey,), writes=("tmpf0",))
        else:
            B.v("tensor_scalar", out=t0[:, 0:512], in0=ps_ap, scalar1=bias_col,
                scalar2=0.0, op0=ALU.add, op1=ALU.min,
                reads=(pskey, "l2bcol"), writes=("tmpf0",))
            B.v("tensor_scalar", out=t0[:, 512:1024], in0=ps_ap, scalar1=bias_col,
                scalar2=0.0, op0=ALU.add, op1=ALU.max,
                reads=(pskey, "l2bcol"), writes=("tmpf0",))
        B.s("activation", out=t0[:, 0:512], in_=t0[:, 0:512], func=AF.Exp,
            reads=("tmpf0",), writes=("tmpf0",))
        B.v("tensor_add", out=dst_ap, in0=t0[:, 0:512], in1=t0[:, 512:1024],
            reads=("tmpf0",), writes=(dkey,))

    def k_cb2(tc, nv, ps):
        dst = ktm[tc // 2][:, (tc % 2) * 1024 + nv * 512:(tc % 2) * 1024 + nv * 512 + 512]
        elu1_from_psum(ps[:, :], dst, f"bfA{tc//2}", f"ps{nv}")

    tm_proj("l2a_wkl", "l2a_bkl", k_cb2)
    def v_cb(tc, nv, ps):
        dst = bfv[:, nv * 512:nv * 512 + 512]
        B.v("tensor_copy", out=dst, in_=ps[:, :], reads=(f"ps{nv}",), writes=("bfV",))
        B.dma(P.vlin_dram[tc * 128:(tc + 1) * 128, nv * 512:(nv + 1) * 512], dst,
              reads=("bfV",), writes=("vlind",))
    tm_proj("l2a_wvl", "l2a_bvl", v_cb)
    kvsb = wres
    for ev in range(2):
        for tc in range(16):
            B.dma(vl[tc % 2], P.vlin_dram[tc * 128:(tc + 1) * 128,
                                          ev * 512:(ev + 1) * 512],
                  reads=("vlind",), writes=("mxin",))
            for dm in range(DC):
                B.mm(P.ps[dm][:, :],
                     ktm[tc // 2][:, (tc % 2) * 1024 + dm * 128:(tc % 2) * 1024 + dm * 128 + 128],
                     vl[tc % 2],
                     start=(tc == 0), stop=(tc == 15),
                     reads=(f"bfA{tc//2}", "mxin"), writes=(f"ps{dm}",))
        for dm in range(DC):
            B.v("tensor_copy",
                out=kvsb[dm // 2][:, (dm % 2) * 1024 + ev * 512:(dm % 2) * 1024 + ev * 512 + 512],
                in_=P.ps[dm][:, :], reads=(f"ps{dm}",), writes=(wkeys[dm // 2],))
    for dm in range(DC):
        for tc in range(16):
            B.mm(P.ps[0][:, 0:1],
                 ktm[tc // 2][:, (tc % 2) * 1024 + dm * 128:(tc % 2) * 1024 + dm * 128 + 128],
                 onesr[:, 128:129],
                 start=(tc == 0), stop=(tc == 15),
                 reads=(f"bfA{tc//2}", "ones"), writes=("ps0",))
        B.v("tensor_copy", out=ksum[:, dm:dm + 1], in_=P.ps[0][:, 0:1],
            reads=("ps0",), writes=("l2ksum",))
    qfm = ktm

    def q_cb(oc, tt, ps, bias_col):
        dst = qfm[oc][:, tt * 512:(tt + 1) * 512]
        elu1_from_psum(ps[:, :], dst, f"bfA{oc}", f"ps{tt%2}", bias_col=bias_col)
    proj_feature_major("l2a_wql", "l2a_bql", None, None, q_cb)
    P.const_val(1e-6)
    for tt in range(4):
        for dm in range(DC):
            B.mm(P.ps[7][0:1, :], ksum[:, dm:dm + 1],
                 qfm[dm][:, tt * 512:(tt + 1) * 512],
                 start=(dm == 0), stop=(dm == DC - 1),
                 reads=("l2ksum", f"bfA{dm}"), writes=("ps7",))
        B.v("tensor_scalar", out=P.rows[0][0:1, 0:512], in0=P.ps[7][0:1, :],
            scalar1=1e-6, scalar2=None, op0=ALU.add,
            reads=("ps7",), writes=("bc1",))
        B.v("reciprocal", out=P.rows[0][0:1, 0:512], in_=P.rows[0][0:1, 0:512],
            reads=("bc1",), writes=("bc1",))
        P.bcast_row(P.bc32[0][:, 0:512], P.rows[0][0:1, 0:512], 0, 512, 128,
                    "bc1", "bc0")
        for em in range(DC):
            for dm in range(DC):
                B.mm(P.ps[2 + em % 2][:, :],
                     kvsb[dm // 2][:, (dm % 2) * 1024 + em * 128:(dm % 2) * 1024 + em * 128 + 128],
                     qfm[dm][:, tt * 512:(tt + 1) * 512],
                     start=(dm == 0), stop=(dm == DC - 1),
                     reads=(wkeys[dm // 2], f"bfA{dm}"), writes=(f"ps{2+em%2}",))
            B.v("tensor_mul", out=bfv[:, (em % 2) * 512:(em % 2) * 512 + 512],
                in0=P.ps[2 + em % 2][:, :], in1=P.bc32[0][:, 0:512],
                reads=(f"ps{2+em%2}", "bc0"), writes=("bfV",))
            B.dma(P.lin_dram[em * 128:(em + 1) * 128, tt * 512:(tt + 1) * 512],
                  bfv[:, (em % 2) * 512:(em % 2) * 512 + 512],
                  reads=("bfV",), writes=("lind",))
    id16 = pool(P, "id16", [16, 16], F32)
    B.dma(id16[0:16, 0:16], W["ident"][0:16, 0:16], writes=("ident",))
    rbt = ktm[2][0:16, 0:129]
    B.dma(rbt[0:16, :], W["l2a_rbT"][:, :], writes=("bfA2",), queue="gpsimd")
    B.v("reduce_max", out=mb[0:16, 0:1], in_=rbt[0:16, :], axis=AX.X,
        reads=("bfA2",), writes=("l2mb",))
    mbrow = pool(P, "l2mbrow", [1, 16], F32)
    B.mm(P.ps[0][0:1, 0:16], mb[0:16, 0:1], id16[0:16, 0:16],
         start=True, stop=True, reads=("l2mb", "ident"), writes=("ps0",))
    B.v("tensor_copy", out=mbrow[0:1, 0:16], in_=P.ps[0][0:1, 0:16],
        reads=("ps0",), writes=("l2mbrow",))
    for hp in range(8):
        B.dma(wst[0][:].rearrange("p (c m) -> p c m", m=128),
              W["l2a_wqw"].rearrange("(c p) m -> p c m", p=128)[:, :, hp * 128:(hp + 1) * 128],
              writes=("wS0",), queue="gpsimd")
        B.dma(wst[1][:].rearrange("p (c m) -> p c m", m=128),
              W["l2a_wkw"].rearrange("(c p) m -> p c m", p=128)[:, :, hp * 128:(hp + 1) * 128],
              writes=("wS1",), queue="gpsimd")
        load_col(P, bcol[:, 0:1], W["l2a_bqw"], hp * 128, 128, "l2bcol")
        load_col(P, bcol[:, 1:2], W["l2a_bkw"], hp * 128, 128, "l2bcol")
        B.dma(bvr[0:1, :],
              W["l2a_bvw"][hp * 128:(hp + 1) * 128].rearrange("(o n) -> o n", o=1),
              writes=("l0bvr",), queue="gpsimd")
        for tt in range(4):
            for c in range(DC):
                B.mm(P.ps[0][:, :], wst[0][:, c * 128:(c + 1) * 128],
                     P.xn[c][:, tt * 512:(tt + 1) * 512],
                     start=(c == 0), stop=(c == DC - 1),
                     reads=("wS0", f"xn{c}"), writes=("ps0",))
                B.mm(P.ps[1][:, :], wst[1][:, c * 128:(c + 1) * 128],
                     P.xn[c][:, tt * 512:(tt + 1) * 512],
                     start=(c == 0), stop=(c == DC - 1),
                     reads=("wS1", f"xn{c}"), writes=("ps1",))
            B.v("tensor_scalar", out=ktm[0][:, tt * 512:(tt + 1) * 512],
                in0=P.ps[0][:, :], scalar1=bcol[:, 0:1], scalar2=None, op0=ALU.add,
                reads=("ps0", "l2bcol"), writes=("bfA0",))
            B.v("tensor_scalar", out=kp[:, 64 + tt * 512:64 + (tt + 1) * 512],
                in0=P.ps[1][:, :], scalar1=bcol[:, 1:2], scalar2=None, op0=ALU.add,
                reads=("ps1", "l2bcol"), writes=("mxin",))
        B.g("memset", kp[:, 0:64], 0.0, writes=("mxin",))
        B.g("memset", kp[:, 2112:2176], 0.0, writes=("mxin",))
        v3 = bfv[:].rearrange("p (k c) -> p k c", c=130)
        B.g("memset", v3[:, :, 64:65], 1.0, writes=("vtm_ones",))
        B.g("memset", v3[:, :, 129:130], 1.0, writes=("vtm_ones",))
        B.g("memset", bfv[0:64, 0:130], 0.0, writes=("bfV",))
        B.g("memset", bfv[64:128, 16 * 130:17 * 130], 0.0, writes=("bfV",))
        B.dma(wst[0][:].rearrange("p (c m) -> p c m", m=128),
              W["l2a_wvw"].rearrange("(c p) m -> p c m", p=128)[:, :, hp * 128:(hp + 1) * 128],
              writes=("wS0",), queue="gpsimd")
        for tc in range(16):
            for c in range(DC):
                B.mm(P.ps[2][:, 0:128], P.xn[c][:, tc * 128:(tc + 1) * 128],
                     wst[0][:, c * 128:(c + 1) * 128],
                     start=(c == 0), stop=False,
                     reads=("wS0", f"xn{c}"), writes=("ps2",))
            B.mm(P.ps[2][:, 0:128], onesr[0:1, 0:128], bvr[0:1, :],
                 start=False, stop=True, reads=("ones", "l0bvr"), writes=("ps2",))
            for hh in range(2):
                B.v("tensor_copy",
                    out=bfv[64:128, tc * 130 + 65 * hh:tc * 130 + 65 * hh + 64],
                    in_=P.ps[2][0:64, 64 * hh:64 * hh + 64],
                    reads=("ps2",), writes=("bfV",))
                B.v("tensor_copy",
                    out=bfv[0:64, (tc + 1) * 130 + 65 * hh:(tc + 1) * 130 + 65 * hh + 64],
                    in_=P.ps[2][64:128, 64 * hh:64 * hh + 64],
                    reads=("ps2",), writes=("bfV",))
        B.s("activation", out=P.sq[0][:], in_=ktm[0][:], func=AF.Square,
            reads=("bfA0",), writes=("sq0",))
        B.s("activation", out=P.sq[1][:, 0:2048], in_=kp[:, 64:2112], func=AF.Square,
            reads=("mxin",), writes=("sq1",))
        for hh in range(2):
            base = 64 * hh
            for src_i in range(2):
                for tt in range(4):
                    B.mm(P.ps[3][0:1, :], P.ones_bf[base:base + 64, 0:1],
                         P.sq[src_i][base:base + 64, tt * 512:(tt + 1) * 512],
                         start=True, stop=True,
                         reads=(f"sq{src_i}", "ones"), writes=("ps3",))
                    B.v("tensor_copy", out=P.row2[0:1, tt * 512:(tt + 1) * 512],
                        in_=P.ps[3][0:1, :], reads=("ps3",), writes=("tmpf0",))
                B.v("reduce_max", out=P.rows[src_i][0:1, src_i:src_i + 1],
                    in_=P.row2, axis=AX.X, reads=("tmpf0",), writes=(('bc1', 'bc0')[src_i],))
            B.s("activation", out=P.rows[0][0:1, 0:1], in_=P.rows[0][0:1, 0:1],
                func=AF.Sqrt, reads=("bc1",), writes=("bc1",))
            B.s("activation", out=P.rows[1][0:1, 1:2], in_=P.rows[1][0:1, 1:2],
                func=AF.Sqrt, reads=("bc0",), writes=("bc0",))
            B.v("tensor_mul", out=P.rows[0][0:1, 2:3], in0=P.rows[0][0:1, 0:1],
                in1=P.rows[1][0:1, 1:2], reads=("bc1", "bc0"), writes=("bc1",))
            B.v("scalar_tensor_tensor", out=P.rows[0][0:1, 3:4],
                in0=P.rows[0][0:1, 2:3], scalar=0.125,
                in1=mbrow[0:1, 2 * hp + hh:2 * hp + hh + 1],
                op0=ALU.mult, op1=ALU.add, reads=("bc1", "l2mbrow"), writes=("bc1",))
            B.v("tensor_scalar", out=P.rows[0][0:1, 4:5], in0=P.rows[0][0:1, 3:4],
                scalar1=-1.0, scalar2=None, op0=ALU.mult,
                reads=("bc1",), writes=("bc1",))
            P.bcast_row(mh[hh][:, 0:1], P.rows[0][0:1, 4:5], 2 + hh, 1, 128,
                        "bc1", f"l0mh{hh}")
        for hh in range(2):
            base = 64 * hh
            h = 2 * hp + hh
            for var_ in range(3):
                B.dma(bias3[:, var_ * 256:(var_ + 1) * 256].rearrange(
                          "p (jh i) -> p jh i", i=128),
                      W["l2a_biasT"][var_, h].rearrange("(jh p) i -> p jh i", p=128),
                      writes=("bfA1",), queue="gpsimd")
            for qc in range(16):
                var = 0 if qc == 0 else (2 if qc == 15 else 1)
                for jh in range(2):
                    B.mm(P.ps[jh][:, 0:128],
                         kp[base:base + 64, qc * 128 + jh * 128:qc * 128 + jh * 128 + 128],
                         ktm[0][base:base + 64, qc * 128:(qc + 1) * 128],
                         start=True, stop=True,
                         reads=("mxin", "bfA0"), writes=(f"ps{jh}",))
                    B.v("scalar_tensor_tensor", out=P.tmpf[0][:, jh * 128:jh * 128 + 128],
                        in0=P.ps[jh][:, 0:128], scalar=0.125,
                        in1=bias3[:, var * 256 + jh * 128:var * 256 + jh * 128 + 128],
                        op0=ALU.mult, op1=ALU.add,
                        reads=(f"ps{jh}", "bfA1"), writes=("tmpf0",))
                    B.s("activation", out=P.sq[0][:, jh * 128:jh * 128 + 128],
                        in_=P.tmpf[0][:, jh * 128:jh * 128 + 128], func=AF.Exp,
                        bias=mh[hh][:, 0:1],
                        reads=("tmpf0", f"l0mh{hh}"), writes=("sq0",))
                for jh in range(2):
                    B.mm(P.ps[4 + qc // 4][0:65, (qc % 4) * 128:(qc % 4) * 128 + 128],
                         bfv[:, (qc + jh) * 130 + 65 * hh:(qc + jh) * 130 + 65 * hh + 65],
                         P.sq[0][:, jh * 128:jh * 128 + 128],
                         start=(jh == 0), stop=(jh == 1),
                         reads=("bfV", "vtm_ones", "sq0"), writes=(f"ps{4+qc//4}",))
            for qt in range(4):
                B.v("reciprocal", out=P.row2[0:1, qt * 512:(qt + 1) * 512],
                    in_=P.ps[4 + qt][64:65, :], reads=(f"ps{4+qt}",), writes=("tmpf0",))
            P.bcast_row(P.bc32[0][0:64, :], P.row2, 4, T, 64, "tmpf0", "bc0")
            for qt in range(4):
                B.v("tensor_mul", out=wt[qt % 2],
                    in0=P.ps[4 + qt][0:64, :], in1=P.bc32[0][0:64, qt * 512:(qt + 1) * 512],
                    reads=(f"ps{4+qt}", "bc0"), writes=(wtk[qt % 2],))
                B.dma(P.win_dram[h * 64:(h + 1) * 64, qt * 512:(qt + 1) * 512],
                      wt[qt % 2], reads=(wtk[qt % 2],), writes=("wind",))
    lint = [pool(P, "bfB0", [128, T], BF16), pool(P, "bfB1", [128, T], BF16)]
    wint = [pool(P, "bfC0", [128, T], BF16), pool(P, "bfC1", [128, T], BF16)]
    g1r = [pool(P, f"bfA{j}", [128, T], BF16) for j in range(DC)]
    mixt = [bfv[:, 0:T], kp[:, 0:T]]
    mixk = ["bfV", "mxin"]
    for tt in range(4):
        for kc in range(DC):
            B.dma(lint[kc // 4][:, (kc % 4) * 512:(kc % 4) * 512 + 512],
                  P.lin_dram[kc * 128:(kc + 1) * 128, tt * 512:(tt + 1) * 512],
                  reads=("lind",), writes=(wkeys[kc // 4],))
            B.dma(wint[kc // 4][:, (kc % 4) * 512:(kc % 4) * 512 + 512],
                  P.win_dram[kc * 128:(kc + 1) * 128, tt * 512:(tt + 1) * 512],
                  reads=("wind",), writes=(wkeys[2 + kc // 4],))
        for oc in range(DC):
            B.dma(wst[oc % 2][:].rearrange("p (c m) -> p c m", m=128),
                  W["l2a_wg1"].rearrange("(c p) m -> p c m", p=128)[:, 0:8, oc * 128:(oc + 1) * 128],
                  writes=(f"wS{oc%2}",), queue="gpsimd")
            for c in range(DC):
                B.mm(P.ps[oc % 2][:, :], wst[oc % 2][:, c * 128:(c + 1) * 128],
                     lint[c // 4][:, (c % 4) * 512:(c % 4) * 512 + 512],
                     start=(c == 0), stop=False,
                     reads=(f"wS{oc%2}", wkeys[c // 4]), writes=(f"ps{oc%2}",))
            B.dma(wst[oc % 2][:].rearrange("p (c m) -> p c m", m=128),
                  W["l2a_wg1"].rearrange("(c p) m -> p c m", p=128)[:, 8:16, oc * 128:(oc + 1) * 128],
                  writes=(f"wS{oc%2}",), queue="gpsimd")
            for c in range(DC):
                B.mm(P.ps[oc % 2][:, :], wst[oc % 2][:, c * 128:(c + 1) * 128],
                     wint[c // 4][:, (c % 4) * 512:(c % 4) * 512 + 512],
                     start=False, stop=False,
                     reads=(f"wS{oc%2}", wkeys[2 + c // 4]), writes=(f"ps{oc%2}",))
            B.dma(wst[oc % 2][:].rearrange("p (c m) -> p c m", m=128),
                  W["l2a_wg1"].rearrange("(c p) m -> p c m", p=128)[:, 16:24, oc * 128:(oc + 1) * 128],
                  writes=(f"wS{oc%2}",), queue="gpsimd")
            for c in range(DC):
                B.mm(P.ps[oc % 2][:, :], wst[oc % 2][:, c * 128:(c + 1) * 128],
                     P.xn[c][:, tt * 512:(tt + 1) * 512],
                     start=False, stop=(c == DC - 1),
                     reads=(f"wS{oc%2}", f"xn{c}"), writes=(f"ps{oc%2}",))
            load_col(P, bcol[:, oc:oc + 1], W["l2a_bg1"], oc * 128, 128, "l2bcol")
            B.v("tensor_scalar", out=g1r[oc][:, tt * 512:(tt + 1) * 512],
                in0=P.ps[oc % 2][:, :], scalar1=bcol[:, oc:oc + 1], scalar2=0.0,
                op0=ALU.add, op1=ALU.max,
                reads=(f"ps{oc%2}", "l2bcol"), writes=(f"bfA{oc}",))
    for tt in range(4):
        for kc in range(DC):
            B.dma(lint[kc // 4][:, (kc % 4) * 512:(kc % 4) * 512 + 512],
                  P.lin_dram[kc * 128:(kc + 1) * 128, tt * 512:(tt + 1) * 512],
                  reads=("lind",), writes=(wkeys[kc // 4],))
            B.dma(wint[kc // 4][:, (kc % 4) * 512:(kc % 4) * 512 + 512],
                  P.win_dram[kc * 128:(kc + 1) * 128, tt * 512:(tt + 1) * 512],
                  reads=("wind",), writes=(wkeys[2 + kc // 4],))
        for oc in range(DC):
            B.dma(wst[oc % 2][:].rearrange("p (c m) -> p c m", m=128),
                  W["l2a_wg2"].rearrange("(c p) m -> p c m", p=128)[:, :, oc * 128:(oc + 1) * 128],
                  writes=(f"wS{oc%2}",), queue="gpsimd")
            for c in range(DC):
                B.mm(P.ps[oc % 2][:, :], wst[oc % 2][:, c * 128:(c + 1) * 128],
                     g1r[c][:, tt * 512:(tt + 1) * 512],
                     start=(c == 0), stop=(c == DC - 1),
                     reads=(f"wS{oc%2}", f"bfA{c}"), writes=(f"ps{oc%2}",))
            load_col(P, bcol[:, oc:oc + 1], W["l2a_bg2"], oc * 128, 128, "l2bcol")
            B.s("activation", out=P.tmpf[0][:, 0:512], in_=P.ps[oc % 2][:, :],
                func=AF.Sigmoid, bias=bcol[:, oc:oc + 1],
                reads=(f"ps{oc%2}", "l2bcol"), writes=("tmpf0",))
            B.v("tensor_sub", out=P.tmpf[0][:, 512:1024],
                in0=lint[oc // 4][:, (oc % 4) * 512:(oc % 4) * 512 + 512],
                in1=wint[oc // 4][:, (oc % 4) * 512:(oc % 4) * 512 + 512],
                reads=(wkeys[oc // 4], wkeys[2 + oc // 4]), writes=("tmpf0",))
            B.v("tensor_mul", out=P.tmpf[0][:, 512:1024],
                in0=P.tmpf[0][:, 512:1024], in1=P.tmpf[0][:, 0:512],
                reads=("tmpf0",), writes=("tmpf0",))
            B.v("tensor_add", out=mixt[oc % 2][:, (oc // 2) * 512:(oc // 2) * 512 + 512],
                in0=P.tmpf[0][:, 512:1024],
                in1=wint[oc // 4][:, (oc % 4) * 512:(oc % 4) * 512 + 512],
                reads=("tmpf0", wkeys[2 + oc // 4]), writes=(mixk[oc % 2],))
        for dc in range(DC):
            B.dma(wst[dc % 2][:].rearrange("p (c m) -> p c m", m=128),
                  W["l2a_wo"].rearrange("(c p) m -> p c m", p=128)[:, :, dc * 128:(dc + 1) * 128],
                  writes=(f"wS{dc%2}",), queue="gpsimd")
            for c in range(DC):
                B.mm(P.ps[2 + dc % 2][:, :], wst[dc % 2][:, c * 128:(c + 1) * 128],
                     mixt[c % 2][:, (c // 2) * 512:(c // 2) * 512 + 512],
                     start=(c == 0), stop=(c == DC - 1),
                     reads=(f"wS{dc%2}", mixk[c % 2]), writes=(f"ps{2+dc%2}",))
            load_col(P, bcol[:, dc:dc + 1], W["l2a_bo"], dc * 128, 128, "l2bcol")
            B.v("scalar_tensor_tensor",
                out=P.xs[dc][:, tt * 512:(tt + 1) * 512],
                in0=P.ps[2 + dc % 2][:, :], scalar=bcol[:, dc:dc + 1],
                in1=P.xs[dc][:, tt * 512:(tt + 1) * 512],
                op0=ALU.add, op1=ALU.add,
                reads=(f"ps{2+dc%2}", "l2bcol", f"xs{dc}"), writes=(f"xs{dc}",))
    for c in range(DC):
        B.dma(P.bc32[c % 2][:], P.park_dram[c * 128:(c + 1) * 128, :],
              reads=(f"park{c}",), writes=(f"bc{c%2}",))
        B.v("tensor_add", out=P.xs[c][:], in0=P.xs[c][:], in1=P.bc32[c % 2][:],
            reads=(f"xs{c}", f"bc{c%2}"), writes=(f"xs{c}",))



def emit_ln_apply_tiles(P, tiles, keys, gamma_beta=None, act=None, eps=1e-5):
    B = P.B
    TT4 = T // 512
    for tt in range(TT4):
        for c in range(DC):
            B.mm(P.ps[4 + tt][0:1, :], P.ones_bf[:, 0:1],
                 tiles[c][:, tt * 512:(tt + 1) * 512],
                 start=(c == 0), stop=(c == DC - 1),
                 reads=(keys[c], "ones"), writes=(f"ps{4+tt}",))
        B.v("tensor_copy", out=P.rows[0][0:1, tt * 512:(tt + 1) * 512],
            in_=P.ps[4 + tt][0:1, :], reads=(f"ps{4+tt}",), writes=("bc1",))
    for c in range(DC):
        B.s("activation", out=P.sq[c % 2][:], in_=tiles[c][:], func=AF.Square,
            reads=(keys[c],), writes=(f"sq{c%2}",))
        for tt in range(TT4):
            B.mm(P.ps[4 + tt][0:1, :], P.ones_bf[:, 0:1],
                 P.sq[c % 2][:, tt * 512:(tt + 1) * 512],
                 start=(c == 0), stop=(c == DC - 1),
                 reads=(f"sq{c%2}", "ones"), writes=(f"ps{4+tt}",))
    for tt in range(TT4):
        B.v("tensor_copy", out=P.rows[1][0:1, tt * 512:(tt + 1) * 512],
            in_=P.ps[4 + tt][0:1, :], reads=(f"ps{4+tt}",), writes=("bc0",))
    B.v("scalar_tensor_tensor", out=P.row2, in0=P.rows[0][:],
        scalar=1.0 / (D * D), in1=P.rows[0][:], op0=ALU.mult, op1=ALU.mult,
        reads=("bc1",), writes=("tmpf0",))
    B.v("scalar_tensor_tensor", out=P.rows[1][:], in0=P.rows[1][:],
        scalar=1.0 / D, in1=P.row2, op0=ALU.mult, op1=ALU.subtract,
        reads=("bc0", "tmpf0"), writes=("bc0",))
    P.const_val(eps)
    B.s("activation", out=P.rows[1][:], in_=P.rows[1][:], func=AF.Ln, bias=eps,
        reads=("bc0",), writes=("bc0",))
    B.s("activation", out=P.rows[1][:], in_=P.rows[1][:], func=AF.Exp, scale=-0.5,
        reads=("bc0",), writes=("bc0",))
    B.v("scalar_tensor_tensor", out=P.rows[0][:], in0=P.rows[0][:],
        scalar=1.0 / D, in1=P.rows[1][:], op0=ALU.mult, op1=ALU.mult,
        reads=("bc1", "bc0"), writes=("bc1",))
    P.bcast_row(P.bc32[0][:], P.rows[1][:], 0, T, 128, "bc0", "bc0")
    P.bcast_row(P.bc32[1][:], P.rows[0][:], 1, T, 128, "bc1", "bc1")
    for c in range(DC):
        B.v("tensor_mul", out=P.tmpf[0][:], in0=tiles[c][:], in1=P.bc32[0][:],
            reads=(keys[c], "bc0"), writes=("tmpf0",))
        B.v("tensor_sub", out=P.tmpf[0][:], in0=P.tmpf[0][:], in1=P.bc32[1][:],
            reads=("tmpf0", "bc1"), writes=("tmpf0",))
        g_ap, g_key = gamma_beta[0][c]
        b_ap, b_key = gamma_beta[1][c]
        if act is None:
            B.v("tensor_scalar", out=P.xn[c][:], in0=P.tmpf[0][:],
                scalar1=g_ap, scalar2=b_ap, op0=ALU.mult, op1=ALU.add,
                reads=("tmpf0", g_key, b_key), writes=(f"xn{c}",))
        else:
            B.s("activation", out=P.xn[c][:], in_=P.tmpf[0][:], func=act,
                scale=g_ap, bias=b_ap,
                reads=("tmpf0", g_key, b_key), writes=(f"xn{c}",))


def emit_l3_attn(P, W):
    B = P.B
    gat = [pool(P, f"bfA{j}", [128, T], BF16) for j in range(DC)]
    qch = [pool(P, "bfC0", [128, T], BF16), pool(P, "bfC1", [128, T], BF16)]
    qk_ = ["bfC0", "bfC1"]
    kch = [pool(P, "bfB0", [128, T], BF16), pool(P, "bfB1", [128, T], BF16)]
    kk_ = ["bfB0", "bfB1"]
    bfv3 = pool(P, "bfV", [128, 2210], BF16)
    pt = bfv3[:, 0:T]
    wst = [pool(P, "wS0", [128, 1024], BF16), pool(P, "wS1", [128, 1024], BF16)]
    blk16 = pool(P, "l3blk", [128, 8], BF16)
    exp16 = pool(P, "mnA", [128, 256], F32)[0:8, 0:128]
    gsc = pt[0:64, 0:T]
    mcar = pool(P, "mcarry", [128, 256], F32).bitcast(BF16)
    vld = [mcar[:, j * 64:(j + 1) * 64] for j in range(2)]
    ssc = pool(P, "l3ssc", [128, 8], F32)
    B.dma(blk16[:], W["blk16"][:, :], writes=("l3blk",), queue="gpsimd")
    B.dma(exp16[0:8, :], W["exp16"][:, :], writes=("mnA",))
    P.const_val(1e-12)
    rr = P.bc32[1][0:8, :]

    def qk_proj(wname, dst, dkeys):
        for oc in range(2):
            B.dma(wst[oc % 2][:].rearrange("p (c m) -> p c m", m=128),
                  W[wname].rearrange("(c p) m -> p c m", p=128)[:, :, oc * 128:(oc + 1) * 128],
                  writes=(f"wS{oc%2}",), queue="gpsimd")
            for tt in range(4):
                for c in range(DC):
                    B.mm(P.ps[tt % 2][:, :], wst[oc % 2][:, c * 128:(c + 1) * 128],
                         P.xn[c][:, tt * 512:(tt + 1) * 512],
                         start=(c == 0), stop=(c == DC - 1),
                         reads=(f"wS{oc%2}", f"xn{c}"), writes=(f"ps{tt%2}",))
                B.v("tensor_copy", out=dst[oc][:, tt * 512:(tt + 1) * 512],
                    in_=P.ps[tt % 2][:, :], reads=(f"ps{tt%2}",), writes=(dkeys[oc],))
            B.s("activation", out=P.sq[0][:], in_=dst[oc][:], func=AF.Square,
                reads=(dkeys[oc],), writes=("sq0",))
            for tt in range(4):
                B.mm(P.ps[6][0:8, :], blk16[:, 0:8],
                     P.sq[0][:, tt * 512:(tt + 1) * 512],
                     start=True, stop=True, reads=("sq0", "l3blk"), writes=("ps6",))
                B.v("tensor_copy", out=rr[0:8, tt * 512:(tt + 1) * 512],
                    in_=P.ps[6][0:8, :], reads=("ps6",), writes=("bc1",))
            B.s("activation", out=rr[0:8, :], in_=rr[0:8, :], func=AF.Sqrt,
                reads=("bc1",), writes=("bc1",))
            B.v("tensor_scalar", out=rr[0:8, :], in0=rr[0:8, :], scalar1=1e-12,
                scalar2=None, op0=ALU.max, reads=("bc1",), writes=("bc1",))
            B.v("reciprocal", out=rr[0:8, :], in_=rr[0:8, :],
                reads=("bc1",), writes=("bc1",))
            for tt in range(4):
                B.mm(P.ps[6][:, :], exp16[0:8, :], rr[0:8, tt * 512:(tt + 1) * 512],
                     start=True, stop=True, reads=("mnA", "bc1"), writes=("ps6",))
                B.v("tensor_mul", out=dst[oc][:, tt * 512:(tt + 1) * 512],
                    in0=dst[oc][:, tt * 512:(tt + 1) * 512], in1=P.ps[6][:, :],
                    reads=(dkeys[oc], "ps6"), writes=(dkeys[oc],))
    qk_proj("l3a_WqgP", qch, qk_)
    qk_proj("l3a_WkvP", kch, kk_)
    for oc in range(2, 10):
        B.dma(wst[oc % 2][:].rearrange("p (c m) -> p c m", m=128),
              W["l3a_WqgP"].rearrange("(c p) m -> p c m", p=128)[:, :, oc * 128:(oc + 1) * 128],
              writes=(f"wS{oc%2}",), queue="gpsimd")
        for tt in range(4):
            for c in range(DC):
                B.mm(P.ps[tt % 2][:, :], wst[oc % 2][:, c * 128:(c + 1) * 128],
                     P.xn[c][:, tt * 512:(tt + 1) * 512],
                     start=(c == 0), stop=(c == DC - 1),
                     reads=(f"wS{oc%2}", f"xn{c}"), writes=(f"ps{tt%2}",))
            B.v("tensor_copy", out=gat[oc - 2][:, tt * 512:(tt + 1) * 512],
                in_=P.ps[tt % 2][:, :], reads=(f"ps{tt%2}",), writes=(f"bfA{oc-2}",))
    for tc in range(16):
        for nv in range(2):
            for c in range(DC):
                B.dma(wst[c % 2][:, 0:512],
                      W["l3a_WkvP"][c * 128:(c + 1) * 128,
                                    256 + nv * 512:256 + (nv + 1) * 512],
                      writes=(f"wS{c%2}",), queue="gpsimd")
                B.mm(P.ps[2 + nv][:, :], P.xn[c][:, tc * 128:(tc + 1) * 128],
                     wst[c % 2][:, 0:512],
                     start=(c == 0), stop=(c == DC - 1),
                     reads=(f"wS{c%2}", f"xn{c}"), writes=(f"ps{2+nv}",))
            B.s("activation", out=P.sq[0][:, 0:512], in_=P.ps[2 + nv][:, :],
                func=AF.Square, reads=(f"ps{2+nv}",), writes=("sq0",))
            B.v("tensor_reduce", out=ssc[:, 0:8],
                in_=P.sq[0][:, 0:512].rearrange("p (g v) -> p g v", v=64),
                axis=AX.X, op=ALU.add, reads=("sq0",), writes=("l3ssc",))
            B.s("activation", out=ssc[:, 0:8], in_=ssc[:, 0:8], func=AF.Sqrt,
                reads=("l3ssc",), writes=("l3ssc",))
            B.v("tensor_scalar", out=ssc[:, 0:8], in0=ssc[:, 0:8], scalar1=1e-12,
                scalar2=None, op0=ALU.max, reads=("l3ssc",), writes=("l3ssc",))
            B.v("reciprocal", out=ssc[:, 0:8], in_=ssc[:, 0:8],
                reads=("l3ssc",), writes=("l3ssc",))
            for g in range(8):
                B.v("tensor_scalar", out=P.sq[1][:, g * 64:(g + 1) * 64],
                    in0=P.ps[2 + nv][:, g * 64:(g + 1) * 64],
                    scalar1=ssc[:, g:g + 1], scalar2=None, op0=ALU.mult,
                    reads=(f"ps{2+nv}", "l3ssc"), writes=("sq1",))
            B.dma(P.vlin_dram[tc * 128:(tc + 1) * 128, nv * 512:(nv + 1) * 512],
                  P.sq[1][:, 0:512], reads=("sq1",), writes=("vlind",))
    wsum = pool(P, "l3ws", [128, 1], F32)
    for h in range(16):
        if True:
            B.dma(P.xn[1][0:16, :], qch[h // 8][h % 8 * 16:h % 8 * 16 + 16, :],
                  reads=(qk_[h // 8],), writes=("xn1",))
            B.dma(P.xn[0][0:16, :], kch[h // 8][h % 8 * 16:h % 8 * 16 + 16, :],
                  reads=(kk_[h // 8],), writes=("xn0",), queue="gpsimd")
            for kc in range(16):
                B.dma(vld[kc % 2], P.vlin_dram[kc * 128:(kc + 1) * 128,
                                                  h * 64:(h + 1) * 64],
                      reads=("vlind",), writes=("mcarry",))
                for it in range(4):
                    B.mm(P.ps[it % 2][:, :], P.xn[0][0:16, kc * 128:(kc + 1) * 128],
                         P.xn[1][0:16, it * 512:(it + 1) * 512],
                         start=True, stop=True,
                         reads=("xn0", "xn1"), writes=(f"ps{it%2}",))
                    B.v("tensor_scalar", out=P.sq[0][:, it * 512:(it + 1) * 512],
                        in0=P.ps[it % 2][:, :], scalar1=0.0, scalar2=None,
                        op0=ALU.max, reads=(f"ps{it%2}",), writes=("sq0",))
                    B.v("tensor_mul", out=pt[:, it * 512:(it + 1) * 512],
                        in0=P.sq[0][:, it * 512:(it + 1) * 512],
                        in1=P.ps[it % 2][:, :],
                        reads=("sq0", f"ps{it%2}"), writes=("bfV",))
                for it in range(4):
                    B.mm(P.ps[4 + it][0:64, :], vld[kc % 2],
                         pt[:, it * 512:(it + 1) * 512],
                         start=(kc == 0), stop=(kc == 15),
                         reads=("mcarry", "bfV"), writes=(f"ps{4+it}",))
            for it in range(4):
                B.s("activation", out=P.sq[0][0:64, it * 512:(it + 1) * 512],
                    in_=P.ps[4 + it][0:64, :], func=AF.Square,
                    reads=(f"ps{4+it}",), writes=("sq0",))
                B.mm(P.ps[3][0:1, :], P.ones_bf[0:64, 0:1],
                     P.sq[0][0:64, it * 512:(it + 1) * 512],
                     start=True, stop=True, reads=("sq0", "ones"), writes=("ps3",))
                B.v("tensor_copy", out=P.row2[0:1, it * 512:(it + 1) * 512],
                    in_=P.ps[3][0:1, :], reads=("ps3",), writes=("tmpf0",))
            B.s("activation", out=P.row2, in_=P.row2, func=AF.Sqrt,
                reads=("tmpf0",), writes=("tmpf0",))
            B.s("activation", out=P.rows[0][:], in_=P.row2, func=AF.Tanh,
                reads=("tmpf0",), writes=("bc1",))
            B.v("tensor_scalar", out=P.row2, in0=P.row2, scalar1=1e-12,
                scalar2=None, op0=ALU.max, reads=("tmpf0",), writes=("tmpf0",))
            B.v("reciprocal", out=P.row2, in_=P.row2, reads=("tmpf0",), writes=("tmpf0",))
            B.v("tensor_mul", out=P.rows[0][:], in0=P.rows[0][:], in1=P.row2,
                reads=("bc1", "tmpf0"), writes=("bc1",))
            P.bcast_row(P.bc32[0][0:64, :], P.rows[0][:], 4, T, 64, "bc1", "bc0")
            B.s("activation", out=gsc,
                in_=gat[h // 2][(h % 2) * 64:(h % 2) * 64 + 64, :], func=AF.Silu,
                reads=(f"bfA{h//2}",), writes=("bfV",))
            B.s("activation", out=gsc, in_=gsc, func=AF.Tanh,
                reads=("bfV",), writes=("bfV",))
            for it in range(4):
                B.v("tensor_mul", out=P.tmpf[0][0:64, 0:512],
                    in0=P.ps[4 + it][0:64, :],
                    in1=P.bc32[0][0:64, it * 512:(it + 1) * 512],
                    reads=(f"ps{4+it}", "bc0"), writes=("tmpf0",))
                B.v("tensor_mul",
                    out=gat[h // 2][(h % 2) * 64:(h % 2) * 64 + 64, it * 512:(it + 1) * 512],
                    in0=P.tmpf[0][0:64, 0:512],
                    in1=gsc[:, it * 512:(it + 1) * 512],
                    reads=("tmpf0", "bfV"), writes=(f"bfA{h//2}",))
    for dc in range(DC):
        B.dma(wst[dc % 2][:].rearrange("p (c m) -> p c m", m=128),
              W["l3a_Wout"].rearrange("(c p) m -> p c m", p=128)[:, :, dc * 128:(dc + 1) * 128],
              writes=(f"wS{dc%2}",), queue="gpsimd")
        for tt in range(4):
            for hc in range(DC):
                B.mm(P.ps[2 + tt % 2][:, :], wst[dc % 2][:, hc * 128:(hc + 1) * 128],
                     gat[hc][:, tt * 512:(tt + 1) * 512],
                     start=(hc == 0), stop=(hc == DC - 1),
                     reads=(f"wS{dc%2}", f"bfA{hc}"), writes=(f"ps{2+tt%2}",))
            B.v("tensor_add", out=P.xs[dc][:, tt * 512:(tt + 1) * 512],
                in0=P.xs[dc][:, tt * 512:(tt + 1) * 512], in1=P.ps[2 + tt % 2][:, :],
                reads=(f"ps{2+tt%2}", f"xs{dc}"), writes=(f"xs{dc}",))



def emit_l3_ffn(P, W):
    B = P.B
    wst = [pool(P, "wS0", [128, 1024], BF16), pool(P, "wS1", [128, 1024], BF16)]
    yt = [pool(P, f"bfA{j}", [128, T], BF16) for j in range(DC)]
    kp2 = pool(P, "mxin", [128, 16 * 260], F32).bitcast(BF16)
    lr = kp2[0:32, 0:T]
    bfb0 = pool(P, "bfB0", [128, T], BF16)
    qi = bfb0[:].bitcast(I32)[:, 0:1024]
    bcolx = pool(P, "l2bcol", [128, 8], F32)
    wdq = P.nc.dram_tensor("wdq_dram", [6, 1024, 1024], BF16, kind="Internal").ap() \
        if not hasattr(P, "_wdq") else P._wdq
    P._wdq = wdq
    done_key = "l3_dequantized"
    if not getattr(P, "_l3_dq_done", False):
        P._l3_dq_done = True
        for mat in range(6):
            for c in range(DC):
                B.dma(qi, W["l3_qT"][mat, c * 128:(c + 1) * 128, :],
                      writes=("bfB0",))
                B.dma(P.bc32[0][:, 0:1024], W["l3_scT"][mat, c * 128:(c + 1) * 128, :],
                      writes=("bc0",))
                B.v("tensor_copy", out=P.tmpf[0][:, 0:1024], in_=qi,
                    reads=("bfB0",), writes=("tmpf0",))
                B.v("scalar_tensor_tensor", out=P.tmpf[0][:, 0:1024],
                    in0=P.tmpf[0][:, 0:1024], scalar=1.0 / 7.5,
                    in1=P.bc32[0][:, 0:1024], op0=ALU.mult, op1=ALU.mult,
                    reads=("tmpf0", "bc0"), writes=("tmpf0",))
                B.v("tensor_sub", out=P.tmpf[0][:, 0:1024],
                    in0=P.tmpf[0][:, 0:1024], in1=P.bc32[0][:, 0:1024],
                    reads=("tmpf0", "bc0"), writes=("tmpf0",))
                B.v("tensor_copy", out=P.sq[0][:, 0:1024], in_=P.tmpf[0][:, 0:1024],
                    reads=("tmpf0",), writes=("sq0",))
                B.dma(wdq[mat, c * 128:(c + 1) * 128, :], P.sq[0][:, 0:1024],
                      reads=("sq0",), writes=(done_key,))
    emit_ln(P)
    for c in range(DC):
        B.dma(P.park_dram[c * 128:(c + 1) * 128, :], P.xs[c][:],
              reads=(f"xs{c}",), writes=(f"park{c}",))
        B.v("tensor_mul", out=P.tmpf[0][:], in0=P.xs[c][:], in1=P.bc32[0][:],
            reads=(f"xs{c}", "bc0"), writes=("tmpf0",))
        B.v("tensor_sub", out=P.xs[c][:], in0=P.tmpf[0][:], in1=P.bc32[1][:],
            reads=("tmpf0", "bc1"), writes=(f"xs{c}",))

    def qlora(mat, src_tiles, src_keys, epilogue):
        B.dma(wst[0][:, 0:256].rearrange("p (c m) -> p c m", m=32),
              W["l3_laT"].rearrange("e (c p) r -> e p c r", p=128)[mat],
              writes=("wS0",), queue="gpsimd")
        for tt in range(4):
            for c in range(DC):
                B.mm(P.ps[6][0:32, :], wst[0][:, c * 32:(c + 1) * 32],
                     src_tiles[c][:, tt * 512:(tt + 1) * 512],
                     start=(c == 0), stop=(c == DC - 1),
                     reads=("wS0", src_keys[c]), writes=("ps6",))
            B.v("tensor_copy", out=lr[0:32, tt * 512:(tt + 1) * 512],
                in_=P.ps[6][0:32, :], reads=("ps6",), writes=("mxin",))
        B.dma(wst[1][0:32, :], W["l3_lbT"][mat, 0:32, :],
              writes=("wS1",), queue="gpsimd")
        for oc in range(DC):
            B.dma(wst[0][:].rearrange("p (c m) -> p c m", m=128),
                  wdq[mat].rearrange("(c p) m -> p c m", p=128)[:, :, oc * 128:(oc + 1) * 128],
                  writes=("wS0",), queue="gpsimd")
            for tt in range(4):
                for c in range(DC):
                    B.mm(P.ps[2 + tt % 2][:, :], wst[0][:, c * 128:(c + 1) * 128],
                         src_tiles[c][:, tt * 512:(tt + 1) * 512],
                         start=(c == 0), stop=False,
                         reads=("wS0", src_keys[c]), writes=(f"ps{2+tt%2}",))
                B.mm(P.ps[2 + tt % 2][:, :], wst[1][0:32, oc * 128:(oc + 1) * 128],
                     lr[0:32, tt * 512:(tt + 1) * 512],
                     start=False, stop=True,
                     reads=("wS1", "mxin"), writes=(f"ps{2+tt%2}",))
                epilogue(oc, tt, P.ps[2 + tt % 2])

    for blk in range(3):
        i0, i1 = 2 * blk, 2 * blk + 1
        for c in range(DC):
            B.v("tensor_copy", out=P.xn[c][:], in_=P.xs[c][:],
                reads=(f"xs{c}",), writes=(f"xn{c}",))
        def ep1(oc, tt, ps, _b=[None]):
            if tt == 0:
                load_col(P, bcolx[:, oc:oc + 1], W["l3_bias"][i0], oc * 128, 128,
                         "l2bcol")
            B.v("tensor_scalar", out=yt[oc][:, tt * 512:(tt + 1) * 512],
                in0=ps[:, :], scalar1=bcolx[:, oc:oc + 1], scalar2=None,
                op0=ALU.add, reads=(f"ps{2+tt%2}", "l2bcol"), writes=(f"bfA{oc}",))
        qlora(i0, P.xn, [f"xn{c}" for c in range(DC)], ep1)
        gb_g = []
        gb_b = []
        gts = [pool(P, f"gbg{c}", [128, 1], F32) for c in range(DC)]
        bts = [pool(P, f"gbb{c}", [128, 1], F32) for c in range(DC)]
        for c in range(DC):
            load_col(P, gts[c], W["l3_lng"][blk], c * 128, 128, f"gbg{c}")
            load_col(P, bts[c], W["l3_lnb"][blk], c * 128, 128, f"gbb{c}")
            gb_g.append((gts[c][:, 0:1], f"gbg{c}"))
            gb_b.append((bts[c][:, 0:1], f"gbb{c}"))
        emit_ln_apply_tiles(P, yt, [f"bfA{j}" for j in range(DC)],
                            gamma_beta=(gb_g, gb_b), act=AF.Gelu)
        def ep2(oc, tt, ps):
            if tt == 0:
                load_col(P, bcolx[:, oc:oc + 1], W["l3_bias"][i1], oc * 128, 128,
                         "l2bcol")
            B.v("scalar_tensor_tensor",
                out=P.xs[oc][:, tt * 512:(tt + 1) * 512], in0=ps[:, :],
                scalar=bcolx[:, oc:oc + 1],
                in1=P.xs[oc][:, tt * 512:(tt + 1) * 512],
                op0=ALU.add, op1=ALU.add,
                reads=(f"ps{2+tt%2}", "l2bcol", f"xs{oc}"), writes=(f"xs{oc}",))
        qlora(i1, P.xn, [f"xn{c}" for c in range(DC)], ep2)
    for c in range(DC):
        B.dma(P.bc32[c % 2][:], P.park_dram[c * 128:(c + 1) * 128, :],
              reads=(f"park{c}",), writes=(f"bc{c%2}",))
        B.v("scalar_tensor_tensor", out=P.xs[c][:], in0=P.xs[c][:], scalar=0.5,
            in1=P.bc32[c % 2][:], op0=ALU.mult, op1=ALU.add,
            reads=(f"xs{c}", f"bc{c%2}"), writes=(f"xs{c}",))



def host_prep(inp):
    w = {}
    tr = lambda a: np.ascontiguousarray(np.asarray(a).T)
    w["l0_fwT"] = tr(inp["L0_ffn_weight"])
    w["l0_WwT"] = tr(inp["L0_attn_W_w"])
    w["l0_Wb"] = np.asarray(inp["L0_attn_W_b"])
    w["l0_owT"] = tr(inp["L0_attn_out_w"])
    w["l0_ob"] = np.asarray(inp["L0_attn_out_b"])
    w["ident"] = np.eye(128, dtype=np.float32)
    w["l1_rwT"] = tr(inp["L1_ffn_router_w"])
    w["l1_rb"] = np.asarray(inp["L1_ffn_router_bias"])
    w["l1_up"] = np.asarray(inp["L1_ffn_up_proj"])
    w["l1_gate"] = np.asarray(inp["L1_ffn_gate_proj"])
    w["l1_down"] = np.asarray(inp["L1_ffn_down_proj"])
    w["l1_shgT"] = tr(inp["L1_ffn_sh_gate"])
    w["l1_shuT"] = tr(inp["L1_ffn_sh_up"])
    w["l1_shdT"] = tr(inp["L1_ffn_sh_down"])
    sel = np.zeros((16, 16 * 128), dtype=np.float32)
    for n in range(16):
        sel[n, n * 128:(n + 1) * 128] = 1.0
    w["seln"] = sel
    for i in range(4):
        w[f"l1m_inT{i}"] = tr(inp["L1_attn_in_proj_w"][i])
        w[f"l1m_cw{i}"] = np.ascontiguousarray(
            np.asarray(inp["L1_attn_conv_w"][i]).reshape(2048, 4))
        w[f"l1m_cb{i}"] = np.asarray(inp["L1_attn_conv_b"][i])
        w[f"l1m_xpT{i}"] = tr(inp["L1_attn_x_proj_w"][i])
        w[f"l1m_dtT{i}"] = tr(inp["L1_attn_dt_proj_w"][i])
        w[f"l1m_dtb{i}"] = np.asarray(inp["L1_attn_dt_proj_b"][i])
        w[f"l1m_A{i}"] = np.asarray(inp["L1_attn_A_log"][i])
        w[f"l1m_D{i}"] = np.asarray(inp["L1_attn_D_skip"][i])
        w[f"l1m_outT{i}"] = tr(inp["L1_attn_out_proj_w"][i])
        w[f"l1m_lnw{i}"] = np.asarray(inp["L1_attn_ln_w"][i])
        w[f"l1m_lnb{i}"] = np.asarray(inp["L1_attn_ln_b"][i])
    w["l2_rw"] = np.asarray(inp["L2_ffn_router_w"])
    w["l2_rb"] = np.asarray(inp["L2_ffn_router_b"])
    w["l2_w1"] = np.asarray(inp["L2_ffn_w1"])
    w["l2_b1"] = np.asarray(inp["L2_ffn_b1"])
    w["l2_w2"] = np.asarray(inp["L2_ffn_w2"])
    w["l2_b2"] = np.asarray(inp["L2_ffn_b2"])
    for nm, key in (("wql", "L2_attn_wq_lin"), ("wkl", "L2_attn_wk_lin"),
                    ("wvl", "L2_attn_wv_lin"), ("wqw", "L2_attn_wq_loc"),
                    ("wkw", "L2_attn_wk_loc"), ("wvw", "L2_attn_wv_loc"),
                    ("wo", "L2_attn_wo"), ("wg1", "L2_attn_wg1"),
                    ("wg2", "L2_attn_wg2")):
        w[f"l2a_{nm}"] = np.asarray(inp[key])
    for nm, key in (("bql", "L2_attn_bq_lin"), ("bkl", "L2_attn_bk_lin"),
                    ("bvl", "L2_attn_bv_lin"), ("bqw", "L2_attn_bq_loc"),
                    ("bkw", "L2_attn_bk_loc"), ("bvw", "L2_attn_bv_loc"),
                    ("bo", "L2_attn_bo"), ("bg1", "L2_attn_bg1"),
                    ("bg2", "L2_attn_bg2"), ("lng", "L2_attn_ln_g"),
                    ("lnb", "L2_attn_ln_b")):
        w[f"l2a_{nm}"] = np.asarray(inp[key])
    rb_ = np.asarray(inp["L2_attn_rel_bias"])
    w["l2a_rbT"] = np.ascontiguousarray(rb_.T)
    bias = np.full((3, 16, 256, 128), NEG, dtype=np.float32)
    jj = np.arange(256)[:, None]
    ii = np.arange(128)[None, :]
    off = jj - 64 - ii
    band = np.abs(off) <= 32
    for v, qc in ((0, 0), (1, 1), (2, 15)):
        kglob = qc * 128 - 64 + jj + 0 * ii
        valid = band & (kglob >= 0) & (kglob < T)
        for h in range(16):
            vals = np.where(valid, rb_[np.clip(off + 64, 0, 128), h], NEG)
            bias[v, h] = vals.astype(np.float32)
    w["l2a_biasT"] = bias
    perm_q = [80 * h + i for h in range(16) for i in range(16)]
    perm_g = [80 * h + 16 + j for h in range(16) for j in range(64)]
    perm = perm_q + perm_g
    w["l3a_WqgP"] = np.ascontiguousarray(np.asarray(inp["L3_attn_Wqg"])[:, perm])
    w["l3a_WkvP"] = np.ascontiguousarray(np.asarray(inp["L3_attn_Wkv"])[:, perm])
    w["l3a_Wout"] = np.asarray(inp["L3_attn_Wout"])
    b16 = np.zeros((128, 8), dtype=np.float32)
    for k_ in range(128):
        b16[k_, k_ // 16] = 1.0
    w["blk16"] = b16
    e16 = np.zeros((8, 128), dtype=np.float32)
    for m_ in range(128):
        e16[m_ // 16, m_] = 1.0
    w["exp16"] = e16
    w["l3_qT"] = np.ascontiguousarray(np.asarray(inp["L3_ffn_qweights"]).transpose(0, 2, 1))
    sc_ = np.asarray(inp["L3_ffn_scales"])
    w["l3_scT"] = np.ascontiguousarray(np.repeat(sc_, 16, axis=2).transpose(0, 2, 1))
    w["l3_bias"] = np.asarray(inp["L3_ffn_biases"])
    w["l3_laT"] = np.ascontiguousarray(np.asarray(inp["L3_ffn_lora_a"]).transpose(0, 2, 1))
    w["l3_lbT"] = np.asarray(inp["L3_ffn_lora_b"]).transpose(0, 2, 1)
    w["l3_lbT"] = np.ascontiguousarray(np.asarray(inp["L3_ffn_lora_b"]).transpose(0, 2, 1))
    w["l3_lng"] = np.asarray(inp["L3_ffn_ln_gamma"])
    w["l3_lnb"] = np.asarray(inp["L3_ffn_ln_beta"])
    return w


L1F_W = ["l1_rwT", "l1_rb", "l1_up", "l1_gate", "l1_down", "l1_shgT",
         "l1_shuT", "l1_shdT", "ident"]
SUBLAYERS = {
    "l0f_a": (emit_l0_ffn, ["l0_fwT"]),
    "l0_attn": (emit_l0_attn, ["l0_WwT", "l0_Wb", "l0_owT", "l0_ob"]),
    "l0f_b": (emit_l0_ffn, ["l0_fwT"]),
    "l1f_a": (emit_l1_ffn, L1F_W),
    "l1f_b": (emit_l1_ffn, L1F_W),
    "l1_mamba": (emit_l1_mamba, ["seln"] + [f"l1m_{k}{i}" for i in range(4)
                 for k in ("inT", "cw", "cb", "xpT", "dtT", "dtb", "A", "D",
                           "outT", "lnw", "lnb")]),
    "l2f_a": (emit_l2_ffn, ["l2_rw", "l2_rb", "l2_w1", "l2_b1", "l2_w2",
                            "l2_b2", "ident"]),
    "l2f_b": (emit_l2_ffn, ["l2_rw", "l2_rb", "l2_w1", "l2_b1", "l2_w2",
                            "l2_b2", "ident"]),
    "l2_attn": (emit_l2_attn, ["l2a_wql", "l2a_wkl", "l2a_wvl", "l2a_wqw",
                "l2a_wkw", "l2a_wvw", "l2a_wo", "l2a_wg1", "l2a_wg2",
                "l2a_bql", "l2a_bkl", "l2a_bvl", "l2a_bqw", "l2a_bkw",
                "l2a_bvw", "l2a_bo", "l2a_bg1", "l2a_bg2", "l2a_rbT",
                "l2a_biasT", "l2a_lng", "l2a_lnb", "ident"]),
    "l3_attn": (emit_l3_attn, ["l3a_WqgP", "l3a_WkvP", "l3a_Wout", "blk16",
                               "exp16"]),
    "l3f_a": (emit_l3_ffn, ["l3_qT", "l3_scT", "l3_bias", "l3_laT", "l3_lbT",
                            "l3_lng", "l3_lnb"]),
    "l3f_b": (emit_l3_ffn, ["l3_qT", "l3_scT", "l3_bias", "l3_laT", "l3_lbT",
                            "l3_lng", "l3_lnb"]),
}

LN_KINDS = {
    "l0f_a": "plain", "l0_attn": "plain", "l0f_b": "plain",
    "l1f_a": "plain", "l1_mamba": "plain", "l1f_b": "plain",
    "l2f_a": "plain", "l2_attn": "plain_post", "l2f_b": "plain",
    "l3f_a": "plain", "l3_attn": "plain", "l3f_b": "plain",
}


def build_group(names, prep):
    nc = bass.Bass("TRN2", target_bir_lowering=False)
    es = ExitStack()
    x_in = nc.dram_tensor("x_in", [D, T], F32, kind="ExternalInput").ap()
    x_out = nc.dram_tensor("x_out", [D, T], F32, kind="ExternalOutput").ap()
    wnames = []
    for nm in names:
        wnames += SUBLAYERS[nm][1]
    wnames = sorted(set(wnames))
    W = {k: nc.dram_tensor(k, list(prep[k].shape),
                           I32 if prep[k].dtype == np.int32 else F32,
                           kind="ExternalInput").ap()
         for k in wnames}
    P = Prog(nc, es)
    P.stage_init()
    B = P.B
    block = es.enter_context(nc.Block())
    B.g("memset", P.ones_bf[:], 1.0, writes=("ones",))
    for c in range(DC):
        B.dma(P.xs[c][:], x_in[c * 128:(c + 1) * 128, :], writes=(f"xs{c}",))
    for nm in names:
        emit_ln(P, park_u=(nm == "l2_attn"))
        SUBLAYERS[nm][0](P, W)
    for c in range(DC):
        B.dma(x_out[c * 128:(c + 1) * 128, :], P.xs[c][:],
              reads=(f"xs{c}",), writes=("x_out",), queue="sync")
    B.finalize(es, block, barrier_writes=("x_out",))
    es.close()
    return nc, wnames


ALL_SUBLAYERS = ["l0f_a", "l0_attn", "l0f_b",
                 "l1f_a", "l1_mamba", "l1f_b",
                 "l2f_a", "l2_attn", "l2f_b",
                 "l3f_a", "l3_attn", "l3f_b"]
GROUPS = [ALL_SUBLAYERS]


def kernel(**inputs):
    prep = host_prep(inputs)
    x = np.asarray(inputs["L0_attn_x"])
    streams = [np.ascontiguousarray(x[b].T) for b in range(NCORES)]
    for names in GROUPS:
        nc, wnames = build_group(names, prep)
        in_maps = []
        for b in range(NCORES):
            m = {"x_in": streams[b]}
            for k in wnames:
                m[k] = prep[k]
            in_maps.append(m)
        res = run_bass_kernel_spmd(nc, in_maps, core_ids=list(range(NCORES)))
        streams = [res.results[b]["x_out"] for b in range(NCORES)]
    out = np.stack([s.T for s in streams]).astype(np.float32)
    return out










